# version 04
# baseline (speedup 1.0000x reference)
import numpy as np
from contextlib import ExitStack

import concourse.bass as bass
import concourse.mybir as mybir
from concourse.bass_utils import run_bass_kernel_spmd

F32 = mybir.dt.float32
BF16 = mybir.dt.bfloat16
AF = mybir.ActivationFunctionType
ALU = mybir.AluOpType
AX = mybir.AxisListType

NC = 8
D = 2048
S = 8192
NT = 64
LT = 8
TOK = LT * 128
NH = 16
HD = 128
KC = 16
SCALE = HD ** (-0.5)
EPS_LN = 1e-5
EPS_BL = 1e-6
QB = 128.0
CLIP = 128.0 - 1e-6
NEG = -1.0e6

LAUNCH_GROUPS = [["A", "B", "C", "D"]]
DEBUG_TAPS = False
D_STAGES = 4


def core_tiles(c):
    return sorted([16 * g + c for g in range(4)] + [16 * g + 15 - c for g in range(4)])


OWNER = {}
for _c in range(NC):
    for _l, _jg in enumerate(core_tiles(_c)):
        OWNER[_jg] = (_c, _l)


def seg_list(lo, hi, width=512):
    out = []
    while lo < hi:
        nxt = min(hi, (lo // width + 1) * width)
        out.append((lo, nxt))
        lo = nxt
    return out


class Ctx:

    def __init__(self, nc, stack):
        self.nc = nc
        self.stack = stack
        self.sems = {}
        self.base = {}
        self.free_pool = []

    def sem(self, name):
        if name not in self.sems:
            try:
                h = self.stack.enter_context(self.nc.semaphore(name))
                b = 0
            except KeyError:
                h, b = self.free_pool.pop()
            self.sems[name] = [h, b]
            self.base[name] = b
        return self.sems[name]

    def retire(self, prefix):
        for n in [k for k in self.sems if k.startswith(prefix)]:
            h, v = self.sems.pop(n)
            self.free_pool.append((h, v))
            del self.base[n]

    def inc(self, name, instr, k=1):
        s = self.sem(name)
        instr.then_inc(s[0], k)
        s[1] += k
        if hasattr(self, "vg"):
            self.vg.note_inc(instr, name, s[1])
        return s[1]

    def val(self, name):
        s = self.sem(name)
        return s[1] - self.base[name]

    def wv(self, eng, name, v):
        s = self.sem(name)
        eng.wait_ge(s[0], self.base[name] + v)




def _ap_iv(a):
    try:
        name = a.tensor.name
        lo = int(a.offset)
        span = 1
        for st_, ct in [(int(p[0]), int(p[1])) for p in a.ap]:
            span += (ct - 1) * abs(st_)
        return (name, lo, lo + span)
    except Exception:
        return (None, 0, 1 << 62)


class VGuard:

    def __init__(self, ctx):
        self.ctx = ctx
        self.hist = []
        self.n = 0

    def bind(self, eng):
        self.e = eng
        return self

    def note_inc(self, instr, name, val):
        for h in self.hist:
            if instr is h[0]:
                h[2] = (name, val)

    def _emit(self, fn, out, ins, args, kwargs):
        ivs = [_ap_iv(a) for a in ins]
        hz = None
        for h in reversed(self.hist):
            piv = h[1]
            for iv in ivs:
                if iv[0] is not None and iv[0] == piv[0] and \
                        iv[1] < piv[2] and iv[2] > piv[1]:
                    hz = h
                    break
            if hz is not None:
                break
        if hz is not None:
            if hz[2] is not None:
                s = self.ctx.sem(hz[2][0])
                self.e.wait_ge(s[0], hz[2][1])
            else:
                s = self.ctx.sem("vbar")
                hz[0].then_inc(s[0], 1)
                s[1] += 1
                self.e.wait_ge(s[0], s[1])
            self.hist = []
        i = fn(*args, **kwargs)
        self.hist.append([i, _ap_iv(out), None])
        if len(self.hist) > 2:
            self.hist.pop(0)
        return i

    def wait_ge(self, sem, val):
        return self.e.wait_ge(sem, val)

    def tensor_copy(self, out, in_):
        return self._emit(self.e.tensor_copy, out, [in_], (out, in_), {})

    def tensor_tensor(self, out, in0, in1, op):
        return self._emit(self.e.tensor_tensor, out, [in0, in1],
                          (out, in0, in1, op), {})

    def tensor_scalar(self, out, in0, s1, s2, op0, op1=None, **kw):
        ins = [in0] + [s for s in (s1, s2) if isinstance(s, bass.AP)]
        if op1 is None:
            return self._emit(self.e.tensor_scalar, out, ins,
                              (out, in0, s1, s2, op0), kw)
        return self._emit(self.e.tensor_scalar, out, ins,
                          (out, in0, s1, s2, op0, op1), kw)

    def tensor_reduce(self, out, in_, ax, op, **kw):
        return self._emit(self.e.tensor_reduce, out, [in_],
                          (out, in_, ax, op), kw)

    def reciprocal(self, out, in_):
        return self._emit(self.e.reciprocal, out, [in_], (out, in_), {})

    def memset(self, ap, c):
        return self._emit(self.e.memset, ap, [], (ap, c), {})

    def nop(self):
        return self.e.nop()


def fmv(buf, k, lo, n):
    return buf[:, k * TOK + lo: k * TOK + lo + n]


def emit_preln(ctx, block, tag, src_dram, featT, io, eps=EPS_LN, quant=False,
               gate=None):
    xt = io["xt"]
    xm = io["xm"]
    lnb = io["lnb"]
    sq = io["sq"]
    st = io["st"]
    ptr = io["ptr"]
    ident = io["ident"]

    @block.sync
    def _(sync):
        if gate is not None:
            ctx.wv(sync, gate[0], gate[1])
        for t in range(LT):
            if t >= 2:
                ctx.wv(sync, tag + "xfree", t - 1)
            ctx.inc(tag + "xl", sync.dma_start(
                out=xt[t % 2][:], in_=src_dram[t * 128:(t + 1) * 128, :]), 16)

    @block.vector
    def _(v):
        v = ctx.vg.bind(v)
        for t in range(LT):
            ctx.wv(v, tag + "xl", 16 * (t + 1))
            if t >= 1:
                ctx.wv(v, tag + "lnr", t)
            v.tensor_reduce(st[:, 0:1], xt[t % 2][:], AX.X, ALU.add, negate=True)
            v.tensor_scalar(st[:, 1:2], st[:, 0:1], 1.0 / D, None, ALU.mult)
            i = v.tensor_scalar(xm[t % 2][:], xt[t % 2][:], st[:, 1:2], None, ALU.add)
            ctx.inc(tag + "xmr", i)
            if quant:
                v.tensor_reduce(st[:, 2:3], xm[t % 2][:], AX.X, ALU.max,
                                apply_absolute_value=True)
            ctx.inc(tag + "xfree", v.tensor_copy(st[:, 15:16], st[:, 1:2]))
            ctx.wv(v, tag + "sqd", t + 1)
            ctx.inc(tag + "vpe", v.tensor_scalar(st[:, 4:5], st[:, 5:6], 1.0 / D, eps,
                                                 ALU.mult, ALU.add))
            ctx.wv(v, tag + "sdd", t + 1)
            i = v.reciprocal(st[:, 7:8], st[:, 6:7])
            if quant:
                v.reciprocal(st[:, 9:10], st[:, 2:3])
                v.tensor_scalar(st[:, 10:11], st[:, 9:10], QB, None, ALU.mult)
                v.tensor_tensor(st[:, 11:12], st[:, 7:8], st[:, 2:3], ALU.mult)
                i = v.tensor_scalar(st[:, 12:13], st[:, 11:12], 1.0 / QB, None,
                                    ALU.mult)
            ctx.inc(tag + "rstd", i)
            if quant:
                ctx.wv(v, tag + "q1", t + 1)
                ctx.inc(tag + "q2", v.tensor_scalar(sq[:], sq[:], CLIP, -CLIP,
                                                    ALU.min, ALU.max))
            if DEBUG_TAPS and tag == "DL":
                ctx.wv(v, tag + "lnr", t + 1)
                v.tensor_copy(io["dbg_lnb"][:, t * D:(t + 1) * D], lnb[t % 2][:])
                v.tensor_copy(io["dbg_st"][:, t * 16:(t + 1) * 16], st[:])
            for k in range(KC):
                ctx.wv(v, tag + "tr", t * KC + k + 1)
                ctx.inc(tag + "tre", v.tensor_copy(fmv(featT, k, t * 128, 128),
                                                   ptr[k % 2][:]))

    @block.scalar
    def _(s):
        for t in range(LT):
            ctx.wv(s, tag + "xmr", t + 1)
            ctx.inc(tag + "sqd", s.activation(sq[:], xm[t % 2][:], AF.Square,
                                              accum_out=st[:, 5:6]))
            ctx.wv(s, tag + "vpe", t + 1)
            ctx.inc(tag + "sdd", s.activation(st[:, 6:7], st[:, 4:5], AF.Sqrt))
            ctx.wv(s, tag + "rstd", t + 1)
            if t >= 2:
                ctx.wv(s, tag + "tr", (t - 1) * KC)
            if quant:
                ctx.inc(tag + "q1", s.activation(sq[:], xm[t % 2][:], AF.Identity,
                                                 scale=st[:, 10:11]))
                ctx.wv(s, tag + "q2", t + 1)
                ctx.inc(tag + "lnr", s.activation(lnb[t % 2][:], sq[:], AF.Identity,
                                                  scale=st[:, 12:13]))
            else:
                ctx.inc(tag + "lnr", s.activation(lnb[t % 2][:], xm[t % 2][:],
                                                  AF.Identity, scale=st[:, 7:8]))

    @block.tensor
    def _(te):
        for t in range(LT):
            ctx.wv(te, tag + "lnr", t + 1)
            for k in range(KC):
                idx = t * KC + k
                if idx >= 2:
                    ctx.wv(te, tag + "tre", idx - 1)
                ctx.inc(tag + "tr", te.transpose(ptr[k % 2][:],
                                                 lnb[t % 2][:, k * 128:(k + 1) * 128],
                                                 ident[:]))


def emit_epilogue(ctx, block, tag, acc, src_dram, dst_dram, io, ready):
    xt = io["xt"][0]
    xm = io["xm"][0]
    sq = io["sq"]
    st = io["st"]
    lnf = io["lnf"][0]

    @block.sync
    def _(sync):
        ctx.wv(sync, ready[0][0], ready[0][1])
        for t in range(LT):
            if t >= 1:
                ctx.wv(sync, tag + "yd", t)
            ctx.inc(tag + "xl", sync.dma_start(
                out=xt[:], in_=src_dram[t * 128:(t + 1) * 128, :]), 16)
            ctx.wv(sync, tag + "yd", t + 1)
            ctx.inc(tag + "yo", sync.dma_start(
                out=dst_dram[t * 128:(t + 1) * 128, :], in_=lnf[:]), 16)

    @block.vector
    def _(v):
        v = ctx.vg.bind(v)
        for t in range(LT):
            a = acc[:, t * D:(t + 1) * D]
            ctx.wv(v, ready[t][0], ready[t][1])
            if t >= 1:
                ctx.wv(v, tag + "lnv", t)
            v.tensor_reduce(st[:, 0:1], a, AX.X, ALU.add, negate=True)
            v.tensor_scalar(st[:, 1:2], st[:, 0:1], 1.0 / D, None, ALU.mult)
            ctx.inc(tag + "xmr", v.tensor_scalar(xm[:], a, st[:, 1:2],
                                                 None, ALU.add))
            ctx.wv(v, tag + "sqd", t + 1)
            ctx.inc(tag + "vpe", v.tensor_scalar(st[:, 4:5], st[:, 5:6], 1.0 / D,
                                                 EPS_LN, ALU.mult, ALU.add))
            ctx.wv(v, tag + "sdd", t + 1)
            ctx.inc(tag + "rstd", v.reciprocal(st[:, 7:8], st[:, 6:7]))
            ctx.wv(v, tag + "lnv", t + 1)
            ctx.wv(v, tag + "xl", 16 * (t + 1))
            if t >= 1:
                ctx.wv(v, tag + "yo", 16 * t)
            ctx.inc(tag + "yd", v.tensor_tensor(lnf[:], sq[:], xt[:], ALU.add))

    @block.scalar
    def _(s):
        for t in range(LT):
            ctx.wv(s, tag + "xmr", t + 1)
            if t >= 1:
                ctx.wv(s, tag + "yd", t)
            ctx.inc(tag + "sqd", s.activation(sq[:], xm[:], AF.Square,
                                              accum_out=st[:, 5:6]))
            ctx.wv(s, tag + "vpe", t + 1)
            ctx.inc(tag + "sdd", s.activation(st[:, 6:7], st[:, 4:5], AF.Sqrt))
            ctx.wv(s, tag + "rstd", t + 1)
            ctx.inc(tag + "lnv", s.activation(sq[:], xm[:], AF.Identity,
                                              scale=st[:, 7:8]))


def phase_gate(ctx, block, bar):
    for e in ["sync", "gpsimd", "vector", "scalar", "tensor"]:
        @getattr(block, e)
        def _(q):
            ctx.wv(q, "ph", bar)


def phase_close(ctx, block):
    jk = ctx.junk

    @block.vector
    def _(v):
        v = ctx.vg.bind(v)
        ctx.inc("ph", v.memset(jk[0][:], 0.0))

    @block.scalar
    def _(s):
        ctx.inc("ph", s.copy(jk[1][:], jk[0][:]))

    @block.gpsimd
    def _(g):
        ctx.inc("ph", g.memset(jk[2][:], 0.0))

    @block.sync
    def _(sync):
        ctx.inc("ph", sync.dma_start(out=jk[0][:], in_=jk[1][:]), 16)
    return ctx.val("ph")


def alloc_io(ctx, ps, pfx, slim=False):
    nc = ctx.nc
    nb = 1 if slim else 2
    io = {
        "xt": [ps.enter_context(nc.sbuf_tensor(pfx + "xt%d" % i, [128, D], F32))
               for i in range(nb)],
        "xm": [ps.enter_context(nc.sbuf_tensor(pfx + "xm%d" % i, [128, D], F32))
               for i in range(nb)],
        "sq": ps.enter_context(nc.sbuf_tensor(pfx + "sq", [128, D], F32)),
        "st": ps.enter_context(nc.sbuf_tensor(pfx + "st", [128, 16], F32)),
        "ident": ctx.ident,
    }
    if slim:
        io["lnf"] = [ps.enter_context(nc.sbuf_tensor(pfx + "lnf", [128, D], F32))]
    else:
        io["lnb"] = [ps.enter_context(nc.sbuf_tensor(pfx + "lnb%d" % i,
                                                     [128, D], BF16))
                     for i in range(2)]
        io["ptr"] = [ps.enter_context(nc.psum_tensor(pfx + "ptr%d" % i,
                                                     [128, 128], BF16))
                     for i in range(2)]
    return io


def emit_phase_B(ctx, block, dram, bar):
    nc = ctx.nc
    with ExitStack() as ps:
        featT = ps.enter_context(nc.sbuf_tensor("b_featT", [128, KC * TOK], BF16))
        wb = ps.enter_context(nc.sbuf_tensor("b_wb", [128, KC * 1024], BF16))
        acc = ps.enter_context(nc.sbuf_tensor("b_acc", [128, LT * D], F32))
        wr = ps.enter_context(nc.sbuf_tensor("b_wr", [128, KC * 8], BF16))
        af = ps.enter_context(nc.sbuf_tensor("b_af", [128, KC * 64], BF16))
        bfl = ps.enter_context(nc.sbuf_tensor("b_bfl", [64, D], BF16))
        bb = ps.enter_context(nc.sbuf_tensor("b_bb", [1, D], BF16))
        hst = ps.enter_context(nc.sbuf_tensor("b_hst", [64, LT * 128], BF16))
        lg = ps.enter_context(nc.sbuf_tensor("b_lg", [128, 8], F32))
        gsc = ps.enter_context(nc.sbuf_tensor("b_gsc", [128, 32], F32))
        hsb = ps.enter_context(nc.sbuf_tensor("b_hsb", [128, 64], BF16))
        if DEBUG_TAPS:
            dbg_gs = ps.enter_context(nc.sbuf_tensor("b_dgs", [128, 64], F32))
            dbg_lg = ps.enter_context(nc.sbuf_tensor("b_dlg", [128, 64], F32))
            dbg_ct = ps.enter_context(nc.sbuf_tensor("b_dct", [128, 64], F32))
            dbg_hsb = ps.enter_context(nc.sbuf_tensor("b_dhsb", [128, 512], F32))
        pmm = [ps.enter_context(nc.psum_tensor("b_pm%d" % i, [128, 512], F32))
               for i in range(4)]
        plog = ps.enter_context(nc.psum_tensor("b_plog", [128, 128], F32))
        phs = ps.enter_context(nc.psum_tensor("b_phs", [64, 128], BF16))
        phase_gate(ctx, block, bar)

        @block.gpsimd
        def _(g):
            ctx.inc("Bsw", g.dma_start(
                out=wr[:].rearrange("p (k e) -> p k e", k=KC),
                in_=dram["l0_wrT"].rearrange("(k p) e -> p k e", p=128)), 16)
            ctx.inc("Bsw", g.dma_start(
                out=af[:].rearrange("p (k e) -> p k e", k=KC),
                in_=dram["l0_af"].rearrange("(k p) e -> p k e", p=128)), 16)
            ctx.inc("Bsw", g.dma_start(out=bfl[:], in_=dram["l0_bf"]), 16)
            ctx.inc("Bsw", g.dma_start(out=bb[:], in_=dram["l0_bb"]), 16)

        with ExitStack() as pl:
            io = alloc_io(ctx, pl, "b_")
            emit_preln(ctx, block, "BL", dram["bsrc"], featT, io)

        @block.tensor
        def _(te):
            ctx.wv(te, "Bsw", 64)
            for t in range(LT):
                ctx.wv(te, "BLtre", (t + 1) * KC)
                if t >= 1:
                    ctx.wv(te, "Blge", t)
                for k in range(KC):
                    te.matmul(plog[:, 0:8], fmv(featT, k, t * 128, 128),
                              wr[:, k * 8:(k + 1) * 8],
                              start=(k == 0), stop=(k == KC - 1))
                for k in range(KC):
                    i = te.matmul(plog[:, 8:72], fmv(featT, k, t * 128, 128),
                                  af[:, k * 64:(k + 1) * 64],
                                  start=(k == 0), stop=(k == KC - 1))
                ctx.inc("Blog", i)
                ctx.wv(te, "Bhs", t + 1)
                ctx.inc("BhsT", te.transpose(phs[:], hsb[:], ctx.ident[:]))

        @block.vector
        def _(v):
            v = ctx.vg.bind(v)
            for t in range(LT):
                ctx.wv(v, "Blog", t + 1)
                v.tensor_copy(lg[:], plog[:, 0:8])
                v.tensor_reduce(gsc[:, 0:1], lg[:], AX.X, ALU.max, negate=True)
                for e in range(8):
                    v.tensor_scalar(gsc[:, 8:16], lg[:], lg[:, e:e + 1], None, ALU.is_gt)
                    v.tensor_reduce(gsc[:, 16 + e:17 + e], gsc[:, 8:16], AX.X, ALU.add)
                i = v.tensor_scalar(gsc[:, 24:32], gsc[:, 16:24], 1.5, None, ALU.is_lt)
                ctx.inc("Bgm", i)
                ctx.wv(v, "Bge", t + 1)
                v.tensor_tensor(gsc[:, 8:16], gsc[:, 8:16], gsc[:, 24:32], ALU.mult)
                v.tensor_reduce(gsc[:, 2:3], gsc[:, 8:16], AX.X, ALU.add)
                v.reciprocal(gsc[:, 3:4], gsc[:, 2:3])
                v.tensor_scalar(gsc[:, 8:16], gsc[:, 8:16], gsc[:, 3:4], None, ALU.mult)
                if t >= 1:
                    ctx.wv(v, "BhsT", t)
                if DEBUG_TAPS:
                    v.tensor_copy(dbg_gs[:, t * 8:(t + 1) * 8], gsc[:, 8:16])
                    v.tensor_copy(dbg_lg[:, t * 8:(t + 1) * 8], lg[:])
                    v.tensor_copy(dbg_ct[:, t * 8:(t + 1) * 8], gsc[:, 16:24])
                i = None
                for e in range(8):
                    i = v.tensor_scalar(hsb[:, e * 8:(e + 1) * 8],
                                        plog[:, 8 + e * 8:16 + e * 8],
                                        gsc[:, 8 + e:9 + e], 2.0, ALU.mult, ALU.mult)
                ctx.inc("Blge", i)
                if DEBUG_TAPS:
                    v.tensor_copy(dbg_hsb[:, t * 64:(t + 1) * 64], hsb[:])
                ctx.inc("Bhs", v.tensor_copy(gsc[:, 4:5], gsc[:, 3:4]))
                ctx.wv(v, "BhsT", t + 1)
                ctx.inc("BhsE", v.tensor_copy(hst[:, t * 128:(t + 1) * 128], phs[:]))

        @block.scalar
        def _(s):
            for t in range(LT):
                ctx.wv(s, "Bgm", t + 1)
                ctx.inc("Bge", s.activation(gsc[:, 8:16], lg[:], AF.Exp,
                                            bias=gsc[:, 0:1]))

        @block.gpsimd
        def _(sync):
            for k in range(KC):
                ctx.inc("Bwl", sync.dma_start(
                    out=wb[:, k * 1024:(k + 1) * 1024],
                    in_=dram["l0_wbT"][k * 128:(k + 1) * 128, 0:1024]), 16)

        @block.tensor
        def _(te):
            for half in range(2):
                ctx.wv(te, "Bwl", 16 * KC * (half + 1))
                for t in range(LT):
                    ctx.wv(te, "BhsE", LT)
                    for n in range(2):
                        p = pmm[(t * 2 + n) % 4]
                        co = half * 1024 + n * 512
                        idx = 2 * LT * half + t * 2 + n
                        if idx >= 4:
                            ctx.wv(te, "Bacc", idx - 3)
                        for k in range(KC):
                            te.matmul(p[:], fmv(featT, k, t * 128, 128),
                                      wb[:, k * 1024 + n * 512:k * 1024 + n * 512 + 512],
                                      start=(k == 0), stop=False)
                        te.matmul(p[:], ctx.ones_row[:, 0:128], bb[:, co:co + 512],
                                  start=False, stop=False)
                        i = te.matmul(p[:], hst[:, t * 128:(t + 1) * 128],
                                      bfl[:, co:co + 512], start=False, stop=True)
                        ctx.inc("Bmm", i)

        @block.gpsimd
        def _(sync):
            ctx.wv(sync, "Bmm", 2 * LT)
            for k in range(KC):
                ctx.inc("Bwl", sync.dma_start(
                    out=wb[:, k * 1024:(k + 1) * 1024],
                    in_=dram["l0_wbT"][k * 128:(k + 1) * 128, 1024:2048]), 16)

        rdy = [("Bacc", 2 * LT + 2 * t + 2) for t in range(LT)]

        @block.vector
        def _(v):
            v = ctx.vg.bind(v)
            for half in range(2):
                for t in range(LT):
                    for n in range(2):
                        idx = 2 * LT * half + t * 2 + n
                        ctx.wv(v, "Bmm", idx + 1)
                        co = half * 1024 + n * 512
                        i = v.tensor_copy(acc[:, t * D + co:t * D + co + 512],
                                          pmm[(t * 2 + n) % 4][:])
                        ctx.inc("Bacc", i)

        io_ep = alloc_io(ctx, ps, "be_", slim=True)
        if DEBUG_TAPS:
            @block.sync
            def _(sync):
                ctx.wv(sync, "BLtre", LT * KC)
                ctx.inc("Bdbg", sync.dma_start(out=dram["dbg_featT"],
                                               in_=featT[:]), 16)
                ctx.wv(sync, "BhsE", LT)
                ctx.inc("Bdbg", sync.dma_start(out=dram["dbg_hst"],
                                               in_=hst[:]), 16)
                ctx.wv(sync, "Bacc", 4 * LT)
                ctx.inc("Bdbg", sync.dma_start(out=dram["dbg_acc"],
                                               in_=acc[:]), 16)
                ctx.inc("Bdbg", sync.dma_start(out=dram["dbg_gs"], in_=dbg_gs[:]), 16)
                ctx.inc("Bdbg", sync.dma_start(out=dram["dbg_lg"], in_=dbg_lg[:]), 16)
                ctx.inc("Bdbg", sync.dma_start(out=dram["dbg_ct"], in_=dbg_ct[:]), 16)
                ctx.inc("Bdbg", sync.dma_start(out=dram["dbg_hsb"],
                                               in_=dbg_hsb[:]), 16)
                ctx.wv(sync, "Bdbg", 112)

        emit_epilogue(ctx, block, "BE", acc, dram["bsrc"], dram["bdst"], io_ep, rdy)

        @block.sync
        def _(sync):
            ctx.wv(sync, "BEyo", ctx.val("BEyo"))

        return phase_close(ctx, block)


def emit_phase_C(ctx, block, dram, bar):
    nc = ctx.nc
    with ExitStack() as ps:
        wb = ps.enter_context(nc.sbuf_tensor("c_wb", [128, KC * 1024], BF16))
        oT = ps.enter_context(nc.sbuf_tensor("c_oT", [128, KC * TOK], BF16))
        phase_gate(ctx, block, bar)

        with ExitStack() as p2:
            featT = p2.enter_context(nc.sbuf_tensor("c_featT", [128, KC * TOK],
                                                    BF16))
            with ExitStack() as pl:
                io = alloc_io(ctx, pl, "c_")
                emit_preln(ctx, block, "CL", dram["csrc"], featT, io)
            qT = p2.enter_context(nc.sbuf_tensor("c_qT", [128, KC * TOK], BF16))
            kT = p2.enter_context(nc.sbuf_tensor("c_kT", [128, KC * TOK], BF16))
            vT = p2.enter_context(nc.sbuf_tensor("c_vT", [128, KC * TOK], BF16))
            ppj = p2.enter_context(ExitStack())
            pmm = [ppj.enter_context(nc.psum_tensor("c_pm%d" % i, [128, 512], F32))
                   for i in range(4)]

            @block.gpsimd
            def _(g):
                for wi, wn in enumerate(["l1_wq", "l1_wk", "l1_wv"]):
                    for half in range(2):
                        if wi * 2 + half > 0:
                            ctx.wv(g, "Cpj", (wi * 2 + half) * 8)
                        else:
                            ctx.wv(g, "CLtre", LT * KC)
                        for k in range(KC):
                            ctx.inc("Cwl", g.dma_start(
                                out=wb[:, k * 1024:(k + 1) * 1024],
                                in_=dram[wn][k * 128:(k + 1) * 128,
                                             half * 1024:(half + 1) * 1024]), 16)

            @block.tensor
            def _(te):
                ctx.wv(te, "CLtre", LT * KC)
                for wi in range(3):
                    for half in range(2):
                        ctx.wv(te, "Cwl", 16 * KC * (wi * 2 + half + 1))
                        for m8 in range(8):
                            for th in range(2):
                                idx = ((wi * 2 + half) * 8 + m8) * 2 + th
                                if idx >= 4:
                                    ctx.wv(te, "Cpe", idx - 3)
                                p = pmm[idx % 4]
                                for k in range(KC):
                                    i = te.matmul(
                                        p[:],
                                        wb[:, k * 1024 + m8 * 128:
                                           k * 1024 + m8 * 128 + 128],
                                        fmv(featT, k, th * 512, 512),
                                        start=(k == 0), stop=(k == KC - 1))
                                ctx.inc("Cpj", i, 1) if th == 1 else None

            @block.vector
            def _(v):
                v = ctx.vg.bind(v)
                for wi, dst in enumerate([qT, kT, vT]):
                    for half in range(2):
                        for m8 in range(8):
                            m = half * 8 + m8
                            for th in range(2):
                                idx = ((wi * 2 + half) * 8 + m8) * 2 + th
                                ctx.wv(v, "Cpj", (idx // 2) + 1)
                                ctx.inc("Cpe", v.tensor_copy(
                                    fmv(dst, m, th * 512, 512), pmm[idx % 4][:]))

            ppj.close()
            with ExitStack() as p3:
                pk = [p3.enter_context(nc.sbuf_tensor("c_pk%d" % i, [128, 128],
                                                      BF16)) for i in range(2)]
                pq = [p3.enter_context(nc.sbuf_tensor("c_pq%d" % i, [128, 128],
                                                      BF16)) for i in range(2)]
                pv = [p3.enter_context(nc.sbuf_tensor("c_pv%d" % i, [128, 128],
                                                      BF16)) for i in range(2)]
                stm = [p3.enter_context(nc.sbuf_tensor("c_stm%d" % i, [128, 128],
                                                       BF16)) for i in range(2)]
                vbk = [p3.enter_context(nc.sbuf_tensor("c_vbk%d" % i, [128, 128],
                                                       BF16)) for i in range(2)]
                osb = [p3.enter_context(nc.sbuf_tensor("c_osb%d" % i, [128, 128],
                                                       BF16)) for i in range(2)]
                pst = [p3.enter_context(nc.psum_tensor("c_pst%d" % i, [128, 128],
                                                       F32)) for i in range(2)]
                pvt = [p3.enter_context(nc.psum_tensor("c_pvt%d" % i, [128, 128],
                                                       BF16)) for i in range(2)]
                po = [p3.enter_context(nc.psum_tensor("c_po%d" % i, [128, 128],
                                                      F32)) for i in range(2)]
                pot = [p3.enter_context(nc.psum_tensor("c_pot%d" % i, [128, 128],
                                                       BF16)) for i in range(2)]
                NG = TOK // 8

                def kview(buf, g):
                    return buf[:, :].rearrange("p (kh tok) -> p kh tok", kh=KC)[
                        :, :, g * 8:(g + 1) * 8]

                @block.gpsimd
                def _(g_):
                    ctx.wv(g_, "Cpe", 96)
                    for g in range(NG):
                        b = g % 2
                        if g >= 2:
                            ctx.wv(g_, "Cste", g - 1)
                            ctx.wv(g_, "Cvbe", g - 1)
                        g_.tensor_copy(pk[b][:].rearrange("p (a c) -> p a c", a=KC),
                                       kview(kT, g))
                        i = g_.tensor_copy(pq[b][:].rearrange("p (a c) -> p a c",
                                                              a=KC), kview(qT, g))
                        ctx.inc("Cpk", i, 1)
                        i = g_.tensor_copy(pv[b][:].rearrange("p (a c) -> p a c",
                                                              a=KC), kview(vT, g))
                        ctx.inc("Cpv", i, 1)

                @block.tensor
                def _(te):
                    for g in range(NG):
                        b = g % 2
                        ctx.wv(te, "Cpk", g + 1)
                        if g >= 2:
                            ctx.wv(te, "Cste", g - 1)
                        ctx.inc("Cst", te.matmul(pst[b][:], pk[b][:], pq[b][:],
                                                 start=True, stop=True))
                        ctx.wv(te, "Cpv", g + 1)
                        if g >= 2:
                            ctx.wv(te, "Cvbe", g - 1)
                        ctx.inc("Cvt", te.transpose(pvt[b][:], pv[b][:],
                                                    ctx.ident[:]))
                        ctx.wv(te, "Cste", g + 1)
                        ctx.wv(te, "Cvbe", g + 1)
                        if g >= 2:
                            ctx.wv(te, "Cose", g - 1)
                        ctx.inc("Cmo", te.matmul(po[b][:], stm[b][:], vbk[b][:],
                                                 start=True, stop=True))
                        ctx.wv(te, "Cose", g + 1)
                        if g >= 2:
                            ctx.wv(te, "Cote", g - 1)
                        ctx.inc("Cot", te.transpose(pot[b][:], osb[b][:],
                                                    ctx.ident[:]))

                @block.vector
                def _(v):
                    v = ctx.vg.bind(v)
                    for g in range(NG):
                        b = g % 2
                        ctx.wv(v, "Cst", g + 1)
                        if g >= 2:
                            ctx.wv(v, "Cmo", g - 1)
                        i = v.tensor_tensor(stm[b][:], pst[b][:], ctx.dmask[:],
                                            ALU.mult)
                        ctx.inc("Cste", i)
                        ctx.wv(v, "Cvt", g + 1)
                        i = v.tensor_copy(vbk[b][:], pvt[b][:])
                        ctx.inc("Cvbe", i)
                        ctx.wv(v, "Cmo", g + 1)
                        if g >= 2:
                            ctx.wv(v, "Cot", g - 1)
                        ctx.inc("Cose", v.tensor_copy(osb[b][:], po[b][:]))
                        ctx.wv(v, "Cot", g + 1)
                        ctx.inc("Cote", v.tensor_copy(
                            oT[:, :].rearrange("p (h tok) -> p h tok", h=KC)[
                                :, :, g * 8:(g + 1) * 8],
                            pot[b][:].rearrange("p (h tok) -> p h tok", h=KC)))

        with ExitStack() as p4:
            acc = p4.enter_context(nc.sbuf_tensor("c_acc", [128, LT * D], F32))
            io2 = alloc_io(ctx, p4, "c2_", slim=True)
            pmm = [p4.enter_context(nc.psum_tensor("c2_pm%d" % i, [128, 512], F32))
                   for i in range(4)]

            @block.gpsimd
            def _(g):
                for half in range(2):
                    if half == 0:
                        ctx.wv(g, "Cpj", 48)
                    else:
                        ctx.wv(g, "Com", 2 * LT)
                    for k in range(KC):
                        ctx.inc("Cwo", g.dma_start(
                            out=wb[:, k * 1024:(k + 1) * 1024],
                            in_=dram["l1_wo"][k * 128:(k + 1) * 128,
                                              half * 1024:(half + 1) * 1024]), 16)

            @block.tensor
            def _(te):
                ctx.wv(te, "Cote", TOK // 8)
                for half in range(2):
                    ctx.wv(te, "Cwo", 16 * KC * (half + 1))
                    for t in range(LT):
                        for n in range(2):
                            idx = 2 * LT * half + t * 2 + n
                            if idx >= 4:
                                ctx.wv(te, "Coe", idx - 3)
                            p = pmm[idx % 4]
                            for h in range(KC):
                                i = te.matmul(
                                    p[:], fmv(oT, h, t * 128, 128),
                                    wb[:, h * 1024 + n * 512:
                                       h * 1024 + n * 512 + 512],
                                    start=(h == 0), stop=(h == KC - 1))
                            ctx.inc("Com", i)

            rdy = [("Coe", 2 * LT + 2 * t + 2) for t in range(LT)]

            @block.vector
            def _(v):
                v = ctx.vg.bind(v)
                for half in range(2):
                    for t in range(LT):
                        for n in range(2):
                            idx = 2 * LT * half + t * 2 + n
                            ctx.wv(v, "Com", idx + 1)
                            co = half * 1024 + n * 512
                            i = v.tensor_copy(acc[:, t * D + co:t * D + co + 512],
                                              pmm[idx % 4][:])
                            ctx.inc("Coe", i)

            emit_epilogue(ctx, block, "CE", acc, dram["csrc"], dram["cdst"],
                          io2, rdy)

            @block.sync
            def _(sync):
                ctx.wv(sync, "CEyo", ctx.val("CEyo"))

        return phase_close(ctx, block)


def emit_phase_D(ctx, block, dram, bar):
    nc = ctx.nc
    with ExitStack() as ps:
        phase_gate(ctx, block, bar)

        with ExitStack() as p1:
            wsl = [p1.enter_context(nc.sbuf_tensor("d_wsl%d" % i, [128, 16384], F32))
                   for i in range(2)]
            tp = p1.enter_context(nc.sbuf_tensor("d_tp", [128, 8192], BF16))
            tn = p1.enter_context(nc.sbuf_tensor("d_tn", [128, 8192], BF16))
            th = p1.enter_context(nc.sbuf_tensor("d_th", [128, 8192], BF16))
            sc = p1.enter_context(nc.sbuf_tensor("d_sc", [128, 8], F32))
            ppart = p1.enter_context(nc.sbuf_tensor("d_ppart", [2, 128], F32))
            scb = p1.enter_context(nc.sbuf_tensor("d_scb", [1, 8], BF16))
            slo = p1.enter_context(nc.sbuf_tensor("d_slo", [1, 16], F32))
            gb = p1.enter_context(nc.sbuf_tensor("d_gb", [128, 8], F32))
            pb = p1.enter_context(nc.psum_tensor("d_pb", [128, 8], F32))

            @block.gpsimd
            def _(g):
                ctx.inc("Dsl", g.dma_start(
                    out=wsl[0][:].rearrange("p (a c) -> p a c", a=2),
                    in_=dram["wfc_slice"].rearrange("(a p) c -> p a c", p=128)), 16)
                ctx.inc("Dsl", g.dma_start(
                    out=wsl[1][:].rearrange("p (a c) -> p a c", a=8),
                    in_=dram["wp_slice"].rearrange("(a p) c -> p a c", p=128)), 16)

            @block.vector
            def _(v):
                v = ctx.vg.bind(v)
                ctx.wv(v, "Dsl", 32)
                v.tensor_reduce(sc[:, 0:1], wsl[0][:], AX.X, ALU.add,
                                apply_absolute_value=True)
                i = v.tensor_reduce(sc[:, 1:2], wsl[1][:], AX.X, ALU.add,
                                    apply_absolute_value=True)
                ctx.inc("Dgs", i)

            @block.gpsimd
            def _(g):
                ctx.wv(g, "Dgs", 1)
                ctx.inc("Dgb", g.dma_start(out=dram["psc"][0:128, 0:2],
                                           in_=sc[:, 0:2]), 16)
                ctx.wv(g, "Dgb", 16)
                ctx.inc("Dgb", g.dma_start(
                    out=ppart[:, :],
                    in_=dram["psc"][0:128, 0:2].rearrange("p c -> c p"),
                    allow_slow_non_contiguous=True), 16)

            @block.vector
            def _(v):
                v = ctx.vg.bind(v)
                ctx.wv(v, "Dgb", 32)
                i = v.tensor_reduce(sc[0:2, 2:3], ppart[:, :], AX.X, ALU.add)
                ctx.inc("Dgr", i)

            @block.gpsimd
            def _(g):
                ctx.wv(g, "Dgr", 1)
                ctx.inc("Dar0", g.dma_start(out=dram["ar_in"], in_=sc[0:2, 2:3]), 16)
                ctx.wv(g, "Dar0", 16)
                ctx.inc("Dcc", g.collective_compute(
                    "AllReduce", ALU.add, replica_groups=[list(range(NC))],
                    ins=[dram["ar_in"].opt()], outs=[dram["ar_out"].opt()]))
                ctx.wv(g, "Dcc", 1)
                ctx.inc("Dar1", g.dma_start(out=sc[0:2, 3:4], in_=dram["ar_out"]), 16)

            @block.gpsimd
            def _(g):
                ctx.wv(g, "Dar1", 16)
                ctx.inc("Dar2", g.dma_start(out=dram["psc"][1:2, 0:2],
                                            in_=sc[0:2, 3:4]), 16)
                ctx.wv(g, "Dar2", 16)
                ctx.inc("Dar2", g.dma_start(out=slo[0:1, 0:2],
                                            in_=dram["psc"][1:2, 0:2]), 16)

            @block.vector
            def _(v):
                v = ctx.vg.bind(v)
                ctx.wv(v, "Dar2", 32)
                v.tensor_scalar(slo[0:1, 2:4], slo[0:1, 0:2],
                                1.0 / (2048.0 * 8192.0), None, ALU.mult)
                v.tensor_scalar(slo[0:1, 4:6], slo[0:1, 2:4], EPS_BL, 0.5,
                                ALU.add, ALU.mult)
                v.tensor_copy(slo[0:1, 8:12].rearrange("o (a b) -> o a b", b=2)
                              [:, :, 0:1],
                              slo[0:1, 2:4].rearrange("o (a b) -> o a b", b=1))
                v.tensor_copy(slo[0:1, 8:12].rearrange("o (a b) -> o a b", b=2)
                              [:, :, 1:2],
                              slo[0:1, 4:6].rearrange("o (a b) -> o a b", b=1))
                v.tensor_copy(scb[0:1, 0:4], slo[0:1, 8:12])
                v.tensor_tensor(slo[0:1, 12:16], slo[0:1, 8:12], scb[0:1, 0:4],
                                ALU.subtract)
                i = v.tensor_copy(scb[0:1, 4:8], slo[0:1, 12:16])
                ctx.inc("Dlo", i)

            @block.tensor
            def _(te):
                ctx.wv(te, "Dlo", 1)
                te.matmul(pb[:, 0:4], ctx.ones_row[:, 0:128], scb[0:1, 0:4],
                          start=True, stop=False)
                i = te.matmul(pb[:, 0:4], ctx.ones_row[:, 0:128], scb[0:1, 4:8],
                              start=False, stop=True)
                ctx.inc("Dbc", i)

            @block.vector
            def _(v):
                v = ctx.vg.bind(v)
                ctx.wv(v, "Dbc", 1)
                v.tensor_copy(gb[:, 0:4], pb[:, 0:4])
                i = v.tensor_scalar(gb[:, 4:6], gb[:, 0:2], -1.0, None, ALU.mult)
                v.tensor_scalar(gb[:, 6:8], gb[:, 2:4], -1.0, None, ALU.mult)
                ctx.wv(v, "Dzf", 1 + 32 * 16)
                for m in range(2):
                    gc, tc = (0, 1) if m == 0 else (2, 3)
                    ntc = 5 if m == 0 else 7
                    for hh in range(2):
                        w = wsl[m][:, hh * 8192:(hh + 1) * 8192]
                        v.tensor_scalar(tp[:], w, gb[:, tc:tc + 1],
                                        gb[:, gc:gc + 1], ALU.is_gt, ALU.mult)
                        v.tensor_scalar(tn[:], w, gb[:, ntc:ntc + 1],
                                        gb[:, gc:gc + 1], ALU.is_lt, ALU.mult)
                        if m * 2 + hh >= 1:
                            ctx.wv(v, "Dst", 16 * (m * 2 + hh))
                        i = v.tensor_tensor(th[:], tp[:], tn[:], ALU.subtract)
                        ctx.inc("Dtz", i)

            @block.gpsimd
            def _(g):
                g.memset(tp[:], 0.0).then_inc(ctx.sem("Dzf")[0], 1)
                ctx.sem("Dzf")[1] += 1
                ctx.wv(g, "Dzf", ctx.val("Dzf"))
                for stgn, grp in (("stage_wfc", 1), ("stage_wp", 4)):
                    for i in range(16):
                        ctx.inc("Dzf", g.dma_start(
                            out=dram[stgn].rearrange("(i p) c -> p i c", p=128)
                            [:, i * grp:(i + 1) * grp, :],
                            in_=tp[:].rearrange("p (o c) -> p o c", o=grp)), 16)
                with g.register("rkr3") as rkr3:
                    g.reg_load(rkr3, ctx.rks[0:1, 0:1])
                    for m in range(2):
                        stgn = "stage_wfc" if m == 0 else "stage_wp"
                        na = 2 if m == 0 else 8
                        rows = 256 if m == 0 else 1024
                        for hh in range(2):
                            ctx.wv(g, "Dtz", m * 2 + hh + 1)
                            if m == 0 and hh == 0:
                                ctx.wv(g, "Dzf", ctx.val("Dzf"))
                            for r in range(NC):
                                with g.If_eq(rkr3, r):
                                    dst = dram[stgn][r * rows:(r + 1) * rows,
                                                     :].rearrange(
                                        "(a p) c -> p a c", p=128)[
                                        :, hh * (na // 2):(hh + 1) * (na // 2), :]
                                    i = g.dma_start(
                                        out=dst,
                                        in_=th[:].rearrange("p (o c) -> p o c",
                                                            o=na // 2))
                                    i.then_inc(ctx.sem("Dst")[0], 16)
                            ctx.sem("Dst")[1] += 16
                ctx.wv(g, "Dst", 64)
                ctx.inc("Dcc", g.collective_compute(
                    "AllReduce", ALU.add, replica_groups=[list(range(NC))],
                    ins=[dram["stage_wfc"].opt()], outs=[dram["gath_wfc"].opt()]))
                ctx.inc("Dcc", g.collective_compute(
                    "AllReduce", ALU.add, replica_groups=[list(range(NC))],
                    ins=[dram["stage_wp"].opt()], outs=[dram["gath_wp"].opt()]))
                ctx.inc("Dcc", g.collective_compute(
                    "AllReduce", ALU.add, replica_groups=[list(range(NC))],
                    ins=[dram["ar_in"].opt()], outs=[dram["fence_d"].opt()]))

        if D_STAGES < 2:
            @block.sync
            def _(sync):
                ctx.wv(sync, "Dcc", 4)
                for t in range(LT):
                    ctx.inc("Dby", sync.dma_start(
                        out=dram["ddst"][t * 128:(t + 1) * 128, :],
                        in_=dram["dsrc"][t * 128:(t + 1) * 128, :]), 16)
                ctx.wv(sync, "Dby", 16 * LT)
            return phase_close(ctx, block)

        with ExitStack() as p2:
            hp_sum = p2.enter_context(nc.sbuf_tensor("d_hps", [128, 128], F32))
            hp_sq = p2.enter_context(nc.sbuf_tensor("d_hpq", [128, 128], F32))
            hp_max = p2.enter_context(nc.sbuf_tensor("d_hpx", [128, 32], F32))
            hp_min = p2.enter_context(nc.sbuf_tensor("d_hpn", [128, 32], F32))
            qsc = p2.enter_context(nc.sbuf_tensor("d_qsc", [128, 48], F32))
            qs2 = p2.enter_context(nc.sbuf_tensor("d_qs2", [128, 64], F32))

            pf = p2.enter_context(ExitStack())
            featT = pf.enter_context(nc.sbuf_tensor("d_featT", [128, KC * TOK],
                                                    BF16))
            with ExitStack() as pl:
                io = alloc_io(ctx, pl, "d_")
                if DEBUG_TAPS:
                    io["dbg_lnb"] = pl.enter_context(
                        nc.sbuf_tensor("d_dbglnb", [128, LT * D], BF16))
                    io["dbg_st"] = pl.enter_context(
                        nc.sbuf_tensor("d_dbgst", [128, LT * 16], F32))
                emit_preln(ctx, block, "DL", dram["dsrc"], featT, io, eps=EPS_BL,
                           quant=True, gate=("Dst", 64))
                if DEBUG_TAPS:
                    @block.sync
                    def _(sync):
                        ctx.wv(sync, "DLtre", LT * KC)
                        ctx.inc("Ddbg", sync.dma_start(out=dram["dbg_dlnb"],
                                                       in_=io["dbg_lnb"][:]), 16)
                        ctx.inc("Ddbg", sync.dma_start(out=dram["dbg_dst"],
                                                       in_=io["dbg_st"][:]), 16)

            if True:
                p3 = pf
                wfb = p3.enter_context(nc.sbuf_tensor("d_wfb", [128, KC * D], BF16))
                htl = [p3.enter_context(nc.sbuf_tensor("d_ht%d" % i, [128, D], BF16))
                       for i in range(2)]
                hsq = p3.enter_context(nc.sbuf_tensor("d_hsq", [128, D], BF16))
                pmm = [p3.enter_context(nc.psum_tensor("d_pm%d" % i, [128, 512], F32))
                       for i in range(8)]

                @block.sync
                def _(sync):
                    ctx.wv(sync, "DLtre", LT * KC)
                    for s_ in range(4):
                        if s_ >= 1:
                            ctx.wv(sync, "Dp1", s_ * LT * 4)
                        ctx.wv(sync, "Dcc", 4)
                        for k in range(KC):
                            ctx.inc("Dwf", sync.dma_start(
                                out=wfb[:, k * D:(k + 1) * D],
                                in_=dram["gath_wfc"][k * 128:(k + 1) * 128,
                                                     s_ * D:(s_ + 1) * D]), 16)

                @block.tensor
                def _(te):
                    ctx.wv(te, "DLtre", LT * KC)
                    for s_ in range(4):
                        ctx.wv(te, "Dwf", 16 * KC * (s_ + 1))
                        for t in range(LT):
                            for n in range(4):
                                idx = (s_ * LT + t) * 4 + n
                                if idx >= 8:
                                    ctx.wv(te, "Dge", idx - 7)
                                p = pmm[idx % 8]
                                for k in range(KC):
                                    i = te.matmul(
                                        p[:], fmv(featT, k, t * 128, 128),
                                        wfb[:, k * D + n * 512:k * D + n * 512 + 512],
                                        start=(k == 0), stop=(k == KC - 1))
                                ctx.inc("Dp1", i)

                @block.scalar
                def _(s):
                    for s_ in range(4):
                        for t in range(LT):
                            st_i = s_ * LT + t
                            b = st_i % 2
                            if st_i >= 2:
                                ctx.wv(s, "Dhs", 16 * (st_i - 1))
                                ctx.wv(s, "Dsqe", st_i - 1)
                            for n in range(4):
                                idx = st_i * 4 + n
                                ctx.wv(s, "Dp1", idx + 1)
                                pc = s_ * 32 + t * 4 + n
                                ctx.inc("Dge", s.activation(
                                    htl[b][:, n * 512:(n + 1) * 512],
                                    pmm[idx % 8][:], AF.Gelu,
                                    accum_out=hp_sum[:, pc:pc + 1]))
                            ctx.wv(s, "Dge", st_i * 4 + 4)
                            ctx.inc("Dsq", s.activation(
                                hsq[:], htl[b][:], AF.Square,
                                accum_out=hp_sq[:, s_ * 32 + t * 4:
                                                 s_ * 32 + t * 4 + 1]))

                @block.vector
                def _(v):
                    v = ctx.vg.bind(v)
                    for s_ in range(4):
                        for t in range(LT):
                            st_i = s_ * LT + t
                            b = st_i % 2
                            ctx.wv(v, "Dsq", st_i + 1)
                            v.tensor_reduce(hp_max[:, s_ * 8 + t:s_ * 8 + t + 1],
                                            htl[b][:], AX.X, ALU.max)
                            i = v.tensor_reduce(hp_min[:, s_ * 8 + t:s_ * 8 + t + 1],
                                                htl[b][:], AX.X, ALU.min)
                            ctx.inc("Dsqe", i)

                @block.gpsimd
                def _(g):
                    for s_ in range(4):
                        for t in range(LT):
                            st_i = s_ * LT + t
                            ctx.wv(g, "Dsqe", st_i + 1)
                            ctx.inc("Dhs", g.dma_start(
                                out=dram["h_spill"][t * 128:(t + 1) * 128,
                                                    s_ * D:(s_ + 1) * D],
                                in_=htl[st_i % 2][:]), 16)

            pf.close()

            if DEBUG_TAPS:
                @block.sync
                def _(sync):
                    ctx.wv(sync, "DLtre", LT * KC)
                    ctx.inc("Ddbg", sync.dma_start(out=dram["dbg_dfeat"],
                                                   in_=featT[:]), 16)
                    ctx.wv(sync, "Dhs", 16 * 32)
                    ctx.inc("Ddbg", sync.dma_start(
                        out=dram["dbg_dh"], in_=dram["h_spill"][:, :]), 16)
                    ctx.wv(sync, "Dcc", 4)
                    ctx.inc("Ddbg", sync.dma_start(
                        out=dram["dbg_dwt"], in_=dram["gath_wfc"][0:128, :]), 16)
                    ctx.inc("Ddbg", sync.dma_start(
                        out=dram["dbg_dwp"], in_=dram["gath_wp"][0:128, :]), 16)
                    ctx.wv(sync, "Ddbg", 64)

            if D_STAGES < 3:
                @block.sync
                def _(sync):
                    ctx.wv(sync, "Dhs", 16 * 32)
                    for t in range(LT):
                        ctx.inc("Dby", sync.dma_start(
                            out=dram["ddst"][t * 128:(t + 1) * 128, :],
                            in_=dram["dsrc"][t * 128:(t + 1) * 128, :]), 16)
                    ctx.wv(sync, "Dby", 16 * LT)
                return phase_close(ctx, block)

            @block.vector
            def _(v):
                v = ctx.vg.bind(v)
                ctx.wv(v, "Dhs", 16 * 32)
                for t in range(LT):
                    c = t * 6
                    v.tensor_reduce(
                        qsc[:, c:c + 1],
                        hp_sum[:, :].rearrange("p (s f) -> p s f", s=4)[
                            :, :, t * 4:t * 4 + 4], AX.XY, ALU.add)
                    v.tensor_reduce(
                        qsc[:, c + 1:c + 2],
                        hp_sq[:, :].rearrange("p (s f) -> p s f", s=4)[
                            :, :, t * 4:t * 4 + 1], AX.XY, ALU.add)
                    v.tensor_reduce(
                        qsc[:, c + 2:c + 3],
                        hp_max[:, :].rearrange("p (s f) -> p s f", s=4)[
                            :, :, t:t + 1], AX.XY, ALU.max)
                    i = v.tensor_reduce(
                        qsc[:, c + 3:c + 4],
                        hp_min[:, :].rearrange("p (s f) -> p s f", s=4)[
                            :, :, t:t + 1], AX.XY, ALU.min)
                    ctx.inc("Dqs", i)
                for t in range(LT):
                    c, o = t * 6, t * 8
                    v.tensor_scalar(qs2[:, o:o + 1], qsc[:, c:c + 1], 1.0 / 8192,
                                    None, ALU.mult)
                    v.tensor_scalar(qs2[:, o + 1:o + 2], qsc[:, c + 1:c + 2],
                                    1.0 / 8192, None, ALU.mult)
                    v.tensor_tensor(qs2[:, o + 2:o + 3], qs2[:, o:o + 1],
                                    qs2[:, o:o + 1], ALU.mult)
                    v.tensor_tensor(qs2[:, o + 2:o + 3], qs2[:, o + 1:o + 2],
                                    qs2[:, o + 2:o + 3], ALU.subtract)
                    i = v.tensor_scalar(qs2[:, o + 2:o + 3], qs2[:, o + 2:o + 3],
                                        EPS_BL, None, ALU.add)
                    ctx.inc("Dv1", i)
                ctx.wv(v, "Dv2", LT)
                for t in range(LT):
                    c, o = t * 6, t * 8
                    v.reciprocal(qs2[:, o + 3:o + 4], qs2[:, o + 7:o + 8])
                    v.tensor_tensor(qs2[:, o + 4:o + 5], qsc[:, c + 2:c + 3],
                                    qs2[:, o:o + 1], ALU.subtract)
                    v.tensor_tensor(qs2[:, o + 5:o + 6], qs2[:, o:o + 1],
                                    qsc[:, c + 3:c + 4], ALU.subtract)
                    v.tensor_tensor(qs2[:, o + 4:o + 5], qs2[:, o + 4:o + 5],
                                    qs2[:, o + 5:o + 6], ALU.max)
                    v.reciprocal(qs2[:, o + 5:o + 6], qs2[:, o + 4:o + 5])
                    v.tensor_scalar(qs2[:, o + 5:o + 6], qs2[:, o + 5:o + 6], QB,
                                    None, ALU.mult)
                    v.tensor_tensor(qs2[:, o + 6:o + 7], qs2[:, o + 3:o + 4],
                                    qs2[:, o + 4:o + 5], ALU.mult)
                    v.tensor_scalar(qs2[:, o + 6:o + 7], qs2[:, o + 6:o + 7],
                                    1.0 / QB, None, ALU.mult)
                    v.tensor_tensor(qs2[:, o + 7:o + 8], qs2[:, o:o + 1],
                                    qs2[:, o + 5:o + 6], ALU.mult)
                    i = v.tensor_scalar(qs2[:, o + 7:o + 8], qs2[:, o + 7:o + 8],
                                        -1.0, None, ALU.mult)
                    ctx.inc("Dv3", i)

            @block.scalar
            def _(s):
                ctx.wv(s, "Dv1", LT)
                for t in range(LT):
                    o = t * 8
                    ctx.inc("Dv2", s.activation(qs2[:, o + 7:o + 8],
                                                qs2[:, o + 2:o + 3], AF.Sqrt))

            if DEBUG_TAPS:
                @block.sync
                def _(sync):
                    ctx.wv(sync, "Dv3", LT)
                    ctx.inc("Ddbg", sync.dma_start(out=dram["dbg_dqs"],
                                                   in_=qs2[:]), 16)

            if D_STAGES < 4:
                @block.sync
                def _(sync):
                    ctx.wv(sync, "Dv3", LT)
                    for t in range(LT):
                        ctx.inc("Dby", sync.dma_start(
                            out=dram["ddst"][t * 128:(t + 1) * 128, :],
                            in_=dram["dsrc"][t * 128:(t + 1) * 128, :]), 16)
                    ctx.wv(sync, "Dby", 16 * LT)
                return phase_close(ctx, block)

            with ExitStack() as p4:
                xh2 = p4.enter_context(nc.sbuf_tensor("d_xh2", [128, KC * 1024], BF16))
                wpb = p4.enter_context(nc.sbuf_tensor("d_wpb", [128, 8 * D], BF16))
                acc = p4.enter_context(nc.sbuf_tensor("d_acc", [128, LT * D], F32))
                hl = [p4.enter_context(nc.sbuf_tensor("d_hl%d" % i, [128, D], BF16))
                      for i in range(2)]
                t1 = p4.enter_context(nc.sbuf_tensor("d_t1", [128, D], F32))
                xq = [p4.enter_context(nc.sbuf_tensor("d_xq%d" % i, [128, D], BF16))
                      for i in range(2)]
                ptr2 = [p4.enter_context(nc.psum_tensor("d_pt2%d" % i, [128, 128],
                                                        BF16)) for i in range(2)]
                pm2 = [p4.enter_context(nc.psum_tensor("d_pn%d" % i, [128, 512], F32))
                       for i in range(4)]

                @block.gpsimd
                def _(g):
                    ctx.wv(g, "Dv3", LT)
                    for s_ in range(4):
                        for t in range(LT):
                            st_i = s_ * LT + t
                            if st_i >= 2:
                                ctx.wv(g, "Dq2", st_i - 1)
                            ctx.inc("Dhl", g.dma_start(
                                out=hl[st_i % 2][:],
                                in_=dram["h_spill"][t * 128:(t + 1) * 128,
                                                    s_ * D:(s_ + 1) * D]), 16)

                @block.scalar
                def _(s):
                    ctx.wv(s, "Dv3", LT)
                    for s_ in range(4):
                        for t in range(LT):
                            st_i = s_ * LT + t
                            o = t * 8
                            ctx.wv(s, "Dhl", 16 * (st_i + 1))
                            if st_i >= 1:
                                ctx.wv(s, "Dclip", st_i)
                            ctx.inc("Dq1", s.activation(t1[:], hl[st_i % 2][:],
                                                        AF.Identity,
                                                        bias=qs2[:, o + 7:o + 8],
                                                        scale=qs2[:, o + 5:o + 6]))
                            ctx.wv(s, "Dclip", st_i + 1)
                            if st_i >= 2:
                                ctx.wv(s, "Dtr2", (st_i - 1) * KC)
                            ctx.inc("Dq2", s.activation(xq[st_i % 2][:], t1[:],
                                                        AF.Identity,
                                                        scale=qs2[:, o + 6:o + 7]))


                @block.sync
                def _(sync):
                    ctx.wv(sync, "Dv3", LT)
                    for s_ in range(4):
                        for kh in range(2):
                            sk = s_ * 2 + kh
                            if sk >= 1:
                                ctx.wv(sync, "Dp2", sk * LT * 4)
                            for k in range(8):
                                ctx.inc("Dwp", sync.dma_start(
                                    out=wpb[:, k * D:(k + 1) * D],
                                    in_=dram["gath_wp"][
                                        s_ * D + kh * 1024 + k * 128:
                                        s_ * D + kh * 1024 + (k + 1) * 128, :]), 16)

                @block.tensor
                def _(te):
                    for s_ in range(4):
                        for t in range(LT):
                            st_i = s_ * LT + t
                            ctx.wv(te, "Dq2", st_i + 1)
                            for k in range(KC):
                                idx = st_i * KC + k
                                if idx >= 2:
                                    ctx.wv(te, "Dtre2", idx - 1)
                                ctx.inc("Dtr2", te.transpose(
                                    ptr2[k % 2][:],
                                    xq[st_i % 2][:, k * 128:(k + 1) * 128],
                                    ctx.ident[:]))
                        ctx.wv(te, "Dtre2", (s_ + 1) * LT * KC)
                        for kh in range(2):
                            sk = s_ * 2 + kh
                            ctx.wv(te, "Dwp", 16 * 8 * (sk + 1))
                            for t in range(LT):
                                for n in range(4):
                                    idx = (sk * LT + t) * 4 + n
                                    if idx >= 4:
                                        ctx.wv(te, "Dae", idx - 3)
                                    p = pm2[idx % 4]
                                    for k in range(8):
                                        i = te.matmul(
                                            p[:],
                                            xh2[:, (kh * 8 + k) * 1024 + t * 128:
                                                (kh * 8 + k) * 1024 + t * 128 + 128],
                                            wpb[:, k * D + n * 512:
                                                k * D + n * 512 + 512],
                                            start=(k == 0), stop=(k == 7))
                                    ctx.inc("Dp2", i)

                rdy = [("Dae", (7 * LT + t) * 4 + 4) for t in range(LT)]

                @block.vector
                def _(v):
                    v = ctx.vg.bind(v)
                    for s_ in range(4):
                        for t in range(LT):
                            st_i = s_ * LT + t
                            ctx.wv(v, "Dq1", st_i + 1)
                            ctx.inc("Dclip", v.tensor_scalar(t1[:], t1[:], CLIP,
                                                             -CLIP, ALU.min,
                                                             ALU.max))
                            for k in range(KC):
                                ctx.wv(v, "Dtr2", st_i * KC + k + 1)
                                ctx.inc("Dtre2", v.tensor_copy(
                                    xh2[:, k * 1024 + t * 128:
                                        k * 1024 + t * 128 + 128],
                                    ptr2[k % 2][:]))
                        for kh in range(2):
                            sk = s_ * 2 + kh
                            for t in range(LT):
                                for n in range(4):
                                    idx = (sk * LT + t) * 4 + n
                                    ctx.wv(v, "Dp2", idx + 1)
                                    dst = acc[:, t * D + n * 512:
                                              t * D + n * 512 + 512]
                                    if sk == 0:
                                        i = v.tensor_copy(dst, pm2[idx % 4][:])
                                    else:
                                        i = v.tensor_tensor(dst, pm2[idx % 4][:],
                                                            dst, ALU.add)
                                    ctx.inc("Dae", i)

                if DEBUG_TAPS:
                    @block.sync
                    def _(sync):
                        ctx.wv(sync, "Dae", 8 * LT * 4)
                        ctx.inc("Ddbg", sync.dma_start(out=dram["dbg_dacc"],
                                                       in_=acc[:]), 16)

                with ExitStack() as p5:
                    io2 = alloc_io(ctx, p5, "d2_", slim=True)
                    emit_epilogue(ctx, block, "DE", acc, dram["dsrc"],
                                  dram["ddst"], io2, rdy)

                    @block.sync
                    def _(sync):
                        ctx.wv(sync, "DEyo", ctx.val("DEyo"))

        return phase_close(ctx, block)


def lt0_of(jg):
    return 0 if jg <= 8 else (jg - 1) // 8


def emit_phase_A(ctx, block, dram, bar):
    nc = ctx.nc
    with ExitStack() as ps:
        wb = ps.enter_context(nc.sbuf_tensor("a_wb", [128, KC * 1024], BF16))
        qoT = ps.enter_context(nc.sbuf_tensor("a_qoT", [128, KC * TOK], BF16))
        amask = ps.enter_context(nc.sbuf_tensor("a_amask", [128, 64 * 128], BF16))
        bias16 = ps.enter_context(nc.sbuf_tensor("a_bias16", [128, 16], F32))
        maxkb = ps.enter_context(nc.sbuf_tensor("a_maxkb", [128, 1], F32))
        phase_gate(ctx, block, bar)

        @block.gpsimd
        def _(g):
            ctx.inc("Aam", g.dma_start(
                out=amask[:].rearrange("p (a c) -> p a c", a=64),
                in_=dram["amask"].rearrange("(a p) c -> p a c", p=128)), 16)

        with ExitStack() as p2:
            featT = p2.enter_context(nc.sbuf_tensor("a_featT", [128, KC * TOK],
                                                    BF16))
            with ExitStack() as pl:
                io = alloc_io(ctx, pl, "a_")
                emit_preln(ctx, block, "AL", dram["asrc"], featT, io)

            zbuf = p2.enter_context(nc.sbuf_tensor("a_zbuf", [128, 8192], BF16))
            ktb = [p2.enter_context(nc.sbuf_tensor("a_ktb%d" % i, [128, 1024], BF16))
                   for i in range(2)]
            k2b = [p2.enter_context(nc.sbuf_tensor("a_k2b%d" % i, [128, 1024], BF16))
                   for i in range(2)]
            vtb = [p2.enter_context(nc.sbuf_tensor("a_vtb%d" % i, [128, 1024], BF16))
                   for i in range(2)]
            sqn = p2.enter_context(nc.sbuf_tensor("a_sqn", [128, 128], F32))
            mqp = p2.enter_context(nc.sbuf_tensor("a_mqp", [128, 16], F32))
            nsb = p2.enter_context(nc.sbuf_tensor("a_nsb", [128, 16], BF16))
            nrow = p2.enter_context(nc.sbuf_tensor("a_nrow", [1, 160], BF16))
            nsc = p2.enter_context(nc.sbuf_tensor("a_nsc", [128, 8], F32))
            mk8 = p2.enter_context(nc.sbuf_tensor("a_mk8", [1, 8], F32))
            pk_ = [p2.enter_context(nc.psum_tensor("a_pk%d" % i, [128, 512], F32))
                   for i in range(4)]
            pnk = p2.enter_context(nc.psum_tensor("a_pnk", [128, 128], F32))
            pnq = p2.enter_context(nc.psum_tensor("a_pnq", [128, 128], F32))
            pnr = p2.enter_context(nc.psum_tensor("a_pnr", [128, 16], F32))
            pnt = p2.enter_context(nc.psum_tensor("a_pnt", [16, 128], BF16))

            order = [("K", "l0_wkT"), ("V", "l0_wvT"), ("Q", "l0_wqT")]


            @block.tensor
            def _(te):
                for wi, (wname, wd) in enumerate(order):
                    for half in range(2):
                        gidx = wi * 2 + half
                        if wname == "Q" and half == 0:
                            ctx.wv(te, "Acc", 2)
                        ctx.wv(te, "Awl", 16 * KC * (gidx + 1))
                        for u in range(8):
                            for th in range(2):
                                idx = (gidx * 8 + u) * 2 + th
                                if idx >= 4:
                                    ctx.wv(te, "Ape", idx - 3)
                                p = pk_[idx % 4]
                                if wname == "V":
                                    for k in range(KC):
                                        i = te.matmul(
                                            p[:], fmv(featT, k, u * 128, 128),
                                            wb[:, k * 1024 + th * 512:
                                               k * 1024 + th * 512 + 512],
                                            start=(k == 0), stop=(k == KC - 1))
                                else:
                                    for k in range(KC):
                                        i = te.matmul(
                                            p[:],
                                            wb[:, k * 1024 + u * 128:
                                               k * 1024 + u * 128 + 128],
                                            fmv(featT, k, th * 512, 512),
                                            start=(k == 0), stop=(k == KC - 1))
                                ctx.inc("Amm", i, 1) if th == 1 else None
                            if wname == "K":
                                m = half * 8 + u
                                ctx.wv(te, "Aksq", m + 1)
                                for t in range(LT):
                                    i = te.matmul(pnk[:, m * 8 + t:m * 8 + t + 1],
                                                  k2b[m % 2][:, t * 128:(t + 1) * 128],
                                                  ctx.ones_col[:],
                                                  start=True, stop=True)
                                ctx.inc("Aknm", i)
                            elif wname == "Q":
                                m = half * 8 + u
                                ctx.wv(te, "Aqsq", m + 1)
                                for t in range(LT):
                                    i = te.matmul(pnq[:, m * 8 + t:m * 8 + t + 1],
                                                  k2b[m % 2][:, t * 128:(t + 1) * 128],
                                                  ctx.ones_col[:],
                                                  start=True, stop=True)
                                ctx.inc("Aqnm", i)

            @block.vector
            def _(v):
                v = ctx.vg.bind(v)
                for wi, (wname, wd) in enumerate(order):
                    for half in range(2):
                        gidx = wi * 2 + half
                        for u in range(8):
                            m = half * 8 + u
                            for th in range(2):
                                idx = (gidx * 8 + u) * 2 + th
                                ctx.wv(v, "Amm", (idx // 2) + 1)
                                if wname == "K":
                                    if m >= 2 and th == 0:
                                        ctx.wv(v, "Aks", 32 * (m - 1))
                                        ctx.wv(v, "Aksq", m - 1)
                                    dst = ktb[m % 2][:, th * 512:(th + 1) * 512]
                                elif wname == "Q":
                                    dst = fmv(qoT, m, th * 512, 512)
                                else:
                                    vu = (gidx - 2) * 8 + u
                                    if vu >= 2 and th == 0:
                                        ctx.wv(v, "Avs", 32 * (vu - 1))
                                    dst = vtb[u % 2][:, th * 512:(th + 1) * 512]
                                ctx.inc("Ape", v.tensor_copy(dst, pk_[idx % 4][:]))

            @block.scalar
            def _(s):
                for wname, cnt_sem in (("K", "Aksq"), ("Q", "Aqsq")):
                    for m in range(16):
                        uidx = m if wname == "K" else 32 + m
                        ctx.wv(s, "Ape", (uidx + 1) * 2)
                        if wname == "Q" and m < 2:
                            ctx.wv(s, "Aknm", 16)
                        if m >= 2:
                            ctx.wv(s, "Aknm" if wname == "K" else "Aqnm", m - 1)
                        ctx.inc(cnt_sem, s.activation(k2b[m % 2][:],
                                                      ktb[m % 2][:] if wname == "K"
                                                      else fmv(qoT, m, 0, 1024),
                                                      AF.Square))

            def stream_half(g, wd, gidx, half):
                if gidx > 0:
                    ctx.wv(g, "Amm", gidx * 8)
                for k in range(KC):
                    ctx.inc("Awl", g.dma_start(
                        out=wb[:, k * 1024:(k + 1) * 1024],
                        in_=dram[wd][k * 128:(k + 1) * 128,
                                     half * 1024:(half + 1) * 1024]), 16)

            def kstage(g, rkreg, ms):
                for m in ms:
                    ctx.wv(g, "Ape", (m + 1) * 2)
                    for r in range(NC):
                        with g.If_eq(rkreg, r):
                            i = g.dma_start(
                                out=dram["stage_k"][r * D + m * 128:
                                                    r * D + (m + 1) * 128, :],
                                in_=ktb[m % 2][:])
                            i.then_inc(ctx.sem("Aks")[0], 16)
                    ctx.sem("Aks")[1] += 16
                    ctx.inc("Aks", g.dma_start(
                        out=dram["stage_k2"][m * 128:(m + 1) * 128, :],
                        in_=ktb[m % 2][:]), 16)

            def vstage(g, rkreg, vus):
                for vu in vus:
                    half, t = vu // 8, vu % 8
                    ctx.wv(g, "Ape", (16 + vu + 1) * 2)
                    for r in range(NC):
                        with g.If_eq(rkreg, r):
                            dst = dram["stage_v"][
                                r * NH * TOK:(r + 1) * NH * TOK, :].rearrange(
                                "(h l p) e -> p h l e", p=128, l=LT)[
                                :, half * 8:(half + 1) * 8, t:t + 1, :]
                            i = g.dma_start(
                                out=dst,
                                in_=vtb[t % 2][:].rearrange(
                                    "p (h o e) -> p h o e", h=8, o=1))
                            i.then_inc(ctx.sem("Avs")[0], 16)
                    ctx.sem("Avs")[1] += 16
                    dst = dram["stage_v2"].rearrange(
                        "(h l p) e -> p h l e", p=128, l=LT)[
                        :, half * 8:(half + 1) * 8, t:t + 1, :]
                    ctx.inc("Avs", g.dma_start(
                        out=dst,
                        in_=vtb[t % 2][:].rearrange("p (h o e) -> p h o e",
                                                    h=8, o=1)), 16)

            @block.gpsimd
            def _(g):
                with g.register("rkr2") as rkr2:
                    g.reg_load(rkr2, ctx.rks[0:1, 0:1])
                    ctx.wv(g, "ALtre", LT * KC)
                    g.memset(zbuf[:], 0.0).then_inc(ctx.sem("Azf")[0], 1)
                    ctx.sem("Azf")[1] += 1
                    ctx.wv(g, "Azf", ctx.val("Azf"))
                    for i in range(16):
                        ctx.inc("Azf", g.dma_start(
                            out=dram["stage_k"].rearrange("(i p) c -> p i c",
                                                          p=128)[:, i * 8:
                                                                 (i + 1) * 8, :],
                            in_=zbuf[:, 0:8192].rearrange("p (i c) -> p i c",
                                                          i=8)), 16)
                    for i in range(16):
                        ctx.inc("Azf", g.dma_start(
                            out=dram["stage_v"].rearrange("(i p) c -> p i c",
                                                          p=128)[:, i * 64:
                                                                 (i + 1) * 64, :],
                            in_=zbuf[:, 0:8192].rearrange("p (i c) -> p i c",
                                                          i=64)), 16)
                    ctx.wv(g, "Azf", ctx.val("Azf"))
                    stream_half(g, "l0_wkT", 0, 0)
                    kstage(g, rkr2, range(0, 8))
                    stream_half(g, "l0_wkT", 1, 1)
                    kstage(g, rkr2, range(8, 16))
                    stream_half(g, "l0_wvT", 2, 0)
                    vstage(g, rkr2, range(0, 8))
                    stream_half(g, "l0_wvT", 3, 1)
                    vstage(g, rkr2, range(8, 16))
                    stream_half(g, "l0_wqT", 4, 0)
                    ctx.wv(g, "Aks", 512)
                    ctx.wv(g, "Avs", 512)
                    ctx.wv(g, "Awl", 16 * KC * 5)
                    ctx.wv(g, "Amm", 32)
                    ctx.wv(g, "Aknm", 16)
                    ctx.wv(g, "Aksq", 16)
                    ctx.inc("Acc", g.collective_compute(
                        "AllReduce", ALU.add, replica_groups=[list(range(NC))],
                        ins=[dram["stage_k"].opt()], outs=[dram["gath_k"].opt()]))
                    ctx.inc("Acc", g.collective_compute(
                        "AllReduce", ALU.add, replica_groups=[list(range(NC))],
                        ins=[dram["stage_v"].opt()], outs=[dram["gath_v"].opt()]))
                    stream_half(g, "l0_wqT", 5, 1)

            @block.vector
            def _(v):
                v = ctx.vg.bind(v)
                ctx.wv(v, "Acc", 2)
                ctx.wv(v, "Aknm", 16)
                v.tensor_copy(sqn[:], pnk[:])
                v.tensor_reduce(nsc[:, 0:1], sqn[:], AX.X, ALU.max)
                ctx.inc("Akn1", v.tensor_copy(nsb[:, 0:1], nsc[:, 0:1]))

            @block.tensor
            def _(te):
                ctx.wv(te, "Akn1", 1)
                ctx.inc("Akn2", te.transpose(pnt[0:1, :], nsb[:, 0:1],
                                             ctx.ident[:]))

            @block.vector
            def _(v):
                v = ctx.vg.bind(v)
                ctx.wv(v, "Akn2", 1)
                v.tensor_copy(nrow[0:1, 0:128], pnt[0:1, :])
                ctx.inc("Akn3", v.tensor_reduce(nsc[0:1, 1:2], nrow[0:1, 0:128],
                                                AX.X, ALU.max))

            @block.gpsimd
            def _(g):
                ctx.wv(g, "Akn3", 1)
                ctx.inc("Amk", g.dma_start(out=dram["stage_mk"],
                                           in_=nsc[0:1, 1:2]), 16)
                ctx.wv(g, "Amk", 16)
                ctx.inc("Acc", g.collective_compute(
                    "AllReduce", ALU.max, replica_groups=[list(range(NC))],
                    ins=[dram["stage_mk"].opt()], outs=[dram["gath_mk"].opt()]))
                ctx.inc("Acc", g.collective_compute(
                    "AllReduce", ALU.add, replica_groups=[list(range(NC))],
                    ins=[dram["stage_mk"].opt()], outs=[dram["fence_a"].opt()]))
                ctx.wv(g, "Acc", 4)
                ctx.inc("Amk", g.dma_start(out=mk8[0:1, 0:1],
                                           in_=dram["gath_mk"]), 16)

            @block.vector
            def _(v):
                v = ctx.vg.bind(v)
                ctx.wv(v, "Amk", 32)
                ctx.inc("Akn4", v.tensor_reduce(nsc[0:1, 2:3], mk8[0:1, 0:1],
                                                AX.X, ALU.max))

            @block.scalar
            def _(s):
                ctx.wv(s, "Akn4", 1)
                ctx.inc("Akn5", s.activation(nsc[0:1, 3:4], nsc[0:1, 2:3], AF.Sqrt))

            @block.vector
            def _(v):
                v = ctx.vg.bind(v)
                ctx.wv(v, "Akn5", 1)
                ctx.inc("Akn6", v.tensor_copy(nrow[0:1, 128:129], nsc[0:1, 3:4]))

            @block.tensor
            def _(te):
                ctx.wv(te, "Akn6", 1)
                ctx.inc("Akn7", te.matmul(pnr[:, 0:1], ctx.ones_row[:, 0:128],
                                          nrow[0:1, 128:129],
                                          start=True, stop=True))

            @block.vector
            def _(v):
                v = ctx.vg.bind(v)
                ctx.wv(v, "Akn7", 1)
                ctx.inc("Amaxk", v.tensor_copy(maxkb[:], pnr[:, 0:1]))

            @block.vector
            def _(v):
                v = ctx.vg.bind(v)
                ctx.wv(v, "Aqnm", 16)
                v.tensor_copy(sqn[:], pnq[:])
                for h in range(16):
                    v.tensor_reduce(mqp[:, h:h + 1], sqn[:, h * 8:(h + 1) * 8],
                                    AX.X, ALU.max)
                ctx.inc("Aqn1", v.tensor_copy(nsb[:, 0:16], mqp[:]))

            @block.tensor
            def _(te):
                ctx.wv(te, "Aqn1", 1)
                ctx.inc("Aqn2", te.transpose(pnt[:, :], nsb[:, 0:16], ctx.ident[:]))

            @block.vector
            def _(v):
                v = ctx.vg.bind(v)
                ctx.wv(v, "Aqn2", 1)
                v.tensor_copy(sqn[0:16, 0:128], pnt[:, :])
                ctx.inc("Aqn3", v.tensor_reduce(nsc[0:16, 4:5], sqn[0:16, 0:128],
                                                AX.X, ALU.max))

            @block.scalar
            def _(s):
                ctx.wv(s, "Aqn3", 1)
                ctx.inc("Aqn4", s.activation(nsc[0:16, 5:6], nsc[0:16, 4:5],
                                             AF.Sqrt))

            @block.vector
            def _(v):
                v = ctx.vg.bind(v)
                ctx.wv(v, "Aqn4", 1)
                ctx.wv(v, "Amaxk", 1)
                v.tensor_scalar(nsc[0:16, 6:7], nsc[0:16, 5:6], maxkb[0:16, 0:1],
                                -SCALE, ALU.mult, ALU.mult)
                ctx.inc("Aqn5", v.tensor_copy(nsb[0:16, 1:2], nsc[0:16, 6:7]))

            @block.tensor
            def _(te):
                ctx.wv(te, "Aqn5", 1)
                ctx.inc("Aqn6", te.transpose(pnt[0:1, 0:16], nsb[0:16, 1:2],
                                             ctx.ident[0:16, 0:16]))

            @block.vector
            def _(v):
                v = ctx.vg.bind(v)
                ctx.wv(v, "Aqn6", 1)
                ctx.inc("Aqn7", v.tensor_copy(nrow[0:1, 144:160], pnt[0:1, 0:16]))

            @block.tensor
            def _(te):
                ctx.wv(te, "Aqn7", 1)
                ctx.inc("Aqn8", te.matmul(pnr[:, 0:16], ctx.ones_row[:, 0:128],
                                          nrow[0:1, 144:160], start=True, stop=True))

            @block.vector
            def _(v):
                v = ctx.vg.bind(v)
                ctx.wv(v, "Aqn8", 1)
                ctx.inc("Abias", v.tensor_copy(bias16[:], pnr[:, 0:16]))

        with ExitStack() as p4:
            kbuf = [p4.enter_context(nc.sbuf_tensor("a_kb%d" % i, [128, 8192], BF16))
                    for i in range(2)]
            vbuf = [p4.enter_context(nc.sbuf_tensor("a_vb%d" % i, [128, 8192], BF16))
                    for i in range(2)]
            pt = [p4.enter_context(nc.sbuf_tensor("a_pt%d" % i, [128, 1024], BF16))
                  for i in range(2)]
            lrow = p4.enter_context(nc.sbuf_tensor("a_lrow", [1, 1024], BF16))
            lrow32 = p4.enter_context(nc.sbuf_tensor("a_lrow32", [1, 1024], F32))
            rb = p4.enter_context(nc.sbuf_tensor("a_rb", [128, 1024], F32))
            pST = [p4.enter_context(nc.psum_tensor("a_pST%d" % i, [128, 512], F32))
                   for i in range(4)]
            pOT = [p4.enter_context(nc.psum_tensor("a_pOT%d" % i, [128, 512], F32))
                   for i in range(2)]
            pL = [p4.enter_context(nc.psum_tensor("a_pL%d" % i, [1, 512], F32))
                  for i in range(2)]

            def kcol(jg):
                r, l = OWNER[jg]
                return r * 1024 + l * 128

            @block.gpsimd
            def _(g):
                ctx.wv(g, "Acc", 4)
                ctx.wv(g, "Abias", 1)
                with g.register("rkreg") as rkreg:
                    g.reg_load(rkreg, ctx.rks[0:1, 0:1])
                    for h in range(NH):
                        if h >= 2:
                            ctx.wv(g, "Ahd", h - 1)
                        for r in range(NC):
                            with g.If_eq(rkreg, r):
                                i = g.dma_start(
                                    out=kbuf[h % 2][:, r * 1024:(r + 1) * 1024],
                                    in_=dram["stage_k2"][h * 128:(h + 1) * 128, :])
                                i.then_inc(ctx.sem("Akv")[0], 16)
                                i = g.dma_start(
                                    out=vbuf[h % 2][:, r * 1024:(r + 1) * 1024]
                                    .rearrange("p (l e) -> p l e", l=LT),
                                    in_=dram["stage_v2"][h * 1024:(h + 1) * 1024,
                                                         :].rearrange(
                                        "(l p) e -> p l e", p=128))
                                i.then_inc(ctx.sem("Akv")[0], 16)
                            with g.Else():
                                i = g.dma_start(
                                    out=kbuf[h % 2][:, r * 1024:(r + 1) * 1024],
                                    in_=dram["gath_k"][r * 2048 + h * 128:
                                                       r * 2048 + (h + 1) * 128,
                                                       :])
                                i.then_inc(ctx.sem("Akv")[0], 16)
                                i = g.dma_start(
                                    out=vbuf[h % 2][:, r * 1024:(r + 1) * 1024]
                                    .rearrange("p (l e) -> p l e", l=LT),
                                    in_=dram["gath_v"][r * 16384 + h * 1024:
                                                       r * 16384 + (h + 1) * 1024,
                                                       :].rearrange(
                                        "(l p) e -> p l e", p=128))
                                i.then_inc(ctx.sem("Akv")[0], 16)
                            ctx.sem("Akv")[1] += 32
                        if DEBUG_TAPS and h == 2:
                            ctx.inc("Adbg", g.dma_start(out=dram["dbg_akb"],
                                                        in_=kbuf[0][:]), 16)
                            ctx.inc("Adbg", g.dma_start(out=dram["dbg_avb"],
                                                        in_=vbuf[0][:]), 16)

            def emit_S(te, h, jg):
                set_ = jg % 2
                lt0 = lt0_of(jg)
                ltm = (jg - 1) // 8 if jg >= 1 else -1
                kc = kcol(jg)
                last = None
                for (lo, hi) in seg_list(lt0 * 128, 1024):
                    bank = lo // 512
                    pdst = pST[set_ * 2 + bank]
                    has_m = ltm >= 0 and lo <= ltm * 128 < hi
                    last = te.matmul(pdst[:, lo - bank * 512:hi - bank * 512],
                                     kbuf[h % 2][:, kc:kc + 128],
                                     qT_att[:, h * 1024 + lo:h * 1024 + hi],
                                     start=True, stop=not has_m)
                    if has_m:
                        i_ = jg - 8 * ltm - 1
                        mc = ltm * 128 - bank * 512
                        last = te.matmul(pdst[:, mc:mc + 128], ctx.ident[:],
                                         amask[:, (ltm * 8 + i_) * 128:
                                               (ltm * 8 + i_) * 128 + 128],
                                         start=False, stop=True)
                ctx.inc("Ast", last)

            qT_att = qoT

            @block.tensor
            def _(te):
                for h in range(NH):
                    ctx.wv(te, "Akv", 256 * (h + 1))
                    if h >= 1:
                        ctx.wv(te, "Abce", 2 * h)
                        ctx.wv(te, "Aoev", 2 * h)
                        ctx.wv(te, "Alr", h)
                    else:
                        ctx.wv(te, "Abias", 1)
                    emit_S(te, h, 0)
                    emit_S(te, h, 1)
                    for jg in range(64):
                        lt0 = lt0_of(jg)
                        ctx.wv(te, "Aex", min(h * 64 + jg + 2, h * 64 + 64))
                        kc = kcol(jg)
                        for (lo, hi) in seg_list(lt0 * 128, 1024):
                            bank = lo // 512
                            te.matmul(pL[bank][:, lo - bank * 512:hi - bank * 512],
                                      ctx.ones_col[:],
                                      pt[jg % 2][:, lo:hi],
                                      start=(jg == 0),
                                      stop=(jg == (32 if bank == 0 else 63)))
                        last = None
                        for (lo, hi) in seg_list(lt0 * 128, 1024):
                            bank = lo // 512
                            last = te.matmul(
                                pOT[bank][:, lo - bank * 512:hi - bank * 512],
                                vbuf[h % 2][:, kc:kc + 128],
                                pt[jg % 2][:, lo:hi],
                                start=(jg == 0), stop=(jg == 63))
                        ctx.inc("Apv", last)
                        if jg + 2 <= 63:
                            emit_S(te, h, jg + 2)
                    ctx.wv(te, "Alr", h + 1)
                    i0 = te.matmul(pST[0][:, :], ctx.ones_row[:, 0:128],
                                   lrow[0:1, 0:512], start=True, stop=True)
                    i1 = te.matmul(pST[1][:, :], ctx.ones_row[:, 0:128],
                                   lrow[0:1, 512:1024], start=True, stop=True)
                    ctx.inc("Ahd", i0)
                    ctx.inc("Abc", i1)

            @block.scalar
            def _(s):
                for h in range(NH):
                    for jg in range(64):
                        set_ = jg % 2
                        lt0 = lt0_of(jg)
                        ctx.wv(s, "Ast", h * 64 + jg + 1)
                        if h * 64 + jg >= 2:
                            ctx.wv(s, "Apv", h * 64 + jg - 1)
                        last = None
                        for (lo, hi) in seg_list(lt0 * 128, 1024):
                            bank = lo // 512
                            last = s.activation(
                                pt[jg % 2][:, lo:hi],
                                pST[set_ * 2 + bank][:, lo - bank * 512:
                                                     hi - bank * 512],
                                AF.Exp, scale=SCALE, bias=bias16[:, h:h + 1])
                        if jg == 63:
                            last = s.copy(ctx.junk[1][:], ctx.junk[0][:])
                        ctx.inc("Aex", last)

            @block.vector
            def _(v):
                v = ctx.vg.bind(v)
                for h in range(NH):
                    ctx.wv(v, "Apv", h * 64 + 64)
                    for half in range(2):
                        v.tensor_scalar(pL[half][:], pL[half][:], 1e-30, None,
                                        ALU.add)
                        v.reciprocal(lrow32[0:1, half * 512:(half + 1) * 512],
                                     pL[half][:])
                        i = v.tensor_copy(lrow[0:1, half * 512:(half + 1) * 512],
                                          lrow32[0:1, half * 512:(half + 1) * 512])
                    ctx.inc("Alr", i)
                    ctx.wv(v, "Abc", h + 1)
                    for half in range(2):
                        ctx.inc("Abce", v.tensor_copy(
                            rb[:, half * 512:(half + 1) * 512], pST[half][:]))
                    for half in range(2):
                        ctx.inc("Aoev", v.tensor_tensor(
                            qoT[:, h * 1024 + half * 512:
                                h * 1024 + half * 512 + 512],
                            pOT[half][:], rb[:, half * 512:(half + 1) * 512],
                            ALU.mult))

        if DEBUG_TAPS:
            @block.sync
            def _(sync):
                ctx.wv(sync, "Abias", 1)
                ctx.inc("Adbg", sync.dma_start(out=dram["dbg_aqT"],
                                               in_=qoT[:]), 16)
                ctx.inc("Adbg", sync.dma_start(out=dram["dbg_agk"],
                                               in_=dram["gath_k"][0:2048, :]), 16)
                ctx.inc("Adbg", sync.dma_start(out=dram["dbg_ak2"],
                                               in_=dram["stage_k2"][:]), 16)
                ctx.wv(sync, "Aoev", 2 * NH)
                ctx.inc("Adbg", sync.dma_start(out=dram["dbg_aoT"],
                                               in_=qoT[:]), 16)
                ctx.wv(sync, "Abias", 1)
                ctx.inc("Adbg", sync.dma_start(out=dram["dbg_abias"][:, 0:16],
                                               in_=bias16[:]), 16)
                ctx.inc("Adbg", sync.dma_start(out=dram["dbg_abias"][:, 16:17],
                                               in_=maxkb[:],
                                               allow_slow_non_contiguous=True), 16)

        with ExitStack() as p5:
            acc = p5.enter_context(nc.sbuf_tensor("a_acc", [128, LT * D], F32))
            io2 = alloc_io(ctx, p5, "a2_", slim=True)
            pmm = [p5.enter_context(nc.psum_tensor("a5_pm%d" % i, [128, 512], F32))
                   for i in range(4)]

            @block.gpsimd
            def _(sync):
                for half in range(2):
                    if half == 0:
                        ctx.wv(sync, "Amm", 48)
                    else:
                        ctx.wv(sync, "Aom", 2 * LT)
                    for k in range(KC):
                        ctx.inc("Awo", sync.dma_start(
                            out=wb[:, k * 1024:(k + 1) * 1024],
                            in_=dram["l0_woT"][k * 128:(k + 1) * 128,
                                               half * 1024:(half + 1) * 1024]), 16)

            @block.tensor
            def _(te):
                ctx.wv(te, "Aoev", 2 * NH)
                for half in range(2):
                    ctx.wv(te, "Awo", 16 * KC * (half + 1))
                    for t in range(LT):
                        for n in range(2):
                            idx = 2 * LT * half + t * 2 + n
                            if idx >= 4:
                                ctx.wv(te, "Aoe", idx - 3)
                            p = pmm[idx % 4]
                            for h in range(KC):
                                i = te.matmul(
                                    p[:], fmv(qoT, h, t * 128, 128),
                                    wb[:, h * 1024 + n * 512:h * 1024 + n * 512 + 512],
                                    start=(h == 0), stop=(h == KC - 1))
                            ctx.inc("Aom", i)

            rdy = [("Aoe", 2 * LT + 2 * t + 2) for t in range(LT)]

            @block.vector
            def _(v):
                v = ctx.vg.bind(v)
                for half in range(2):
                    for t in range(LT):
                        for n in range(2):
                            idx = 2 * LT * half + t * 2 + n
                            ctx.wv(v, "Aom", idx + 1)
                            co = half * 1024 + n * 512
                            i = v.tensor_copy(acc[:, t * D + co:t * D + co + 512],
                                              pmm[idx % 4][:])
                            ctx.inc("Aoe", i)

            if DEBUG_TAPS:
                @block.sync
                def _(sync):
                    ctx.wv(sync, "Aoe", 4 * LT)
                    ctx.inc("Adbg", sync.dma_start(out=dram["dbg_aacc"],
                                                   in_=acc[:]), 16)
            emit_epilogue(ctx, block, "AE", acc, dram["asrc"], dram["adst"],
                          io2, rdy)

            @block.sync
            def _(sync):
                ctx.wv(sync, "AEyo", ctx.val("AEyo"))

        return phase_close(ctx, block)


PHASE_INPUTS = {
    "A": ["l0_wqT", "l0_wkT", "l0_wvT", "l0_woT", "amask"],
    "B": ["l0_wbT", "l0_bb", "l0_wrT", "l0_af", "l0_bf"],
    "C": ["l1_wq", "l1_wk", "l1_wv", "l1_wo", "dmask"],
    "D": ["wfc_slice", "wp_slice"],
}
INPUT_SHAPES = {
    "x_loc": ([TOK, D], F32),
    "rk": ([1, 1], mybir.dt.int32),
    "ident": ([128, 128], BF16),
    "ones_row": ([1, 128], BF16),
    "ones_col": ([128, 1], BF16),
    "dmask": ([128, 128], F32),
    "amask": ([64 * 128, 128], BF16),
    "l0_wqT": ([D, D], F32), "l0_wkT": ([D, D], F32),
    "l0_wvT": ([D, D], F32), "l0_woT": ([D, D], F32),
    "l0_wbT": ([D, D], F32), "l0_bb": ([1, D], F32),
    "l0_wrT": ([D, 8], F32), "l0_af": ([D, 64], F32), "l0_bf": ([64, D], F32),
    "l1_wq": ([D, D], F32), "l1_wk": ([D, D], F32),
    "l1_wv": ([D, D], F32), "l1_wo": ([D, D], F32),
    "wfc_slice": ([256, 4 * D], F32), "wp_slice": ([1024, D], F32),
}

_PROGRAM_CACHE = {}


def build_program(group):
    group = tuple(group)
    if group in _PROGRAM_CACHE:
        return _PROGRAM_CACHE[group]
    nc = bass.Bass("TRN2", target_bir_lowering=False)
    names = ["x_loc", "rk", "ident", "ones_row", "ones_col"]
    for p in group:
        names += PHASE_INPUTS[p]
    ins = {}
    for n in names:
        sh, dt = INPUT_SHAPES[n]
        ins[n] = nc.dram_tensor(n, sh, dt, kind="ExternalInput").ap()
    y_out = nc.dram_tensor("y_loc", [TOK, D], F32, kind="ExternalOutput").ap()

    if DEBUG_TAPS and "A" in group:
        ins["dbg_aoT"] = nc.dram_tensor("dbg_aoT", [128, KC * TOK], BF16,
                                        kind="ExternalOutput").ap()
        ins["dbg_aqT"] = nc.dram_tensor("dbg_aqT", [128, KC * TOK], BF16,
                                        kind="ExternalOutput").ap()
        ins["dbg_agk"] = nc.dram_tensor("dbg_agk", [2048, TOK], BF16,
                                        kind="ExternalOutput").ap()
        ins["dbg_akb"] = nc.dram_tensor("dbg_akb", [128, 8192], BF16,
                                        kind="ExternalOutput").ap()
        ins["dbg_avb"] = nc.dram_tensor("dbg_avb", [128, 8192], BF16,
                                        kind="ExternalOutput").ap()
        ins["dbg_aacc"] = nc.dram_tensor("dbg_aacc", [128, LT * D], F32,
                                         kind="ExternalOutput").ap()
        ins["dbg_abias"] = nc.dram_tensor("dbg_abias", [128, 17], F32,
                                          kind="ExternalOutput").ap()
        ins["dbg_ak2"] = nc.dram_tensor("dbg_ak2", [2048, TOK], BF16,
                                        kind="ExternalOutput").ap()
    if DEBUG_TAPS and "D" in group:
        ins["dbg_dfeat"] = nc.dram_tensor("dbg_dfeat", [128, KC * TOK], BF16,
                                          kind="ExternalOutput").ap()
        ins["dbg_dh"] = nc.dram_tensor("dbg_dh", [TOK, 4 * D], BF16,
                                       kind="ExternalOutput").ap()
        ins["dbg_dwt"] = nc.dram_tensor("dbg_dwt", [128, 4 * D], BF16,
                                        kind="ExternalOutput").ap()
        ins["dbg_dwp"] = nc.dram_tensor("dbg_dwp", [128, D], BF16,
                                        kind="ExternalOutput").ap()
        ins["dbg_dqs"] = nc.dram_tensor("dbg_dqs", [128, 64], F32,
                                        kind="ExternalOutput").ap()
        ins["dbg_dlnb"] = nc.dram_tensor("dbg_dlnb", [128, LT * D], BF16,
                                         kind="ExternalOutput").ap()
        ins["dbg_dst"] = nc.dram_tensor("dbg_dst", [128, LT * 16], F32,
                                        kind="ExternalOutput").ap()
        ins["dbg_dacc"] = nc.dram_tensor("dbg_dacc", [128, LT * D], F32,
                                         kind="ExternalOutput").ap()
    if DEBUG_TAPS and "B" in group:
        dram_dbg = {
            "dbg_featT": nc.dram_tensor("dbg_featT", [128, KC * TOK], BF16,
                                        kind="ExternalOutput").ap(),
            "dbg_hst": nc.dram_tensor("dbg_hst", [64, LT * 128], BF16,
                                      kind="ExternalOutput").ap(),
            "dbg_acc": nc.dram_tensor("dbg_acc", [128, LT * D], F32,
                                      kind="ExternalOutput").ap(),
            "dbg_gs": nc.dram_tensor("dbg_gs", [128, 64], F32,
                                     kind="ExternalOutput").ap(),
            "dbg_lg": nc.dram_tensor("dbg_lg", [128, 64], F32,
                                     kind="ExternalOutput").ap(),
            "dbg_ct": nc.dram_tensor("dbg_ct", [128, 64], F32,
                                     kind="ExternalOutput").ap(),
            "dbg_hsb": nc.dram_tensor("dbg_hsb", [128, 512], F32,
                                      kind="ExternalOutput").ap(),
        }
        ins.update(dram_dbg)
    dram = dict(ins)
    chain = [ins["x_loc"]]
    for i in range(len(group) - 1):
        chain.append(nc.dram_tensor("s%d" % i, [TOK, D], F32).ap())
    chain.append(y_out)
    for i, p in enumerate(group):
        dram[p.lower() + "src"] = chain[i]
        dram[p.lower() + "dst"] = chain[i + 1]

    if "A" in group:
        dram["stage_k"] = nc.dram_tensor("stage_k", [NC * D, TOK], BF16).ap()
        dram["stage_k2"] = nc.dram_tensor("stage_k2", [D, TOK], BF16).ap()
        dram["gath_k"] = nc.dram_tensor("gath_k", [NC * D, TOK], BF16,
                                         addr_space="Shared").ap()
        dram["stage_v"] = nc.dram_tensor("stage_v", [NC * NH * TOK, 128],
                                         BF16).ap()
        dram["stage_v2"] = nc.dram_tensor("stage_v2", [NH * TOK, 128], BF16).ap()
        dram["gath_v"] = nc.dram_tensor("gath_v", [NC * NH * TOK, 128], BF16,
                                        addr_space="Shared").ap()
        dram["stage_mk"] = nc.dram_tensor("stage_mk", [1, 1], F32).ap()
        dram["gath_mk"] = nc.dram_tensor("gath_mk", [1, 1], F32,
                                         addr_space="Shared").ap()
        dram["fence_a"] = nc.dram_tensor("fence_a", [1, 1], F32,
                                         addr_space="Shared").ap()
    if "D" in group:
        dram["psc"] = nc.dram_tensor("psc", [128, 2], F32).ap()
        dram["ar_in"] = nc.dram_tensor("ar_in", [2, 1], F32).ap()
        dram["ar_out"] = nc.dram_tensor("ar_out", [2, 1], F32,
                                        addr_space="Shared").ap()
        dram["stage_wfc"] = nc.dram_tensor("stage_wfc", [D, 4 * D], BF16).ap()
        dram["gath_wfc"] = nc.dram_tensor("gath_wfc", [D, 4 * D], BF16,
                                          addr_space="Shared").ap()
        dram["stage_wp"] = nc.dram_tensor("stage_wp", [4 * D, D], BF16).ap()
        dram["gath_wp"] = nc.dram_tensor("gath_wp", [4 * D, D], BF16,
                                          addr_space="Shared").ap()
        dram["fence_d"] = nc.dram_tensor("fence_d", [2, 1], F32,
                                         addr_space="Shared").ap()
        dram["h_spill"] = nc.dram_tensor("h_spill", [TOK, 4 * D], BF16).ap()

    with ExitStack() as stack:
        ctx = Ctx(nc, stack)
        ctx.vg = VGuard(ctx)
        ctx.ident = stack.enter_context(nc.sbuf_tensor("g_ident", [128, 128], BF16))
        ctx.ones_row = stack.enter_context(nc.sbuf_tensor("g_onr", [1, 128], BF16))
        ctx.ones_col = stack.enter_context(nc.sbuf_tensor("g_onc", [128, 1], BF16))
        ctx.junk = [stack.enter_context(nc.sbuf_tensor("g_jk%d" % i, [1, 4], F32))
                    for i in range(3)]
        ctx.rks = stack.enter_context(nc.sbuf_tensor("g_rk", [1, 1],
                                                     mybir.dt.int32))
        if "C" in group:
            ctx.dmask = stack.enter_context(
                nc.sbuf_tensor("g_dmask", [128, 128], F32))
        block = stack.enter_context(nc.Block())

        @block.gpsimd
        def _(g):
            ctx.inc("ini", g.dma_start(out=ctx.rks[:], in_=ins["rk"]), 16)
            ctx.inc("ini", g.dma_start(out=ctx.ident[:], in_=ins["ident"]), 16)
            ctx.inc("ini", g.dma_start(out=ctx.ones_row[:], in_=ins["ones_row"]), 16)
            ctx.inc("ini", g.dma_start(out=ctx.ones_col[:], in_=ins["ones_col"]), 16)
            if "C" in group:
                ctx.inc("ini", g.dma_start(out=ctx.dmask[:], in_=ins["dmask"]), 16)

        need = ctx.val("ini")

        @block.vector
        def _(v):
            v = ctx.vg.bind(v)
            ctx.wv(v, "ini", need)
            ctx.inc("ph", v.memset(ctx.junk[0][:], 0.0))

        @block.scalar
        def _(s):
            ctx.wv(s, "ini", need)
            ctx.inc("ph", s.copy(ctx.junk[1][:], ctx.junk[0][:]))

        @block.gpsimd
        def _(g):
            ctx.wv(g, "ini", need)
            ctx.inc("ph", g.memset(ctx.junk[2][:], 0.0))

        bar = ctx.val("ph")
        emitters = {"A": emit_phase_A, "B": emit_phase_B,
                    "C": emit_phase_C, "D": emit_phase_D}
        for i, p in enumerate(group):
            if i >= 1:
                ctx.retire(group[i - 1])
            bar = emitters[p](ctx, block, dram, bar)

    _PROGRAM_CACHE[group] = (nc, names)
    return nc, names


def host_prep(inputs):
    import ml_dtypes
    bf = ml_dtypes.bfloat16
    x = np.asarray(inputs["L0_attn_x"])[0]
    pr = {}
    pr["x_loc"] = [np.ascontiguousarray(
        np.concatenate([x[jg * 128:(jg + 1) * 128] for jg in core_tiles(c)], 0))
        for c in range(NC)]
    pr["rk"] = [np.full((1, 1), c, np.int32) for c in range(NC)]
    pr["ident"] = np.eye(128, dtype=bf)
    pr["ones_row"] = np.ones((1, 128), dtype=bf)
    pr["ones_col"] = np.ones((128, 1), dtype=bf)
    r = np.arange(128)
    pr["dmask"] = ((r[:, None] % 8) == (r[None, :] % 8)).astype(np.float32)
    band = np.where(r[:, None] > r[None, :], np.float32(NEG), np.float32(0.0))
    zeros = np.zeros((128, 128), np.float32)
    dead = np.full((128, 128), np.float32(NEG))
    ams = []
    for c in range(NC):
        tiles = []
        for lt in range(LT):
            ib = c if lt % 2 == 0 else 7 - c
            for i in range(8):
                tiles.append(zeros if i < ib else (band if i == ib else dead))
        ams.append(np.concatenate(tiles, 0).astype(bf))
    pr["amask"] = ams
    f32 = np.float32
    pr["l0_wqT"] = np.ascontiguousarray(np.asarray(inputs["L0_attn_wq"]).T)
    pr["l0_wkT"] = np.ascontiguousarray(np.asarray(inputs["L0_attn_wk"]).T)
    pr["l0_wvT"] = np.ascontiguousarray(np.asarray(inputs["L0_attn_wv"]).T)
    pr["l0_woT"] = np.ascontiguousarray(np.asarray(inputs["L0_attn_wo"]).T)
    pr["l0_wbT"] = np.ascontiguousarray(np.asarray(inputs["L0_ffn_W_base"]).T)
    pr["l0_bb"] = np.ascontiguousarray(np.asarray(inputs["L0_ffn_b_base"])
                                       .reshape(1, D))
    pr["l0_wrT"] = np.ascontiguousarray(np.asarray(inputs["L0_ffn_W_router"]).T)
    pr["l0_af"] = np.ascontiguousarray(
        np.asarray(inputs["L0_ffn_A_w"]).transpose(2, 0, 1).reshape(D, 64))
    pr["l0_bf"] = np.ascontiguousarray(
        np.asarray(inputs["L0_ffn_B_w"]).transpose(0, 2, 1).reshape(64, D))
    pr["l1_wq"] = np.ascontiguousarray(np.asarray(inputs["L1_attn_Wq"]))
    pr["l1_wk"] = np.ascontiguousarray(np.asarray(inputs["L1_attn_Wk"]))
    pr["l1_wv"] = np.ascontiguousarray(np.asarray(inputs["L1_attn_Wv"]))
    pr["l1_wo"] = np.ascontiguousarray(np.asarray(inputs["L1_attn_Wo"]))
    wfcT = np.ascontiguousarray(np.asarray(inputs["L1_ffn_w_fc"]).T)
    wpT = np.ascontiguousarray(np.asarray(inputs["L1_ffn_w_proj"]).T)
    pr["wfc_slice"] = [np.ascontiguousarray(wfcT[c * 256:(c + 1) * 256])
                       for c in range(NC)]
    pr["wp_slice"] = [np.ascontiguousarray(wpT[c * 1024:(c + 1) * 1024])
                      for c in range(NC)]
    return pr


def kernel(**inputs):
    pr = host_prep(inputs)
    xcur = pr["x_loc"]
    for group in LAUNCH_GROUPS:
        nc, names = build_program(tuple(group))
        in_maps = []
        for c in range(NC):
            m = {}
            for n in names:
                if n == "x_loc":
                    m[n] = xcur[c]
                elif isinstance(pr[n], list):
                    m[n] = pr[n][c]
                else:
                    m[n] = pr[n]
            in_maps.append(m)
        res = run_bass_kernel_spmd(nc, in_maps, core_ids=list(range(NC)))
        xcur = [np.asarray(res.results[c]["y_loc"]) for c in range(NC)]
    out = np.empty((1, S, D), np.float32)
    for c in range(NC):
        for l, jg in enumerate(core_tiles(c)):
            out[0, jg * 128:(jg + 1) * 128] = xcur[c][l * 128:(l + 1) * 128]
    return out


# version 05 (final)
# speedup vs baseline: 1.0092x; 1.0092x over previous
import numpy as np
from contextlib import ExitStack

import concourse.bass as bass
import concourse.mybir as mybir
from concourse.bass_utils import run_bass_kernel_spmd

F32 = mybir.dt.float32
BF16 = mybir.dt.bfloat16
AF = mybir.ActivationFunctionType
ALU = mybir.AluOpType
AX = mybir.AxisListType

NC = 8
D = 2048
S = 8192
NT = 64
LT = 8
TOK = LT * 128
NH = 16
HD = 128
KC = 16
SCALE = HD ** (-0.5)
EPS_LN = 1e-5
EPS_BL = 1e-6
QB = 128.0
CLIP = 128.0 - 1e-6
NEG = -1.0e6

LAUNCH_GROUPS = [["A", "B", "C", "D"]]
DEBUG_TAPS = False
D_STAGES = 4


def core_tiles(c):
    return sorted([16 * g + c for g in range(4)] + [16 * g + 15 - c for g in range(4)])


OWNER = {}
for _c in range(NC):
    for _l, _jg in enumerate(core_tiles(_c)):
        OWNER[_jg] = (_c, _l)


def seg_list(lo, hi, width=512):
    out = []
    while lo < hi:
        nxt = min(hi, (lo // width + 1) * width)
        out.append((lo, nxt))
        lo = nxt
    return out


class Ctx:

    def __init__(self, nc, stack):
        self.nc = nc
        self.stack = stack
        self.sems = {}
        self.base = {}
        self.free_pool = []

    def sem(self, name):
        if name not in self.sems:
            try:
                h = self.stack.enter_context(self.nc.semaphore(name))
                b = 0
            except KeyError:
                h, b = self.free_pool.pop()
            self.sems[name] = [h, b]
            self.base[name] = b
        return self.sems[name]

    def retire(self, prefix):
        for n in [k for k in self.sems if k.startswith(prefix)]:
            h, v = self.sems.pop(n)
            self.free_pool.append((h, v))
            del self.base[n]

    def inc(self, name, instr, k=1):
        s = self.sem(name)
        instr.then_inc(s[0], k)
        s[1] += k
        if hasattr(self, "vg"):
            self.vg.note_inc(instr, name, s[1])
        return s[1]

    def val(self, name):
        s = self.sem(name)
        return s[1] - self.base[name]

    def wv(self, eng, name, v):
        s = self.sem(name)
        eng.wait_ge(s[0], self.base[name] + v)




def _ap_iv(a):
    try:
        name = a.tensor.name
        lo = int(a.offset)
        span = 1
        for st_, ct in [(int(p[0]), int(p[1])) for p in a.ap]:
            span += (ct - 1) * abs(st_)
        return (name, lo, lo + span)
    except Exception:
        return (None, 0, 1 << 62)


class VGuard:

    def __init__(self, ctx):
        self.ctx = ctx
        self.hist = []
        self.n = 0

    def bind(self, eng):
        self.e = eng
        return self

    def note_inc(self, instr, name, val):
        for h in self.hist:
            if instr is h[0]:
                h[2] = (name, val)

    def _emit(self, fn, out, ins, args, kwargs):
        ivs = [_ap_iv(a) for a in ins]
        hz = None
        for h in reversed(self.hist):
            piv = h[1]
            for iv in ivs:
                if iv[0] is not None and iv[0] == piv[0] and \
                        iv[1] < piv[2] and iv[2] > piv[1]:
                    hz = h
                    break
            if hz is not None:
                break
        if hz is not None:
            if hz[2] is not None:
                s = self.ctx.sem(hz[2][0])
                self.e.wait_ge(s[0], hz[2][1])
            else:
                s = self.ctx.sem("vbar")
                hz[0].then_inc(s[0], 1)
                s[1] += 1
                self.e.wait_ge(s[0], s[1])
            self.hist = []
        i = fn(*args, **kwargs)
        self.hist.append([i, _ap_iv(out), None])
        if len(self.hist) > 2:
            self.hist.pop(0)
        return i

    def wait_ge(self, sem, val):
        return self.e.wait_ge(sem, val)

    def tensor_copy(self, out, in_):
        return self._emit(self.e.tensor_copy, out, [in_], (out, in_), {})

    def tensor_tensor(self, out, in0, in1, op):
        return self._emit(self.e.tensor_tensor, out, [in0, in1],
                          (out, in0, in1, op), {})

    def tensor_scalar(self, out, in0, s1, s2, op0, op1=None, **kw):
        ins = [in0] + [s for s in (s1, s2) if isinstance(s, bass.AP)]
        if op1 is None:
            return self._emit(self.e.tensor_scalar, out, ins,
                              (out, in0, s1, s2, op0), kw)
        return self._emit(self.e.tensor_scalar, out, ins,
                          (out, in0, s1, s2, op0, op1), kw)

    def tensor_reduce(self, out, in_, ax, op, **kw):
        return self._emit(self.e.tensor_reduce, out, [in_],
                          (out, in_, ax, op), kw)

    def reciprocal(self, out, in_):
        return self._emit(self.e.reciprocal, out, [in_], (out, in_), {})

    def memset(self, ap, c):
        return self._emit(self.e.memset, ap, [], (ap, c), {})

    def nop(self):
        return self.e.nop()


def fmv(buf, k, lo, n):
    return buf[:, k * TOK + lo: k * TOK + lo + n]


def emit_preln(ctx, block, tag, src_dram, featT, io, eps=EPS_LN, quant=False,
               gate=None):
    xt = io["xt"]
    xm = io["xm"]
    lnb = io["lnb"]
    sq = io["sq"]
    st = io["st"]
    ptr = io["ptr"]
    ident = io["ident"]

    @block.sync
    def _(sync):
        if gate is not None:
            ctx.wv(sync, gate[0], gate[1])
        for t in range(LT):
            if t >= 2:
                ctx.wv(sync, tag + "xfree", t - 1)
            ctx.inc(tag + "xl", sync.dma_start(
                out=xt[t % 2][:], in_=src_dram[t * 128:(t + 1) * 128, :]), 16)

    @block.vector
    def _(v):
        v = ctx.vg.bind(v)
        for t in range(LT):
            ctx.wv(v, tag + "xl", 16 * (t + 1))
            if t >= 1:
                ctx.wv(v, tag + "lnr", t)
            v.tensor_reduce(st[:, 0:1], xt[t % 2][:], AX.X, ALU.add, negate=True)
            v.tensor_scalar(st[:, 1:2], st[:, 0:1], 1.0 / D, None, ALU.mult)
            i = v.tensor_scalar(xm[t % 2][:], xt[t % 2][:], st[:, 1:2], None, ALU.add)
            ctx.inc(tag + "xmr", i)
            if quant:
                v.tensor_reduce(st[:, 2:3], xm[t % 2][:], AX.X, ALU.max,
                                apply_absolute_value=True)
            ctx.inc(tag + "xfree", v.tensor_copy(st[:, 15:16], st[:, 1:2]))
            ctx.wv(v, tag + "sqd", t + 1)
            ctx.inc(tag + "vpe", v.tensor_scalar(st[:, 4:5], st[:, 5:6], 1.0 / D, eps,
                                                 ALU.mult, ALU.add))
            ctx.wv(v, tag + "sdd", t + 1)
            i = v.reciprocal(st[:, 7:8], st[:, 6:7])
            if quant:
                v.reciprocal(st[:, 9:10], st[:, 2:3])
                v.tensor_scalar(st[:, 10:11], st[:, 9:10], QB, None, ALU.mult)
                v.tensor_tensor(st[:, 11:12], st[:, 7:8], st[:, 2:3], ALU.mult)
                i = v.tensor_scalar(st[:, 12:13], st[:, 11:12], 1.0 / QB, None,
                                    ALU.mult)
            ctx.inc(tag + "rstd", i)
            if quant:
                ctx.wv(v, tag + "q1", t + 1)
                ctx.inc(tag + "q2", v.tensor_scalar(sq[:], sq[:], CLIP, -CLIP,
                                                    ALU.min, ALU.max))
            if DEBUG_TAPS and tag == "DL":
                ctx.wv(v, tag + "lnr", t + 1)
                v.tensor_copy(io["dbg_lnb"][:, t * D:(t + 1) * D], lnb[t % 2][:])
                v.tensor_copy(io["dbg_st"][:, t * 16:(t + 1) * 16], st[:])
            for k in range(KC):
                ctx.wv(v, tag + "tr", t * KC + k + 1)
                ctx.inc(tag + "tre", v.tensor_copy(fmv(featT, k, t * 128, 128),
                                                   ptr[k % 2][:]))

    @block.scalar
    def _(s):
        for t in range(LT):
            ctx.wv(s, tag + "xmr", t + 1)
            ctx.inc(tag + "sqd", s.activation(sq[:], xm[t % 2][:], AF.Square,
                                              accum_out=st[:, 5:6]))
            ctx.wv(s, tag + "vpe", t + 1)
            ctx.inc(tag + "sdd", s.activation(st[:, 6:7], st[:, 4:5], AF.Sqrt))
            ctx.wv(s, tag + "rstd", t + 1)
            if t >= 2:
                ctx.wv(s, tag + "tr", (t - 1) * KC)
            if quant:
                ctx.inc(tag + "q1", s.activation(sq[:], xm[t % 2][:], AF.Identity,
                                                 scale=st[:, 10:11]))
                ctx.wv(s, tag + "q2", t + 1)
                ctx.inc(tag + "lnr", s.activation(lnb[t % 2][:], sq[:], AF.Identity,
                                                  scale=st[:, 12:13]))
            else:
                ctx.inc(tag + "lnr", s.activation(lnb[t % 2][:], xm[t % 2][:],
                                                  AF.Identity, scale=st[:, 7:8]))

    @block.tensor
    def _(te):
        for t in range(LT):
            ctx.wv(te, tag + "lnr", t + 1)
            for k in range(KC):
                idx = t * KC + k
                if idx >= 2:
                    ctx.wv(te, tag + "tre", idx - 1)
                ctx.inc(tag + "tr", te.transpose(ptr[k % 2][:],
                                                 lnb[t % 2][:, k * 128:(k + 1) * 128],
                                                 ident[:]))


def emit_epilogue(ctx, block, tag, acc, src_dram, dst_dram, io, ready):
    xt = io["xt"][0]
    xm = io["xm"][0]
    sq = io["sq"]
    st = io["st"]
    lnf = io["lnf"][0]

    @block.sync
    def _(sync):
        ctx.wv(sync, ready[0][0], ready[0][1])
        for t in range(LT):
            if t >= 1:
                ctx.wv(sync, tag + "yd", t)
            ctx.inc(tag + "xl", sync.dma_start(
                out=xt[:], in_=src_dram[t * 128:(t + 1) * 128, :]), 16)
            ctx.wv(sync, tag + "yd", t + 1)
            ctx.inc(tag + "yo", sync.dma_start(
                out=dst_dram[t * 128:(t + 1) * 128, :], in_=lnf[:]), 16)

    @block.vector
    def _(v):
        v = ctx.vg.bind(v)
        for t in range(LT):
            a = acc[:, t * D:(t + 1) * D]
            ctx.wv(v, ready[t][0], ready[t][1])
            if t >= 1:
                ctx.wv(v, tag + "lnv", t)
            v.tensor_reduce(st[:, 0:1], a, AX.X, ALU.add, negate=True)
            v.tensor_scalar(st[:, 1:2], st[:, 0:1], 1.0 / D, None, ALU.mult)
            ctx.inc(tag + "xmr", v.tensor_scalar(xm[:], a, st[:, 1:2],
                                                 None, ALU.add))
            ctx.wv(v, tag + "sqd", t + 1)
            ctx.inc(tag + "vpe", v.tensor_scalar(st[:, 4:5], st[:, 5:6], 1.0 / D,
                                                 EPS_LN, ALU.mult, ALU.add))
            ctx.wv(v, tag + "sdd", t + 1)
            ctx.inc(tag + "rstd", v.reciprocal(st[:, 7:8], st[:, 6:7]))
            ctx.wv(v, tag + "lnv", t + 1)
            ctx.wv(v, tag + "xl", 16 * (t + 1))
            if t >= 1:
                ctx.wv(v, tag + "yo", 16 * t)
            ctx.inc(tag + "yd", v.tensor_tensor(lnf[:], sq[:], xt[:], ALU.add))

    @block.scalar
    def _(s):
        for t in range(LT):
            ctx.wv(s, tag + "xmr", t + 1)
            if t >= 1:
                ctx.wv(s, tag + "yd", t)
            ctx.inc(tag + "sqd", s.activation(sq[:], xm[:], AF.Square,
                                              accum_out=st[:, 5:6]))
            ctx.wv(s, tag + "vpe", t + 1)
            ctx.inc(tag + "sdd", s.activation(st[:, 6:7], st[:, 4:5], AF.Sqrt))
            ctx.wv(s, tag + "rstd", t + 1)
            ctx.inc(tag + "lnv", s.activation(sq[:], xm[:], AF.Identity,
                                              scale=st[:, 7:8]))


def phase_gate(ctx, block, bar):
    for e in ["sync", "gpsimd", "vector", "scalar", "tensor"]:
        @getattr(block, e)
        def _(q):
            ctx.wv(q, "ph", bar)


def phase_close(ctx, block):
    jk = ctx.junk

    @block.vector
    def _(v):
        v = ctx.vg.bind(v)
        ctx.inc("ph", v.memset(jk[0][:], 0.0))

    @block.scalar
    def _(s):
        ctx.inc("ph", s.copy(jk[1][:], jk[0][:]))

    @block.gpsimd
    def _(g):
        ctx.inc("ph", g.memset(jk[2][:], 0.0))

    @block.sync
    def _(sync):
        ctx.inc("ph", sync.dma_start(out=jk[0][:], in_=jk[1][:]), 16)
    return ctx.val("ph")


def alloc_io(ctx, ps, pfx, slim=False):
    nc = ctx.nc
    nb = 1 if slim else 2
    io = {
        "xt": [ps.enter_context(nc.sbuf_tensor(pfx + "xt%d" % i, [128, D], F32))
               for i in range(nb)],
        "xm": [ps.enter_context(nc.sbuf_tensor(pfx + "xm%d" % i, [128, D], F32))
               for i in range(nb)],
        "sq": ps.enter_context(nc.sbuf_tensor(pfx + "sq", [128, D], F32)),
        "st": ps.enter_context(nc.sbuf_tensor(pfx + "st", [128, 16], F32)),
        "ident": ctx.ident,
    }
    if slim:
        io["lnf"] = [ps.enter_context(nc.sbuf_tensor(pfx + "lnf", [128, D], F32))]
    else:
        io["lnb"] = [ps.enter_context(nc.sbuf_tensor(pfx + "lnb%d" % i,
                                                     [128, D], BF16))
                     for i in range(2)]
        io["ptr"] = [ps.enter_context(nc.psum_tensor(pfx + "ptr%d" % i,
                                                     [128, 128], BF16))
                     for i in range(2)]
    return io


def emit_phase_B(ctx, block, dram, bar):
    nc = ctx.nc
    with ExitStack() as ps:
        featT = ps.enter_context(nc.sbuf_tensor("b_featT", [128, KC * TOK], BF16))
        wb = ps.enter_context(nc.sbuf_tensor("b_wb", [128, KC * 1024], BF16))
        acc = ps.enter_context(nc.sbuf_tensor("b_acc", [128, LT * D], F32))
        wr = ps.enter_context(nc.sbuf_tensor("b_wr", [128, KC * 8], BF16))
        af = ps.enter_context(nc.sbuf_tensor("b_af", [128, KC * 64], BF16))
        bfl = ps.enter_context(nc.sbuf_tensor("b_bfl", [64, D], BF16))
        bb = ps.enter_context(nc.sbuf_tensor("b_bb", [1, D], BF16))
        hst = ps.enter_context(nc.sbuf_tensor("b_hst", [64, LT * 128], BF16))
        lg = ps.enter_context(nc.sbuf_tensor("b_lg", [128, 8], F32))
        gsc = ps.enter_context(nc.sbuf_tensor("b_gsc", [128, 32], F32))
        hsb = ps.enter_context(nc.sbuf_tensor("b_hsb", [128, 64], BF16))
        if DEBUG_TAPS:
            dbg_gs = ps.enter_context(nc.sbuf_tensor("b_dgs", [128, 64], F32))
            dbg_lg = ps.enter_context(nc.sbuf_tensor("b_dlg", [128, 64], F32))
            dbg_ct = ps.enter_context(nc.sbuf_tensor("b_dct", [128, 64], F32))
            dbg_hsb = ps.enter_context(nc.sbuf_tensor("b_dhsb", [128, 512], F32))
        pmm = [ps.enter_context(nc.psum_tensor("b_pm%d" % i, [128, 512], F32))
               for i in range(4)]
        plog = ps.enter_context(nc.psum_tensor("b_plog", [128, 128], F32))
        phs = ps.enter_context(nc.psum_tensor("b_phs", [64, 128], BF16))
        phase_gate(ctx, block, bar)

        @block.gpsimd
        def _(g):
            ctx.inc("Bsw", g.dma_start(
                out=wr[:].rearrange("p (k e) -> p k e", k=KC),
                in_=dram["l0_wrT"].rearrange("(k p) e -> p k e", p=128)), 16)
            ctx.inc("Bsw", g.dma_start(
                out=af[:].rearrange("p (k e) -> p k e", k=KC),
                in_=dram["l0_af"].rearrange("(k p) e -> p k e", p=128)), 16)
            ctx.inc("Bsw", g.dma_start(out=bfl[:], in_=dram["l0_bf"]), 16)
            ctx.inc("Bsw", g.dma_start(out=bb[:], in_=dram["l0_bb"]), 16)

        with ExitStack() as pl:
            io = alloc_io(ctx, pl, "b_")
            emit_preln(ctx, block, "BL", dram["bsrc"], featT, io)

        @block.tensor
        def _(te):
            ctx.wv(te, "Bsw", 64)
            for t in range(LT):
                ctx.wv(te, "BLtre", (t + 1) * KC)
                if t >= 1:
                    ctx.wv(te, "Blge", t)
                for k in range(KC):
                    te.matmul(plog[:, 0:8], fmv(featT, k, t * 128, 128),
                              wr[:, k * 8:(k + 1) * 8],
                              start=(k == 0), stop=(k == KC - 1))
                for k in range(KC):
                    i = te.matmul(plog[:, 8:72], fmv(featT, k, t * 128, 128),
                                  af[:, k * 64:(k + 1) * 64],
                                  start=(k == 0), stop=(k == KC - 1))
                ctx.inc("Blog", i)
                ctx.wv(te, "Bhs", t + 1)
                ctx.inc("BhsT", te.transpose(phs[:], hsb[:], ctx.ident[:]))

        @block.vector
        def _(v):
            v = ctx.vg.bind(v)
            for t in range(LT):
                ctx.wv(v, "Blog", t + 1)
                v.tensor_copy(lg[:], plog[:, 0:8])
                v.tensor_reduce(gsc[:, 0:1], lg[:], AX.X, ALU.max, negate=True)
                for e in range(8):
                    v.tensor_scalar(gsc[:, 8:16], lg[:], lg[:, e:e + 1], None, ALU.is_gt)
                    v.tensor_reduce(gsc[:, 16 + e:17 + e], gsc[:, 8:16], AX.X, ALU.add)
                i = v.tensor_scalar(gsc[:, 24:32], gsc[:, 16:24], 1.5, None, ALU.is_lt)
                ctx.inc("Bgm", i)
                ctx.wv(v, "Bge", t + 1)
                v.tensor_tensor(gsc[:, 8:16], gsc[:, 8:16], gsc[:, 24:32], ALU.mult)
                v.tensor_reduce(gsc[:, 2:3], gsc[:, 8:16], AX.X, ALU.add)
                v.reciprocal(gsc[:, 3:4], gsc[:, 2:3])
                v.tensor_scalar(gsc[:, 8:16], gsc[:, 8:16], gsc[:, 3:4], None, ALU.mult)
                if t >= 1:
                    ctx.wv(v, "BhsT", t)
                if DEBUG_TAPS:
                    v.tensor_copy(dbg_gs[:, t * 8:(t + 1) * 8], gsc[:, 8:16])
                    v.tensor_copy(dbg_lg[:, t * 8:(t + 1) * 8], lg[:])
                    v.tensor_copy(dbg_ct[:, t * 8:(t + 1) * 8], gsc[:, 16:24])
                i = None
                for e in range(8):
                    i = v.tensor_scalar(hsb[:, e * 8:(e + 1) * 8],
                                        plog[:, 8 + e * 8:16 + e * 8],
                                        gsc[:, 8 + e:9 + e], 2.0, ALU.mult, ALU.mult)
                ctx.inc("Blge", i)
                if DEBUG_TAPS:
                    v.tensor_copy(dbg_hsb[:, t * 64:(t + 1) * 64], hsb[:])
                ctx.inc("Bhs", v.tensor_copy(gsc[:, 4:5], gsc[:, 3:4]))
                ctx.wv(v, "BhsT", t + 1)
                ctx.inc("BhsE", v.tensor_copy(hst[:, t * 128:(t + 1) * 128], phs[:]))

        @block.scalar
        def _(s):
            for t in range(LT):
                ctx.wv(s, "Bgm", t + 1)
                ctx.inc("Bge", s.activation(gsc[:, 8:16], lg[:], AF.Exp,
                                            bias=gsc[:, 0:1]))

        @block.gpsimd
        def _(sync):
            for k in range(KC):
                ctx.inc("Bwl", sync.dma_start(
                    out=wb[:, k * 1024:(k + 1) * 1024],
                    in_=dram["l0_wbT"][k * 128:(k + 1) * 128, 0:1024]), 16)

        @block.tensor
        def _(te):
            for half in range(2):
                ctx.wv(te, "Bwl", 16 * KC * (half + 1))
                for t in range(LT):
                    ctx.wv(te, "BhsE", LT)
                    for n in range(2):
                        p = pmm[(t * 2 + n) % 4]
                        co = half * 1024 + n * 512
                        idx = 2 * LT * half + t * 2 + n
                        if idx >= 4:
                            ctx.wv(te, "Bacc", idx - 3)
                        for k in range(KC):
                            te.matmul(p[:], fmv(featT, k, t * 128, 128),
                                      wb[:, k * 1024 + n * 512:k * 1024 + n * 512 + 512],
                                      start=(k == 0), stop=False)
                        te.matmul(p[:], ctx.ones_row[:, 0:128], bb[:, co:co + 512],
                                  start=False, stop=False)
                        i = te.matmul(p[:], hst[:, t * 128:(t + 1) * 128],
                                      bfl[:, co:co + 512], start=False, stop=True)
                        ctx.inc("Bmm", i)

        @block.gpsimd
        def _(sync):
            ctx.wv(sync, "Bmm", 2 * LT)
            for k in range(KC):
                ctx.inc("Bwl", sync.dma_start(
                    out=wb[:, k * 1024:(k + 1) * 1024],
                    in_=dram["l0_wbT"][k * 128:(k + 1) * 128, 1024:2048]), 16)

        rdy = [("Bacc", 2 * LT + 2 * t + 2) for t in range(LT)]

        @block.vector
        def _(v):
            v = ctx.vg.bind(v)
            for half in range(2):
                for t in range(LT):
                    for n in range(2):
                        idx = 2 * LT * half + t * 2 + n
                        ctx.wv(v, "Bmm", idx + 1)
                        co = half * 1024 + n * 512
                        i = v.tensor_copy(acc[:, t * D + co:t * D + co + 512],
                                          pmm[(t * 2 + n) % 4][:])
                        ctx.inc("Bacc", i)

        io_ep = alloc_io(ctx, ps, "be_", slim=True)
        if DEBUG_TAPS:
            @block.sync
            def _(sync):
                ctx.wv(sync, "BLtre", LT * KC)
                ctx.inc("Bdbg", sync.dma_start(out=dram["dbg_featT"],
                                               in_=featT[:]), 16)
                ctx.wv(sync, "BhsE", LT)
                ctx.inc("Bdbg", sync.dma_start(out=dram["dbg_hst"],
                                               in_=hst[:]), 16)
                ctx.wv(sync, "Bacc", 4 * LT)
                ctx.inc("Bdbg", sync.dma_start(out=dram["dbg_acc"],
                                               in_=acc[:]), 16)
                ctx.inc("Bdbg", sync.dma_start(out=dram["dbg_gs"], in_=dbg_gs[:]), 16)
                ctx.inc("Bdbg", sync.dma_start(out=dram["dbg_lg"], in_=dbg_lg[:]), 16)
                ctx.inc("Bdbg", sync.dma_start(out=dram["dbg_ct"], in_=dbg_ct[:]), 16)
                ctx.inc("Bdbg", sync.dma_start(out=dram["dbg_hsb"],
                                               in_=dbg_hsb[:]), 16)
                ctx.wv(sync, "Bdbg", 112)

        emit_epilogue(ctx, block, "BE", acc, dram["bsrc"], dram["bdst"], io_ep, rdy)

        @block.sync
        def _(sync):
            ctx.wv(sync, "BEyo", ctx.val("BEyo"))

        return phase_close(ctx, block)


def emit_phase_C(ctx, block, dram, bar):
    nc = ctx.nc
    with ExitStack() as ps:
        wb = ps.enter_context(nc.sbuf_tensor("c_wb", [128, KC * 1024], BF16))
        oT = ps.enter_context(nc.sbuf_tensor("c_oT", [128, KC * TOK], BF16))
        phase_gate(ctx, block, bar)

        with ExitStack() as p2:
            featT = p2.enter_context(nc.sbuf_tensor("c_featT", [128, KC * TOK],
                                                    BF16))
            with ExitStack() as pl:
                io = alloc_io(ctx, pl, "c_")
                emit_preln(ctx, block, "CL", dram["csrc"], featT, io)
            qT = p2.enter_context(nc.sbuf_tensor("c_qT", [128, KC * TOK], BF16))
            kT = p2.enter_context(nc.sbuf_tensor("c_kT", [128, KC * TOK], BF16))
            vT = p2.enter_context(nc.sbuf_tensor("c_vT", [128, KC * TOK], BF16))
            ppj = p2.enter_context(ExitStack())
            pmm = [ppj.enter_context(nc.psum_tensor("c_pm%d" % i, [128, 512], F32))
                   for i in range(4)]

            @block.gpsimd
            def _(g):
                for wi, wn in enumerate(["l1_wq", "l1_wk", "l1_wv"]):
                    for half in range(2):
                        if wi * 2 + half > 0:
                            ctx.wv(g, "Cpj", (wi * 2 + half) * 8)
                        else:
                            ctx.wv(g, "CLtre", LT * KC)
                        for k in range(KC):
                            ctx.inc("Cwl", g.dma_start(
                                out=wb[:, k * 1024:(k + 1) * 1024],
                                in_=dram[wn][k * 128:(k + 1) * 128,
                                             half * 1024:(half + 1) * 1024]), 16)

            @block.tensor
            def _(te):
                ctx.wv(te, "CLtre", LT * KC)
                for wi in range(3):
                    for half in range(2):
                        ctx.wv(te, "Cwl", 16 * KC * (wi * 2 + half + 1))
                        for m8 in range(8):
                            for th in range(2):
                                idx = ((wi * 2 + half) * 8 + m8) * 2 + th
                                if idx >= 4:
                                    ctx.wv(te, "Cpe", idx - 3)
                                p = pmm[idx % 4]
                                for k in range(KC):
                                    i = te.matmul(
                                        p[:],
                                        wb[:, k * 1024 + m8 * 128:
                                           k * 1024 + m8 * 128 + 128],
                                        fmv(featT, k, th * 512, 512),
                                        start=(k == 0), stop=(k == KC - 1))
                                ctx.inc("Cpj", i, 1) if th == 1 else None

            @block.vector
            def _(v):
                v = ctx.vg.bind(v)
                for wi, dst in enumerate([qT, kT, vT]):
                    for half in range(2):
                        for m8 in range(8):
                            m = half * 8 + m8
                            for th in range(2):
                                idx = ((wi * 2 + half) * 8 + m8) * 2 + th
                                ctx.wv(v, "Cpj", (idx // 2) + 1)
                                ctx.inc("Cpe", v.tensor_copy(
                                    fmv(dst, m, th * 512, 512), pmm[idx % 4][:]))

            ppj.close()
            with ExitStack() as p3:
                pk = [p3.enter_context(nc.sbuf_tensor("c_pk%d" % i, [128, 128],
                                                      BF16)) for i in range(2)]
                pq = [p3.enter_context(nc.sbuf_tensor("c_pq%d" % i, [128, 128],
                                                      BF16)) for i in range(2)]
                pv = [p3.enter_context(nc.sbuf_tensor("c_pv%d" % i, [128, 128],
                                                      BF16)) for i in range(2)]
                stm = [p3.enter_context(nc.sbuf_tensor("c_stm%d" % i, [128, 128],
                                                       BF16)) for i in range(2)]
                vbk = [p3.enter_context(nc.sbuf_tensor("c_vbk%d" % i, [128, 128],
                                                       BF16)) for i in range(2)]
                osb = [p3.enter_context(nc.sbuf_tensor("c_osb%d" % i, [128, 128],
                                                       BF16)) for i in range(2)]
                pst = [p3.enter_context(nc.psum_tensor("c_pst%d" % i, [128, 128],
                                                       F32)) for i in range(2)]
                pvt = [p3.enter_context(nc.psum_tensor("c_pvt%d" % i, [128, 128],
                                                       BF16)) for i in range(2)]
                po = [p3.enter_context(nc.psum_tensor("c_po%d" % i, [128, 128],
                                                      F32)) for i in range(2)]
                pot = [p3.enter_context(nc.psum_tensor("c_pot%d" % i, [128, 128],
                                                       BF16)) for i in range(2)]
                NG = TOK // 8

                def kview(buf, g):
                    return buf[:, :].rearrange("p (kh tok) -> p kh tok", kh=KC)[
                        :, :, g * 8:(g + 1) * 8]

                @block.gpsimd
                def _(g_):
                    ctx.wv(g_, "Cpe", 96)
                    for g in range(NG):
                        b = g % 2
                        if g >= 2:
                            ctx.wv(g_, "Cste", g - 1)
                            ctx.wv(g_, "Cvbe", g - 1)
                        g_.tensor_copy(pk[b][:].rearrange("p (a c) -> p a c", a=KC),
                                       kview(kT, g))
                        i = g_.tensor_copy(pq[b][:].rearrange("p (a c) -> p a c",
                                                              a=KC), kview(qT, g))
                        ctx.inc("Cpk", i, 1)
                        i = g_.tensor_copy(pv[b][:].rearrange("p (a c) -> p a c",
                                                              a=KC), kview(vT, g))
                        ctx.inc("Cpv", i, 1)

                @block.tensor
                def _(te):
                    for g in range(NG):
                        b = g % 2
                        ctx.wv(te, "Cpk", g + 1)
                        if g >= 2:
                            ctx.wv(te, "Cste", g - 1)
                        ctx.inc("Cst", te.matmul(pst[b][:], pk[b][:], pq[b][:],
                                                 start=True, stop=True))
                        ctx.wv(te, "Cpv", g + 1)
                        if g >= 2:
                            ctx.wv(te, "Cvbe", g - 1)
                        ctx.inc("Cvt", te.transpose(pvt[b][:], pv[b][:],
                                                    ctx.ident[:]))
                        ctx.wv(te, "Cste", g + 1)
                        ctx.wv(te, "Cvbe", g + 1)
                        if g >= 2:
                            ctx.wv(te, "Cose", g - 1)
                        ctx.inc("Cmo", te.matmul(po[b][:], stm[b][:], vbk[b][:],
                                                 start=True, stop=True))
                        ctx.wv(te, "Cose", g + 1)
                        if g >= 2:
                            ctx.wv(te, "Cote", g - 1)
                        ctx.inc("Cot", te.transpose(pot[b][:], osb[b][:],
                                                    ctx.ident[:]))

                @block.vector
                def _(v):
                    v = ctx.vg.bind(v)
                    for g in range(NG):
                        b = g % 2
                        ctx.wv(v, "Cst", g + 1)
                        if g >= 2:
                            ctx.wv(v, "Cmo", g - 1)
                        i = v.tensor_tensor(stm[b][:], pst[b][:], ctx.dmask[:],
                                            ALU.mult)
                        ctx.inc("Cste", i)
                        ctx.wv(v, "Cvt", g + 1)
                        i = v.tensor_copy(vbk[b][:], pvt[b][:])
                        ctx.inc("Cvbe", i)
                        ctx.wv(v, "Cmo", g + 1)
                        if g >= 2:
                            ctx.wv(v, "Cot", g - 1)
                        ctx.inc("Cose", v.tensor_copy(osb[b][:], po[b][:]))
                        ctx.wv(v, "Cot", g + 1)
                        ctx.inc("Cote", v.tensor_copy(
                            oT[:, :].rearrange("p (h tok) -> p h tok", h=KC)[
                                :, :, g * 8:(g + 1) * 8],
                            pot[b][:].rearrange("p (h tok) -> p h tok", h=KC)))

        with ExitStack() as p4:
            acc = p4.enter_context(nc.sbuf_tensor("c_acc", [128, LT * D], F32))
            io2 = alloc_io(ctx, p4, "c2_", slim=True)
            pmm = [p4.enter_context(nc.psum_tensor("c2_pm%d" % i, [128, 512], F32))
                   for i in range(4)]

            @block.gpsimd
            def _(g):
                for half in range(2):
                    if half == 0:
                        ctx.wv(g, "Cpj", 48)
                    else:
                        ctx.wv(g, "Com", 2 * LT)
                    for k in range(KC):
                        ctx.inc("Cwo", g.dma_start(
                            out=wb[:, k * 1024:(k + 1) * 1024],
                            in_=dram["l1_wo"][k * 128:(k + 1) * 128,
                                              half * 1024:(half + 1) * 1024]), 16)

            @block.tensor
            def _(te):
                ctx.wv(te, "Cote", TOK // 8)
                for half in range(2):
                    ctx.wv(te, "Cwo", 16 * KC * (half + 1))
                    for t in range(LT):
                        for n in range(2):
                            idx = 2 * LT * half + t * 2 + n
                            if idx >= 4:
                                ctx.wv(te, "Coe", idx - 3)
                            p = pmm[idx % 4]
                            for h in range(KC):
                                i = te.matmul(
                                    p[:], fmv(oT, h, t * 128, 128),
                                    wb[:, h * 1024 + n * 512:
                                       h * 1024 + n * 512 + 512],
                                    start=(h == 0), stop=(h == KC - 1))
                            ctx.inc("Com", i)

            rdy = [("Coe", 2 * LT + 2 * t + 2) for t in range(LT)]

            @block.vector
            def _(v):
                v = ctx.vg.bind(v)
                for half in range(2):
                    for t in range(LT):
                        for n in range(2):
                            idx = 2 * LT * half + t * 2 + n
                            ctx.wv(v, "Com", idx + 1)
                            co = half * 1024 + n * 512
                            i = v.tensor_copy(acc[:, t * D + co:t * D + co + 512],
                                              pmm[idx % 4][:])
                            ctx.inc("Coe", i)

            emit_epilogue(ctx, block, "CE", acc, dram["csrc"], dram["cdst"],
                          io2, rdy)

            @block.sync
            def _(sync):
                ctx.wv(sync, "CEyo", ctx.val("CEyo"))

        return phase_close(ctx, block)


def emit_phase_D(ctx, block, dram, bar):
    nc = ctx.nc
    with ExitStack() as ps:
        phase_gate(ctx, block, bar)

        with ExitStack() as p1:
            wsl = [p1.enter_context(nc.sbuf_tensor("d_wsl%d" % i, [128, 16384], F32))
                   for i in range(2)]
            tp = p1.enter_context(nc.sbuf_tensor("d_tp", [128, 8192], BF16))
            tn = p1.enter_context(nc.sbuf_tensor("d_tn", [128, 8192], BF16))
            th = p1.enter_context(nc.sbuf_tensor("d_th", [128, 8192], BF16))
            sc = p1.enter_context(nc.sbuf_tensor("d_sc", [128, 8], F32))
            ppart = p1.enter_context(nc.sbuf_tensor("d_ppart", [2, 128], F32))
            scb = p1.enter_context(nc.sbuf_tensor("d_scb", [1, 8], BF16))
            slo = p1.enter_context(nc.sbuf_tensor("d_slo", [1, 16], F32))
            gb = p1.enter_context(nc.sbuf_tensor("d_gb", [128, 8], F32))
            pb = p1.enter_context(nc.psum_tensor("d_pb", [128, 8], F32))

            @block.gpsimd
            def _(g):
                ctx.inc("Dsl", g.dma_start(
                    out=wsl[0][:].rearrange("p (a c) -> p a c", a=2),
                    in_=dram["wfc_slice"].rearrange("(a p) c -> p a c", p=128)), 16)
                ctx.inc("Dsl", g.dma_start(
                    out=wsl[1][:].rearrange("p (a c) -> p a c", a=8),
                    in_=dram["wp_slice"].rearrange("(a p) c -> p a c", p=128)), 16)

            @block.vector
            def _(v):
                v = ctx.vg.bind(v)
                ctx.wv(v, "Dsl", 32)
                v.tensor_reduce(sc[:, 0:1], wsl[0][:], AX.X, ALU.add,
                                apply_absolute_value=True)
                i = v.tensor_reduce(sc[:, 1:2], wsl[1][:], AX.X, ALU.add,
                                    apply_absolute_value=True)
                ctx.inc("Dgs", i)

            @block.gpsimd
            def _(g):
                ctx.wv(g, "Dgs", 1)
                ctx.inc("Dgb", g.dma_start(out=dram["psc"][0:128, 0:2],
                                           in_=sc[:, 0:2]), 16)
                ctx.wv(g, "Dgb", 16)
                ctx.inc("Dgb", g.dma_start(
                    out=ppart[:, :],
                    in_=dram["psc"][0:128, 0:2].rearrange("p c -> c p"),
                    allow_slow_non_contiguous=True), 16)

            @block.vector
            def _(v):
                v = ctx.vg.bind(v)
                ctx.wv(v, "Dgb", 32)
                i = v.tensor_reduce(sc[0:2, 2:3], ppart[:, :], AX.X, ALU.add)
                ctx.inc("Dgr", i)

            @block.gpsimd
            def _(g):
                ctx.wv(g, "Dgr", 1)
                ctx.inc("Dar0", g.dma_start(out=dram["ar_in"], in_=sc[0:2, 2:3]), 16)
                ctx.wv(g, "Dar0", 16)
                ctx.inc("Dcc", g.collective_compute(
                    "AllReduce", ALU.add, replica_groups=[list(range(NC))],
                    ins=[dram["ar_in"].opt()], outs=[dram["ar_out"].opt()]))
                ctx.wv(g, "Dcc", 1)
                ctx.inc("Dar1", g.dma_start(out=sc[0:2, 3:4], in_=dram["ar_out"]), 16)

            @block.gpsimd
            def _(g):
                ctx.wv(g, "Dar1", 16)
                ctx.inc("Dar2", g.dma_start(out=dram["psc"][1:2, 0:2],
                                            in_=sc[0:2, 3:4]), 16)
                ctx.wv(g, "Dar2", 16)
                ctx.inc("Dar2", g.dma_start(out=slo[0:1, 0:2],
                                            in_=dram["psc"][1:2, 0:2]), 16)

            @block.vector
            def _(v):
                v = ctx.vg.bind(v)
                ctx.wv(v, "Dar2", 32)
                v.tensor_scalar(slo[0:1, 2:4], slo[0:1, 0:2],
                                1.0 / (2048.0 * 8192.0), None, ALU.mult)
                v.tensor_scalar(slo[0:1, 4:6], slo[0:1, 2:4], EPS_BL, 0.5,
                                ALU.add, ALU.mult)
                v.tensor_copy(slo[0:1, 8:12].rearrange("o (a b) -> o a b", b=2)
                              [:, :, 0:1],
                              slo[0:1, 2:4].rearrange("o (a b) -> o a b", b=1))
                v.tensor_copy(slo[0:1, 8:12].rearrange("o (a b) -> o a b", b=2)
                              [:, :, 1:2],
                              slo[0:1, 4:6].rearrange("o (a b) -> o a b", b=1))
                v.tensor_copy(scb[0:1, 0:4], slo[0:1, 8:12])
                v.tensor_tensor(slo[0:1, 12:16], slo[0:1, 8:12], scb[0:1, 0:4],
                                ALU.subtract)
                i = v.tensor_copy(scb[0:1, 4:8], slo[0:1, 12:16])
                ctx.inc("Dlo", i)

            @block.tensor
            def _(te):
                ctx.wv(te, "Dlo", 1)
                te.matmul(pb[:, 0:4], ctx.ones_row[:, 0:128], scb[0:1, 0:4],
                          start=True, stop=False)
                i = te.matmul(pb[:, 0:4], ctx.ones_row[:, 0:128], scb[0:1, 4:8],
                              start=False, stop=True)
                ctx.inc("Dbc", i)

            @block.vector
            def _(v):
                v = ctx.vg.bind(v)
                ctx.wv(v, "Dbc", 1)
                v.tensor_copy(gb[:, 0:4], pb[:, 0:4])
                i = v.tensor_scalar(gb[:, 4:6], gb[:, 0:2], -1.0, None, ALU.mult)
                v.tensor_scalar(gb[:, 6:8], gb[:, 2:4], -1.0, None, ALU.mult)
                ctx.wv(v, "Dzf", 1 + 32 * 16)
                for m in range(2):
                    gc, tc = (0, 1) if m == 0 else (2, 3)
                    ntc = 5 if m == 0 else 7
                    for hh in range(2):
                        w = wsl[m][:, hh * 8192:(hh + 1) * 8192]
                        v.tensor_scalar(tp[:], w, gb[:, tc:tc + 1],
                                        gb[:, gc:gc + 1], ALU.is_gt, ALU.mult)
                        v.tensor_scalar(tn[:], w, gb[:, ntc:ntc + 1],
                                        gb[:, gc:gc + 1], ALU.is_lt, ALU.mult)
                        if m * 2 + hh >= 1:
                            ctx.wv(v, "Dst", 16 * (m * 2 + hh))
                        i = v.tensor_tensor(th[:], tp[:], tn[:], ALU.subtract)
                        ctx.inc("Dtz", i)

            @block.gpsimd
            def _(g):
                g.memset(tp[:], 0.0).then_inc(ctx.sem("Dzf")[0], 1)
                ctx.sem("Dzf")[1] += 1
                ctx.wv(g, "Dzf", ctx.val("Dzf"))
                for stgn, grp in (("stage_wfc", 1), ("stage_wp", 4)):
                    for i in range(16):
                        ctx.inc("Dzf", g.dma_start(
                            out=dram[stgn].rearrange("(i p) c -> p i c", p=128)
                            [:, i * grp:(i + 1) * grp, :],
                            in_=tp[:].rearrange("p (o c) -> p o c", o=grp)), 16)
                with g.register("rkr3") as rkr3:
                    g.reg_load(rkr3, ctx.rks[0:1, 0:1])
                    for m in range(2):
                        stgn = "stage_wfc" if m == 0 else "stage_wp"
                        na = 2 if m == 0 else 8
                        rows = 256 if m == 0 else 1024
                        for hh in range(2):
                            ctx.wv(g, "Dtz", m * 2 + hh + 1)
                            if m == 0 and hh == 0:
                                ctx.wv(g, "Dzf", ctx.val("Dzf"))
                            for r in range(NC):
                                with g.If_eq(rkr3, r):
                                    dst = dram[stgn][r * rows:(r + 1) * rows,
                                                     :].rearrange(
                                        "(a p) c -> p a c", p=128)[
                                        :, hh * (na // 2):(hh + 1) * (na // 2), :]
                                    i = g.dma_start(
                                        out=dst,
                                        in_=th[:].rearrange("p (o c) -> p o c",
                                                            o=na // 2))
                                    i.then_inc(ctx.sem("Dst")[0], 16)
                            ctx.sem("Dst")[1] += 16
                ctx.wv(g, "Dst", 64)
                ctx.inc("Dcc", g.collective_compute(
                    "AllReduce", ALU.add, replica_groups=[list(range(NC))],
                    ins=[dram["stage_wfc"].opt()], outs=[dram["gath_wfc"].opt()]))
                ctx.inc("Dcc", g.collective_compute(
                    "AllReduce", ALU.add, replica_groups=[list(range(NC))],
                    ins=[dram["stage_wp"].opt()], outs=[dram["gath_wp"].opt()]))
                ctx.inc("Dcc", g.collective_compute(
                    "AllReduce", ALU.add, replica_groups=[list(range(NC))],
                    ins=[dram["ar_in"].opt()], outs=[dram["fence_d"].opt()]))

        if D_STAGES < 2:
            @block.sync
            def _(sync):
                ctx.wv(sync, "Dcc", 4)
                for t in range(LT):
                    ctx.inc("Dby", sync.dma_start(
                        out=dram["ddst"][t * 128:(t + 1) * 128, :],
                        in_=dram["dsrc"][t * 128:(t + 1) * 128, :]), 16)
                ctx.wv(sync, "Dby", 16 * LT)
            return phase_close(ctx, block)

        with ExitStack() as p2:
            hp_sum = p2.enter_context(nc.sbuf_tensor("d_hps", [128, 128], F32))
            hp_sq = p2.enter_context(nc.sbuf_tensor("d_hpq", [128, 128], F32))
            hp_max = p2.enter_context(nc.sbuf_tensor("d_hpx", [128, 32], F32))
            hp_min = p2.enter_context(nc.sbuf_tensor("d_hpn", [128, 32], F32))
            qsc = p2.enter_context(nc.sbuf_tensor("d_qsc", [128, 48], F32))
            qs2 = p2.enter_context(nc.sbuf_tensor("d_qs2", [128, 64], F32))

            pf = p2.enter_context(ExitStack())
            featT = pf.enter_context(nc.sbuf_tensor("d_featT", [128, KC * TOK],
                                                    BF16))
            with ExitStack() as pl:
                io = alloc_io(ctx, pl, "d_")
                if DEBUG_TAPS:
                    io["dbg_lnb"] = pl.enter_context(
                        nc.sbuf_tensor("d_dbglnb", [128, LT * D], BF16))
                    io["dbg_st"] = pl.enter_context(
                        nc.sbuf_tensor("d_dbgst", [128, LT * 16], F32))
                emit_preln(ctx, block, "DL", dram["dsrc"], featT, io, eps=EPS_BL,
                           quant=True, gate=("Dst", 64))
                if DEBUG_TAPS:
                    @block.sync
                    def _(sync):
                        ctx.wv(sync, "DLtre", LT * KC)
                        ctx.inc("Ddbg", sync.dma_start(out=dram["dbg_dlnb"],
                                                       in_=io["dbg_lnb"][:]), 16)
                        ctx.inc("Ddbg", sync.dma_start(out=dram["dbg_dst"],
                                                       in_=io["dbg_st"][:]), 16)

            if True:
                p3 = pf
                wfb = p3.enter_context(nc.sbuf_tensor("d_wfb", [128, KC * D], BF16))
                htl = [p3.enter_context(nc.sbuf_tensor("d_ht%d" % i, [128, D], BF16))
                       for i in range(2)]
                hsq = p3.enter_context(nc.sbuf_tensor("d_hsq", [128, D], BF16))
                pmm = [p3.enter_context(nc.psum_tensor("d_pm%d" % i, [128, 512], F32))
                       for i in range(8)]

                @block.sync
                def _(sync):
                    ctx.wv(sync, "DLtre", LT * KC)
                    for s_ in range(4):
                        if s_ >= 1:
                            ctx.wv(sync, "Dp1", s_ * LT * 4)
                        ctx.wv(sync, "Dcc", 4)
                        for k in range(KC):
                            ctx.inc("Dwf", sync.dma_start(
                                out=wfb[:, k * D:(k + 1) * D],
                                in_=dram["gath_wfc"][k * 128:(k + 1) * 128,
                                                     s_ * D:(s_ + 1) * D]), 16)

                @block.tensor
                def _(te):
                    ctx.wv(te, "DLtre", LT * KC)
                    for s_ in range(4):
                        ctx.wv(te, "Dwf", 16 * KC * (s_ + 1))
                        for t in range(LT):
                            for n in range(4):
                                idx = (s_ * LT + t) * 4 + n
                                if idx >= 8:
                                    ctx.wv(te, "Dge", idx - 7)
                                p = pmm[idx % 8]
                                for k in range(KC):
                                    i = te.matmul(
                                        p[:], fmv(featT, k, t * 128, 128),
                                        wfb[:, k * D + n * 512:k * D + n * 512 + 512],
                                        start=(k == 0), stop=(k == KC - 1))
                                ctx.inc("Dp1", i)

                @block.scalar
                def _(s):
                    for s_ in range(4):
                        for t in range(LT):
                            st_i = s_ * LT + t
                            b = st_i % 2
                            if st_i >= 2:
                                ctx.wv(s, "Dhs", 16 * (st_i - 1))
                                ctx.wv(s, "Dsqe", st_i - 1)
                            for n in range(4):
                                idx = st_i * 4 + n
                                ctx.wv(s, "Dp1", idx + 1)
                                pc = s_ * 32 + t * 4 + n
                                ctx.inc("Dge", s.activation(
                                    htl[b][:, n * 512:(n + 1) * 512],
                                    pmm[idx % 8][:], AF.Gelu,
                                    accum_out=hp_sum[:, pc:pc + 1]))
                            ctx.wv(s, "Dge", st_i * 4 + 4)
                            ctx.inc("Dsq", s.activation(
                                hsq[:], htl[b][:], AF.Square,
                                accum_out=hp_sq[:, s_ * 32 + t * 4:
                                                 s_ * 32 + t * 4 + 1]))

                @block.vector
                def _(v):
                    v = ctx.vg.bind(v)
                    for s_ in range(4):
                        for t in range(LT):
                            st_i = s_ * LT + t
                            b = st_i % 2
                            ctx.wv(v, "Dsq", st_i + 1)
                            v.tensor_reduce(hp_max[:, s_ * 8 + t:s_ * 8 + t + 1],
                                            htl[b][:], AX.X, ALU.max)
                            i = v.tensor_reduce(hp_min[:, s_ * 8 + t:s_ * 8 + t + 1],
                                                htl[b][:], AX.X, ALU.min)
                            ctx.inc("Dsqe", i)

                @block.gpsimd
                def _(g):
                    for s_ in range(4):
                        for t in range(LT):
                            st_i = s_ * LT + t
                            ctx.wv(g, "Dsqe", st_i + 1)
                            ctx.inc("Dhs", g.dma_start(
                                out=dram["h_spill"][t * 128:(t + 1) * 128,
                                                    s_ * D:(s_ + 1) * D],
                                in_=htl[st_i % 2][:]), 16)

            pf.close()

            if DEBUG_TAPS:
                @block.sync
                def _(sync):
                    ctx.wv(sync, "DLtre", LT * KC)
                    ctx.inc("Ddbg", sync.dma_start(out=dram["dbg_dfeat"],
                                                   in_=featT[:]), 16)
                    ctx.wv(sync, "Dhs", 16 * 32)
                    ctx.inc("Ddbg", sync.dma_start(
                        out=dram["dbg_dh"], in_=dram["h_spill"][:, :]), 16)
                    ctx.wv(sync, "Dcc", 4)
                    ctx.inc("Ddbg", sync.dma_start(
                        out=dram["dbg_dwt"], in_=dram["gath_wfc"][0:128, :]), 16)
                    ctx.inc("Ddbg", sync.dma_start(
                        out=dram["dbg_dwp"], in_=dram["gath_wp"][0:128, :]), 16)
                    ctx.wv(sync, "Ddbg", 64)

            if D_STAGES < 3:
                @block.sync
                def _(sync):
                    ctx.wv(sync, "Dhs", 16 * 32)
                    for t in range(LT):
                        ctx.inc("Dby", sync.dma_start(
                            out=dram["ddst"][t * 128:(t + 1) * 128, :],
                            in_=dram["dsrc"][t * 128:(t + 1) * 128, :]), 16)
                    ctx.wv(sync, "Dby", 16 * LT)
                return phase_close(ctx, block)

            @block.vector
            def _(v):
                v = ctx.vg.bind(v)
                ctx.wv(v, "Dhs", 16 * 32)
                for t in range(LT):
                    c = t * 6
                    v.tensor_reduce(
                        qsc[:, c:c + 1],
                        hp_sum[:, :].rearrange("p (s f) -> p s f", s=4)[
                            :, :, t * 4:t * 4 + 4], AX.XY, ALU.add)
                    v.tensor_reduce(
                        qsc[:, c + 1:c + 2],
                        hp_sq[:, :].rearrange("p (s f) -> p s f", s=4)[
                            :, :, t * 4:t * 4 + 1], AX.XY, ALU.add)
                    v.tensor_reduce(
                        qsc[:, c + 2:c + 3],
                        hp_max[:, :].rearrange("p (s f) -> p s f", s=4)[
                            :, :, t:t + 1], AX.XY, ALU.max)
                    i = v.tensor_reduce(
                        qsc[:, c + 3:c + 4],
                        hp_min[:, :].rearrange("p (s f) -> p s f", s=4)[
                            :, :, t:t + 1], AX.XY, ALU.min)
                    ctx.inc("Dqs", i)
                for t in range(LT):
                    c, o = t * 6, t * 8
                    v.tensor_scalar(qs2[:, o:o + 1], qsc[:, c:c + 1], 1.0 / 8192,
                                    None, ALU.mult)
                    v.tensor_scalar(qs2[:, o + 1:o + 2], qsc[:, c + 1:c + 2],
                                    1.0 / 8192, None, ALU.mult)
                    v.tensor_tensor(qs2[:, o + 2:o + 3], qs2[:, o:o + 1],
                                    qs2[:, o:o + 1], ALU.mult)
                    v.tensor_tensor(qs2[:, o + 2:o + 3], qs2[:, o + 1:o + 2],
                                    qs2[:, o + 2:o + 3], ALU.subtract)
                    i = v.tensor_scalar(qs2[:, o + 2:o + 3], qs2[:, o + 2:o + 3],
                                        EPS_BL, None, ALU.add)
                    ctx.inc("Dv1", i)
                ctx.wv(v, "Dv2", LT)
                for t in range(LT):
                    c, o = t * 6, t * 8
                    v.reciprocal(qs2[:, o + 3:o + 4], qs2[:, o + 7:o + 8])
                    v.tensor_tensor(qs2[:, o + 4:o + 5], qsc[:, c + 2:c + 3],
                                    qs2[:, o:o + 1], ALU.subtract)
                    v.tensor_tensor(qs2[:, o + 5:o + 6], qs2[:, o:o + 1],
                                    qsc[:, c + 3:c + 4], ALU.subtract)
                    v.tensor_tensor(qs2[:, o + 4:o + 5], qs2[:, o + 4:o + 5],
                                    qs2[:, o + 5:o + 6], ALU.max)
                    v.reciprocal(qs2[:, o + 5:o + 6], qs2[:, o + 4:o + 5])
                    v.tensor_scalar(qs2[:, o + 5:o + 6], qs2[:, o + 5:o + 6], QB,
                                    None, ALU.mult)
                    v.tensor_tensor(qs2[:, o + 6:o + 7], qs2[:, o + 3:o + 4],
                                    qs2[:, o + 4:o + 5], ALU.mult)
                    v.tensor_scalar(qs2[:, o + 6:o + 7], qs2[:, o + 6:o + 7],
                                    1.0 / QB, None, ALU.mult)
                    v.tensor_tensor(qs2[:, o + 7:o + 8], qs2[:, o:o + 1],
                                    qs2[:, o + 5:o + 6], ALU.mult)
                    i = v.tensor_scalar(qs2[:, o + 7:o + 8], qs2[:, o + 7:o + 8],
                                        -1.0, None, ALU.mult)
                    ctx.inc("Dv3", i)

            @block.scalar
            def _(s):
                ctx.wv(s, "Dv1", LT)
                for t in range(LT):
                    o = t * 8
                    ctx.inc("Dv2", s.activation(qs2[:, o + 7:o + 8],
                                                qs2[:, o + 2:o + 3], AF.Sqrt))

            if DEBUG_TAPS:
                @block.sync
                def _(sync):
                    ctx.wv(sync, "Dv3", LT)
                    ctx.inc("Ddbg", sync.dma_start(out=dram["dbg_dqs"],
                                                   in_=qs2[:]), 16)

            if D_STAGES < 4:
                @block.sync
                def _(sync):
                    ctx.wv(sync, "Dv3", LT)
                    for t in range(LT):
                        ctx.inc("Dby", sync.dma_start(
                            out=dram["ddst"][t * 128:(t + 1) * 128, :],
                            in_=dram["dsrc"][t * 128:(t + 1) * 128, :]), 16)
                    ctx.wv(sync, "Dby", 16 * LT)
                return phase_close(ctx, block)

            with ExitStack() as p4:
                xh2 = p4.enter_context(nc.sbuf_tensor("d_xh2", [128, KC * 1024], BF16))
                wpb = p4.enter_context(nc.sbuf_tensor("d_wpb", [128, 8 * D], BF16))
                acc = p4.enter_context(nc.sbuf_tensor("d_acc", [128, LT * D], F32))
                hl = [p4.enter_context(nc.sbuf_tensor("d_hl%d" % i, [128, D], BF16))
                      for i in range(2)]
                t1 = p4.enter_context(nc.sbuf_tensor("d_t1", [128, D], F32))
                xq = [p4.enter_context(nc.sbuf_tensor("d_xq%d" % i, [128, D], BF16))
                      for i in range(2)]
                ptr2 = [p4.enter_context(nc.psum_tensor("d_pt2%d" % i, [128, 128],
                                                        BF16)) for i in range(2)]
                pm2 = [p4.enter_context(nc.psum_tensor("d_pn%d" % i, [128, 512], F32))
                       for i in range(4)]

                @block.gpsimd
                def _(g):
                    ctx.wv(g, "Dv3", LT)
                    for s_ in range(4):
                        for t in range(LT):
                            st_i = s_ * LT + t
                            if st_i >= 2:
                                ctx.wv(g, "Dq2", st_i - 1)
                            ctx.inc("Dhl", g.dma_start(
                                out=hl[st_i % 2][:],
                                in_=dram["h_spill"][t * 128:(t + 1) * 128,
                                                    s_ * D:(s_ + 1) * D]), 16)

                @block.scalar
                def _(s):
                    ctx.wv(s, "Dv3", LT)
                    for s_ in range(4):
                        for t in range(LT):
                            st_i = s_ * LT + t
                            o = t * 8
                            ctx.wv(s, "Dhl", 16 * (st_i + 1))
                            if st_i >= 1:
                                ctx.wv(s, "Dclip", st_i)
                            ctx.inc("Dq1", s.activation(t1[:], hl[st_i % 2][:],
                                                        AF.Identity,
                                                        bias=qs2[:, o + 7:o + 8],
                                                        scale=qs2[:, o + 5:o + 6]))
                            ctx.wv(s, "Dclip", st_i + 1)
                            if st_i >= 2:
                                ctx.wv(s, "Dtr2", (st_i - 1) * KC)
                            ctx.inc("Dq2", s.activation(xq[st_i % 2][:], t1[:],
                                                        AF.Identity,
                                                        scale=qs2[:, o + 6:o + 7]))


                @block.sync
                def _(sync):
                    ctx.wv(sync, "Dv3", LT)
                    for s_ in range(4):
                        for kh in range(2):
                            sk = s_ * 2 + kh
                            if sk >= 1:
                                ctx.wv(sync, "Dp2", sk * LT * 4)
                            for k in range(8):
                                ctx.inc("Dwp", sync.dma_start(
                                    out=wpb[:, k * D:(k + 1) * D],
                                    in_=dram["gath_wp"][
                                        s_ * D + kh * 1024 + k * 128:
                                        s_ * D + kh * 1024 + (k + 1) * 128, :]), 16)

                @block.tensor
                def _(te):
                    for s_ in range(4):
                        for t in range(LT):
                            st_i = s_ * LT + t
                            ctx.wv(te, "Dq2", st_i + 1)
                            for k in range(KC):
                                idx = st_i * KC + k
                                if idx >= 2:
                                    ctx.wv(te, "Dtre2", idx - 1)
                                ctx.inc("Dtr2", te.transpose(
                                    ptr2[k % 2][:],
                                    xq[st_i % 2][:, k * 128:(k + 1) * 128],
                                    ctx.ident[:]))
                        ctx.wv(te, "Dtre2", (s_ + 1) * LT * KC)
                        for kh in range(2):
                            sk = s_ * 2 + kh
                            ctx.wv(te, "Dwp", 16 * 8 * (sk + 1))
                            for t in range(LT):
                                for n in range(4):
                                    idx = (sk * LT + t) * 4 + n
                                    if idx >= 4:
                                        ctx.wv(te, "Dae", idx - 3)
                                    p = pm2[idx % 4]
                                    for k in range(8):
                                        i = te.matmul(
                                            p[:],
                                            xh2[:, (kh * 8 + k) * 1024 + t * 128:
                                                (kh * 8 + k) * 1024 + t * 128 + 128],
                                            wpb[:, k * D + n * 512:
                                                k * D + n * 512 + 512],
                                            start=(k == 0), stop=(k == 7))
                                    ctx.inc("Dp2", i)

                rdy = [("Dae", (7 * LT + t) * 4 + 4) for t in range(LT)]

                @block.vector
                def _(v):
                    v = ctx.vg.bind(v)
                    for s_ in range(4):
                        for t in range(LT):
                            st_i = s_ * LT + t
                            ctx.wv(v, "Dq1", st_i + 1)
                            ctx.inc("Dclip", v.tensor_scalar(t1[:], t1[:], CLIP,
                                                             -CLIP, ALU.min,
                                                             ALU.max))
                            for k in range(KC):
                                ctx.wv(v, "Dtr2", st_i * KC + k + 1)
                                ctx.inc("Dtre2", v.tensor_copy(
                                    xh2[:, k * 1024 + t * 128:
                                        k * 1024 + t * 128 + 128],
                                    ptr2[k % 2][:]))
                        for kh in range(2):
                            sk = s_ * 2 + kh
                            for t in range(LT):
                                for n in range(4):
                                    idx = (sk * LT + t) * 4 + n
                                    ctx.wv(v, "Dp2", idx + 1)
                                    dst = acc[:, t * D + n * 512:
                                              t * D + n * 512 + 512]
                                    if sk == 0:
                                        i = v.tensor_copy(dst, pm2[idx % 4][:])
                                    else:
                                        i = v.tensor_tensor(dst, pm2[idx % 4][:],
                                                            dst, ALU.add)
                                    ctx.inc("Dae", i)

                if DEBUG_TAPS:
                    @block.sync
                    def _(sync):
                        ctx.wv(sync, "Dae", 8 * LT * 4)
                        ctx.inc("Ddbg", sync.dma_start(out=dram["dbg_dacc"],
                                                       in_=acc[:]), 16)

                with ExitStack() as p5:
                    io2 = alloc_io(ctx, p5, "d2_", slim=True)
                    emit_epilogue(ctx, block, "DE", acc, dram["dsrc"],
                                  dram["ddst"], io2, rdy)

                    @block.sync
                    def _(sync):
                        ctx.wv(sync, "DEyo", ctx.val("DEyo"))

        return phase_close(ctx, block)


def lt0_of(jg):
    return 0 if jg <= 8 else (jg - 1) // 8


def emit_phase_A(ctx, block, dram, bar):
    nc = ctx.nc
    with ExitStack() as ps:
        wb = ps.enter_context(nc.sbuf_tensor("a_wb", [128, KC * 1024], BF16))
        qoT = ps.enter_context(nc.sbuf_tensor("a_qoT", [128, KC * TOK], BF16))
        amask = ps.enter_context(nc.sbuf_tensor("a_amask", [128, 64 * 128], BF16))
        bias16 = ps.enter_context(nc.sbuf_tensor("a_bias16", [128, 16], F32))
        maxkb = ps.enter_context(nc.sbuf_tensor("a_maxkb", [128, 1], F32))
        phase_gate(ctx, block, bar)

        @block.gpsimd
        def _(g):
            ctx.inc("Aam", g.dma_start(
                out=amask[:].rearrange("p (a c) -> p a c", a=64),
                in_=dram["amask"].rearrange("(a p) c -> p a c", p=128)), 16)

        with ExitStack() as p2:
            featT = p2.enter_context(nc.sbuf_tensor("a_featT", [128, KC * TOK],
                                                    BF16))
            with ExitStack() as pl:
                io = alloc_io(ctx, pl, "a_")
                emit_preln(ctx, block, "AL", dram["asrc"], featT, io)

            zbuf = p2.enter_context(nc.sbuf_tensor("a_zbuf", [128, 8192], BF16))
            ktb = [p2.enter_context(nc.sbuf_tensor("a_ktb%d" % i, [128, 1024], BF16))
                   for i in range(2)]
            k2b = [p2.enter_context(nc.sbuf_tensor("a_k2b%d" % i, [128, 1024], BF16))
                   for i in range(2)]
            vtb = [p2.enter_context(nc.sbuf_tensor("a_vtb%d" % i, [128, 1024], BF16))
                   for i in range(2)]
            sqn = p2.enter_context(nc.sbuf_tensor("a_sqn", [128, 128], F32))
            mqp = p2.enter_context(nc.sbuf_tensor("a_mqp", [128, 16], F32))
            nsb = p2.enter_context(nc.sbuf_tensor("a_nsb", [128, 16], BF16))
            nrow = p2.enter_context(nc.sbuf_tensor("a_nrow", [1, 160], BF16))
            nsc = p2.enter_context(nc.sbuf_tensor("a_nsc", [128, 8], F32))
            mk8 = p2.enter_context(nc.sbuf_tensor("a_mk8", [1, 8], F32))
            pk_ = [p2.enter_context(nc.psum_tensor("a_pk%d" % i, [128, 512], F32))
                   for i in range(4)]
            pnk = p2.enter_context(nc.psum_tensor("a_pnk", [128, 128], F32))
            pnq = p2.enter_context(nc.psum_tensor("a_pnq", [128, 128], F32))
            pnr = p2.enter_context(nc.psum_tensor("a_pnr", [128, 16], F32))
            pnt = p2.enter_context(nc.psum_tensor("a_pnt", [16, 128], BF16))

            order = [("K", "l0_wkT"), ("V", "l0_wvT"), ("Q", "l0_wqT")]


            @block.tensor
            def _(te):
                for wi, (wname, wd) in enumerate(order):
                    for half in range(2):
                        gidx = wi * 2 + half
                        if wname == "Q" and half == 0:
                            ctx.wv(te, "Acc", 2)
                        ctx.wv(te, "Awl", 16 * KC * (gidx + 1))
                        for u in range(8):
                            for th in range(2):
                                idx = (gidx * 8 + u) * 2 + th
                                if idx >= 4:
                                    ctx.wv(te, "Ape", idx - 3)
                                p = pk_[idx % 4]
                                if wname == "V":
                                    for k in range(KC):
                                        i = te.matmul(
                                            p[:], fmv(featT, k, u * 128, 128),
                                            wb[:, k * 1024 + th * 512:
                                               k * 1024 + th * 512 + 512],
                                            start=(k == 0), stop=(k == KC - 1))
                                else:
                                    for k in range(KC):
                                        i = te.matmul(
                                            p[:],
                                            wb[:, k * 1024 + u * 128:
                                               k * 1024 + u * 128 + 128],
                                            fmv(featT, k, th * 512, 512),
                                            start=(k == 0), stop=(k == KC - 1))
                                ctx.inc("Amm", i, 1) if th == 1 else None
                            if wname == "K":
                                m = half * 8 + u
                                ctx.wv(te, "Aksq", m + 1)
                                for t in range(LT):
                                    i = te.matmul(pnk[:, m * 8 + t:m * 8 + t + 1],
                                                  k2b[m % 2][:, t * 128:(t + 1) * 128],
                                                  ctx.ones_col[:],
                                                  start=True, stop=True)
                                ctx.inc("Aknm", i)
                            elif wname == "Q":
                                m = half * 8 + u
                                ctx.wv(te, "Aqsq", m + 1)
                                for t in range(LT):
                                    i = te.matmul(pnq[:, m * 8 + t:m * 8 + t + 1],
                                                  k2b[m % 2][:, t * 128:(t + 1) * 128],
                                                  ctx.ones_col[:],
                                                  start=True, stop=True)
                                ctx.inc("Aqnm", i)

            @block.vector
            def _(v):
                v = ctx.vg.bind(v)
                for wi, (wname, wd) in enumerate(order):
                    for half in range(2):
                        gidx = wi * 2 + half
                        for u in range(8):
                            m = half * 8 + u
                            for th in range(2):
                                idx = (gidx * 8 + u) * 2 + th
                                ctx.wv(v, "Amm", (idx // 2) + 1)
                                if wname == "K":
                                    if m >= 2 and th == 0:
                                        ctx.wv(v, "Aks", 32 * (m - 1))
                                        ctx.wv(v, "Aksq", m - 1)
                                    dst = ktb[m % 2][:, th * 512:(th + 1) * 512]
                                elif wname == "Q":
                                    dst = fmv(qoT, m, th * 512, 512)
                                else:
                                    vu = (gidx - 2) * 8 + u
                                    if vu >= 2 and th == 0:
                                        ctx.wv(v, "Avs", 32 * (vu - 1))
                                    dst = vtb[u % 2][:, th * 512:(th + 1) * 512]
                                ctx.inc("Ape", v.tensor_copy(dst, pk_[idx % 4][:]))

            @block.scalar
            def _(s):
                for wname, cnt_sem in (("K", "Aksq"), ("Q", "Aqsq")):
                    for m in range(16):
                        uidx = m if wname == "K" else 32 + m
                        ctx.wv(s, "Ape", (uidx + 1) * 2)
                        if wname == "Q" and m < 2:
                            ctx.wv(s, "Aknm", 16)
                        if m >= 2:
                            ctx.wv(s, "Aknm" if wname == "K" else "Aqnm", m - 1)
                        ctx.inc(cnt_sem, s.activation(k2b[m % 2][:],
                                                      ktb[m % 2][:] if wname == "K"
                                                      else fmv(qoT, m, 0, 1024),
                                                      AF.Square))

            def stream_half(g, wd, gidx, half):
                if gidx > 0:
                    ctx.wv(g, "Amm", gidx * 8)
                for k in range(KC):
                    ctx.inc("Awl", g.dma_start(
                        out=wb[:, k * 1024:(k + 1) * 1024],
                        in_=dram[wd][k * 128:(k + 1) * 128,
                                     half * 1024:(half + 1) * 1024]), 16)

            def kstage(g, rkreg, ms):
                for m in ms:
                    ctx.wv(g, "Ape", (m + 1) * 2)
                    for r in range(NC):
                        with g.If_eq(rkreg, r):
                            i = g.dma_start(
                                out=dram["stage_k"][r * D + m * 128:
                                                    r * D + (m + 1) * 128, :],
                                in_=ktb[m % 2][:])
                            i.then_inc(ctx.sem("Aks")[0], 16)
                    ctx.sem("Aks")[1] += 16
                    ctx.inc("Aks", g.dma_start(
                        out=dram["stage_k2"][m * 128:(m + 1) * 128, :],
                        in_=ktb[m % 2][:]), 16)

            def vstage(g, rkreg, vus):
                for vu in vus:
                    half, t = vu // 8, vu % 8
                    ctx.wv(g, "Ape", (16 + vu + 1) * 2)
                    for r in range(NC):
                        with g.If_eq(rkreg, r):
                            dst = dram["stage_v"][
                                r * NH * TOK:(r + 1) * NH * TOK, :].rearrange(
                                "(h l p) e -> p h l e", p=128, l=LT)[
                                :, half * 8:(half + 1) * 8, t:t + 1, :]
                            i = g.dma_start(
                                out=dst,
                                in_=vtb[t % 2][:].rearrange(
                                    "p (h o e) -> p h o e", h=8, o=1))
                            i.then_inc(ctx.sem("Avs")[0], 16)
                    ctx.sem("Avs")[1] += 16
                    dst = dram["stage_v2"].rearrange(
                        "(h l p) e -> p h l e", p=128, l=LT)[
                        :, half * 8:(half + 1) * 8, t:t + 1, :]
                    ctx.inc("Avs", g.dma_start(
                        out=dst,
                        in_=vtb[t % 2][:].rearrange("p (h o e) -> p h o e",
                                                    h=8, o=1)), 16)

            @block.gpsimd
            def _(g):
                with g.register("rkr2") as rkr2:
                    g.reg_load(rkr2, ctx.rks[0:1, 0:1])
                    ctx.wv(g, "ALtre", LT * KC)
                    g.memset(zbuf[:], 0.0).then_inc(ctx.sem("Azf")[0], 1)
                    ctx.sem("Azf")[1] += 1
                    ctx.wv(g, "Azf", ctx.val("Azf"))
                    for i in range(16):
                        ctx.inc("Azf", g.dma_start(
                            out=dram["stage_k"].rearrange("(i p) c -> p i c",
                                                          p=128)[:, i * 8:
                                                                 (i + 1) * 8, :],
                            in_=zbuf[:, 0:8192].rearrange("p (i c) -> p i c",
                                                          i=8)), 16)
                    for i in range(16):
                        ctx.inc("Azf", g.dma_start(
                            out=dram["stage_v"].rearrange("(i p) c -> p i c",
                                                          p=128)[:, i * 64:
                                                                 (i + 1) * 64, :],
                            in_=zbuf[:, 0:8192].rearrange("p (i c) -> p i c",
                                                          i=64)), 16)
                    ctx.wv(g, "Azf", ctx.val("Azf"))
                    stream_half(g, "l0_wkT", 0, 0)
                    kstage(g, rkr2, range(0, 8))
                    stream_half(g, "l0_wkT", 1, 1)
                    kstage(g, rkr2, range(8, 16))
                    stream_half(g, "l0_wvT", 2, 0)
                    vstage(g, rkr2, range(0, 8))
                    stream_half(g, "l0_wvT", 3, 1)
                    vstage(g, rkr2, range(8, 16))
                    stream_half(g, "l0_wqT", 4, 0)
                    ctx.wv(g, "Aks", 512)
                    ctx.wv(g, "Avs", 512)
                    ctx.wv(g, "Awl", 16 * KC * 5)
                    ctx.wv(g, "Amm", 32)
                    ctx.wv(g, "Aknm", 16)
                    ctx.wv(g, "Aksq", 16)
                    ctx.inc("Acc", g.collective_compute(
                        "AllReduce", ALU.add, replica_groups=[list(range(NC))],
                        ins=[dram["stage_k"].opt()], outs=[dram["gath_k"].opt()]))
                    ctx.inc("Acc", g.collective_compute(
                        "AllReduce", ALU.add, replica_groups=[list(range(NC))],
                        ins=[dram["stage_v"].opt()], outs=[dram["gath_v"].opt()]))
                    stream_half(g, "l0_wqT", 5, 1)

            @block.vector
            def _(v):
                v = ctx.vg.bind(v)
                ctx.wv(v, "Acc", 2)
                ctx.wv(v, "Aknm", 16)
                v.tensor_copy(sqn[:], pnk[:])
                v.tensor_reduce(nsc[:, 0:1], sqn[:], AX.X, ALU.max)
                ctx.inc("Akn1", v.tensor_copy(nsb[:, 0:1], nsc[:, 0:1]))

            @block.tensor
            def _(te):
                ctx.wv(te, "Akn1", 1)
                ctx.inc("Akn2", te.transpose(pnt[0:1, :], nsb[:, 0:1],
                                             ctx.ident[:]))

            @block.vector
            def _(v):
                v = ctx.vg.bind(v)
                ctx.wv(v, "Akn2", 1)
                v.tensor_copy(nrow[0:1, 0:128], pnt[0:1, :])
                ctx.inc("Akn3", v.tensor_reduce(nsc[0:1, 1:2], nrow[0:1, 0:128],
                                                AX.X, ALU.max))

            @block.gpsimd
            def _(g):
                ctx.wv(g, "Akn3", 1)
                ctx.inc("Amk", g.dma_start(out=dram["stage_mk"],
                                           in_=nsc[0:1, 1:2]), 16)
                ctx.wv(g, "Amk", 16)
                ctx.inc("Acc", g.collective_compute(
                    "AllReduce", ALU.max, replica_groups=[list(range(NC))],
                    ins=[dram["stage_mk"].opt()], outs=[dram["gath_mk"].opt()]))
                ctx.inc("Acc", g.collective_compute(
                    "AllReduce", ALU.add, replica_groups=[list(range(NC))],
                    ins=[dram["stage_mk"].opt()], outs=[dram["fence_a"].opt()]))
                ctx.wv(g, "Acc", 4)
                ctx.inc("Amk", g.dma_start(out=mk8[0:1, 0:1],
                                           in_=dram["gath_mk"]), 16)

            @block.vector
            def _(v):
                v = ctx.vg.bind(v)
                ctx.wv(v, "Amk", 32)
                ctx.inc("Akn4", v.tensor_reduce(nsc[0:1, 2:3], mk8[0:1, 0:1],
                                                AX.X, ALU.max))

            @block.scalar
            def _(s):
                ctx.wv(s, "Akn4", 1)
                ctx.inc("Akn5", s.activation(nsc[0:1, 3:4], nsc[0:1, 2:3], AF.Sqrt))

            @block.vector
            def _(v):
                v = ctx.vg.bind(v)
                ctx.wv(v, "Akn5", 1)
                ctx.inc("Akn6", v.tensor_copy(nrow[0:1, 128:129], nsc[0:1, 3:4]))

            @block.tensor
            def _(te):
                ctx.wv(te, "Akn6", 1)
                ctx.inc("Akn7", te.matmul(pnr[:, 0:1], ctx.ones_row[:, 0:128],
                                          nrow[0:1, 128:129],
                                          start=True, stop=True))

            @block.vector
            def _(v):
                v = ctx.vg.bind(v)
                ctx.wv(v, "Akn7", 1)
                ctx.inc("Amaxk", v.tensor_copy(maxkb[:], pnr[:, 0:1]))

            @block.vector
            def _(v):
                v = ctx.vg.bind(v)
                ctx.wv(v, "Aqnm", 16)
                v.tensor_copy(sqn[:], pnq[:])
                for h in range(16):
                    v.tensor_reduce(mqp[:, h:h + 1], sqn[:, h * 8:(h + 1) * 8],
                                    AX.X, ALU.max)
                ctx.inc("Aqn1", v.tensor_copy(nsb[:, 0:16], mqp[:]))

            @block.tensor
            def _(te):
                ctx.wv(te, "Aqn1", 1)
                ctx.inc("Aqn2", te.transpose(pnt[:, :], nsb[:, 0:16], ctx.ident[:]))

            @block.vector
            def _(v):
                v = ctx.vg.bind(v)
                ctx.wv(v, "Aqn2", 1)
                v.tensor_copy(sqn[0:16, 0:128], pnt[:, :])
                ctx.inc("Aqn3", v.tensor_reduce(nsc[0:16, 4:5], sqn[0:16, 0:128],
                                                AX.X, ALU.max))

            @block.scalar
            def _(s):
                ctx.wv(s, "Aqn3", 1)
                ctx.inc("Aqn4", s.activation(nsc[0:16, 5:6], nsc[0:16, 4:5],
                                             AF.Sqrt))

            @block.vector
            def _(v):
                v = ctx.vg.bind(v)
                ctx.wv(v, "Aqn4", 1)
                ctx.wv(v, "Amaxk", 1)
                v.tensor_scalar(nsc[0:16, 6:7], nsc[0:16, 5:6], maxkb[0:16, 0:1],
                                -SCALE, ALU.mult, ALU.mult)
                ctx.inc("Aqn5", v.tensor_copy(nsb[0:16, 1:2], nsc[0:16, 6:7]))

            @block.tensor
            def _(te):
                ctx.wv(te, "Aqn5", 1)
                ctx.inc("Aqn6", te.transpose(pnt[0:1, 0:16], nsb[0:16, 1:2],
                                             ctx.ident[0:16, 0:16]))

            @block.vector
            def _(v):
                v = ctx.vg.bind(v)
                ctx.wv(v, "Aqn6", 1)
                ctx.inc("Aqn7", v.tensor_copy(nrow[0:1, 144:160], pnt[0:1, 0:16]))

            @block.tensor
            def _(te):
                ctx.wv(te, "Aqn7", 1)
                ctx.inc("Aqn8", te.matmul(pnr[:, 0:16], ctx.ones_row[:, 0:128],
                                          nrow[0:1, 144:160], start=True, stop=True))

            @block.vector
            def _(v):
                v = ctx.vg.bind(v)
                ctx.wv(v, "Aqn8", 1)
                ctx.inc("Abias", v.tensor_copy(bias16[:], pnr[:, 0:16]))

        with ExitStack() as p4:
            kbuf = [p4.enter_context(nc.sbuf_tensor("a_kb%d" % i, [128, 8192], BF16))
                    for i in range(2)]
            vbuf = [p4.enter_context(nc.sbuf_tensor("a_vb%d" % i, [128, 8192], BF16))
                    for i in range(2)]
            pt = [p4.enter_context(nc.sbuf_tensor("a_pt%d" % i, [128, 1024], BF16))
                  for i in range(2)]
            lrow = p4.enter_context(nc.sbuf_tensor("a_lrow", [1, 1024], BF16))
            lrow32 = p4.enter_context(nc.sbuf_tensor("a_lrow32", [1, 1024], F32))
            rb = p4.enter_context(nc.sbuf_tensor("a_rb", [128, 1024], F32))
            pST = [p4.enter_context(nc.psum_tensor("a_pST%d" % i, [128, 512], F32))
                   for i in range(4)]
            pOT = [p4.enter_context(nc.psum_tensor("a_pOT%d" % i, [128, 512], F32))
                   for i in range(2)]
            pL = [p4.enter_context(nc.psum_tensor("a_pL%d" % i, [1, 512], F32))
                  for i in range(2)]

            def kcol(jg):
                r, l = OWNER[jg]
                return r * 1024 + l * 128

            @block.sync
            def _(sync):
                ctx.wv(sync, "Acc", 4)
                ctx.wv(sync, "Abias", 1)
                for h in range(NH):
                    if h >= 2:
                        ctx.wv(sync, "Ahd", h - 1)
                    i = sync.dma_start(
                        out=kbuf[h % 2][:].rearrange("p (r c) -> p r c", r=NC),
                        in_=dram["gath_k"].rearrange(
                            "(r a p) c -> p r a c", p=128, a=16)[:, :, h:h + 1, :]
                        .rearrange("p r o c -> p (r o) c"))
                    ctx.inc("Akv", i, 16)
                    for r in range(NC):
                        i = sync.dma_start(
                            out=vbuf[h % 2][:, r * 1024:(r + 1) * 1024]
                            .rearrange("p (l e) -> p l e", l=LT),
                            in_=dram["gath_v"][r * 16384 + h * 1024:
                                               r * 16384 + (h + 1) * 1024,
                                               :].rearrange(
                                "(l p) e -> p l e", p=128))
                        ctx.inc("Akv", i, 16)
                    if DEBUG_TAPS and h == 2:
                        ctx.inc("Adbg", sync.dma_start(out=dram["dbg_akb"],
                                                       in_=kbuf[0][:]), 16)
                        ctx.inc("Adbg", sync.dma_start(out=dram["dbg_avb"],
                                                       in_=vbuf[0][:]), 16)

            def emit_S(te, h, jg):
                set_ = jg % 2
                lt0 = lt0_of(jg)
                ltm = (jg - 1) // 8 if jg >= 1 else -1
                kc = kcol(jg)
                last = None
                for (lo, hi) in seg_list(lt0 * 128, 1024):
                    bank = lo // 512
                    pdst = pST[set_ * 2 + bank]
                    has_m = ltm >= 0 and lo <= ltm * 128 < hi
                    last = te.matmul(pdst[:, lo - bank * 512:hi - bank * 512],
                                     kbuf[h % 2][:, kc:kc + 128],
                                     qT_att[:, h * 1024 + lo:h * 1024 + hi],
                                     start=True, stop=not has_m)
                    if has_m:
                        i_ = jg - 8 * ltm - 1
                        mc = ltm * 128 - bank * 512
                        last = te.matmul(pdst[:, mc:mc + 128], ctx.ident[:],
                                         amask[:, (ltm * 8 + i_) * 128:
                                               (ltm * 8 + i_) * 128 + 128],
                                         start=False, stop=True)
                ctx.inc("Ast", last)

            qT_att = qoT

            @block.tensor
            def _(te):
                for h in range(NH):
                    ctx.wv(te, "Akv", 144 * (h + 1))
                    if h >= 1:
                        ctx.wv(te, "Abce", 2 * h)
                        ctx.wv(te, "Aoev", 2 * h)
                        ctx.wv(te, "Alr", h)
                    else:
                        ctx.wv(te, "Abias", 1)
                    emit_S(te, h, 0)
                    emit_S(te, h, 1)
                    for jg in range(64):
                        lt0 = lt0_of(jg)
                        ctx.wv(te, "Aex", min(h * 64 + jg + 2, h * 64 + 64))
                        kc = kcol(jg)
                        for (lo, hi) in seg_list(lt0 * 128, 1024):
                            bank = lo // 512
                            te.matmul(pL[bank][:, lo - bank * 512:hi - bank * 512],
                                      ctx.ones_col[:],
                                      pt[jg % 2][:, lo:hi],
                                      start=(jg == 0),
                                      stop=(jg == (32 if bank == 0 else 63)))
                        last = None
                        for (lo, hi) in seg_list(lt0 * 128, 1024):
                            bank = lo // 512
                            last = te.matmul(
                                pOT[bank][:, lo - bank * 512:hi - bank * 512],
                                vbuf[h % 2][:, kc:kc + 128],
                                pt[jg % 2][:, lo:hi],
                                start=(jg == 0), stop=(jg == 63))
                        ctx.inc("Apv", last)
                        if jg + 2 <= 63:
                            emit_S(te, h, jg + 2)
                    ctx.wv(te, "Alr", h + 1)
                    i0 = te.matmul(pST[0][:, :], ctx.ones_row[:, 0:128],
                                   lrow[0:1, 0:512], start=True, stop=True)
                    i1 = te.matmul(pST[1][:, :], ctx.ones_row[:, 0:128],
                                   lrow[0:1, 512:1024], start=True, stop=True)
                    ctx.inc("Ahd", i0)
                    ctx.inc("Abc", i1)

            @block.scalar
            def _(s):
                for h in range(NH):
                    for jg in range(64):
                        set_ = jg % 2
                        lt0 = lt0_of(jg)
                        ctx.wv(s, "Ast", h * 64 + jg + 1)
                        if h * 64 + jg >= 2:
                            ctx.wv(s, "Apv", h * 64 + jg - 1)
                        last = None
                        for (lo, hi) in seg_list(lt0 * 128, 1024):
                            bank = lo // 512
                            last = s.activation(
                                pt[jg % 2][:, lo:hi],
                                pST[set_ * 2 + bank][:, lo - bank * 512:
                                                     hi - bank * 512],
                                AF.Exp, scale=SCALE, bias=bias16[:, h:h + 1])
                        if jg == 63:
                            last = s.copy(ctx.junk[1][:], ctx.junk[0][:])
                        ctx.inc("Aex", last)

            @block.vector
            def _(v):
                v = ctx.vg.bind(v)
                for h in range(NH):
                    ctx.wv(v, "Apv", h * 64 + 64)
                    for half in range(2):
                        v.tensor_scalar(pL[half][:], pL[half][:], 1e-30, None,
                                        ALU.add)
                        v.reciprocal(lrow32[0:1, half * 512:(half + 1) * 512],
                                     pL[half][:])
                        i = v.tensor_copy(lrow[0:1, half * 512:(half + 1) * 512],
                                          lrow32[0:1, half * 512:(half + 1) * 512])
                    ctx.inc("Alr", i)
                    ctx.wv(v, "Abc", h + 1)
                    for half in range(2):
                        ctx.inc("Abce", v.tensor_copy(
                            rb[:, half * 512:(half + 1) * 512], pST[half][:]))
                    for half in range(2):
                        ctx.inc("Aoev", v.tensor_tensor(
                            qoT[:, h * 1024 + half * 512:
                                h * 1024 + half * 512 + 512],
                            pOT[half][:], rb[:, half * 512:(half + 1) * 512],
                            ALU.mult))

        if DEBUG_TAPS:
            @block.sync
            def _(sync):
                ctx.wv(sync, "Abias", 1)
                ctx.inc("Adbg", sync.dma_start(out=dram["dbg_aqT"],
                                               in_=qoT[:]), 16)
                ctx.inc("Adbg", sync.dma_start(out=dram["dbg_agk"],
                                               in_=dram["gath_k"][0:2048, :]), 16)
                ctx.inc("Adbg", sync.dma_start(out=dram["dbg_ak2"],
                                               in_=dram["stage_k2"][:]), 16)
                ctx.wv(sync, "Aoev", 2 * NH)
                ctx.inc("Adbg", sync.dma_start(out=dram["dbg_aoT"],
                                               in_=qoT[:]), 16)
                ctx.wv(sync, "Abias", 1)
                ctx.inc("Adbg", sync.dma_start(out=dram["dbg_abias"][:, 0:16],
                                               in_=bias16[:]), 16)
                ctx.inc("Adbg", sync.dma_start(out=dram["dbg_abias"][:, 16:17],
                                               in_=maxkb[:],
                                               allow_slow_non_contiguous=True), 16)

        with ExitStack() as p5:
            acc = p5.enter_context(nc.sbuf_tensor("a_acc", [128, LT * D], F32))
            io2 = alloc_io(ctx, p5, "a2_", slim=True)
            pmm = [p5.enter_context(nc.psum_tensor("a5_pm%d" % i, [128, 512], F32))
                   for i in range(4)]

            @block.gpsimd
            def _(sync):
                for half in range(2):
                    if half == 0:
                        ctx.wv(sync, "Amm", 48)
                    else:
                        ctx.wv(sync, "Aom", 2 * LT)
                    for k in range(KC):
                        ctx.inc("Awo", sync.dma_start(
                            out=wb[:, k * 1024:(k + 1) * 1024],
                            in_=dram["l0_woT"][k * 128:(k + 1) * 128,
                                               half * 1024:(half + 1) * 1024]), 16)

            @block.tensor
            def _(te):
                ctx.wv(te, "Aoev", 2 * NH)
                for half in range(2):
                    ctx.wv(te, "Awo", 16 * KC * (half + 1))
                    for t in range(LT):
                        for n in range(2):
                            idx = 2 * LT * half + t * 2 + n
                            if idx >= 4:
                                ctx.wv(te, "Aoe", idx - 3)
                            p = pmm[idx % 4]
                            for h in range(KC):
                                i = te.matmul(
                                    p[:], fmv(qoT, h, t * 128, 128),
                                    wb[:, h * 1024 + n * 512:h * 1024 + n * 512 + 512],
                                    start=(h == 0), stop=(h == KC - 1))
                            ctx.inc("Aom", i)

            rdy = [("Aoe", 2 * LT + 2 * t + 2) for t in range(LT)]

            @block.vector
            def _(v):
                v = ctx.vg.bind(v)
                for half in range(2):
                    for t in range(LT):
                        for n in range(2):
                            idx = 2 * LT * half + t * 2 + n
                            ctx.wv(v, "Aom", idx + 1)
                            co = half * 1024 + n * 512
                            i = v.tensor_copy(acc[:, t * D + co:t * D + co + 512],
                                              pmm[idx % 4][:])
                            ctx.inc("Aoe", i)

            if DEBUG_TAPS:
                @block.sync
                def _(sync):
                    ctx.wv(sync, "Aoe", 4 * LT)
                    ctx.inc("Adbg", sync.dma_start(out=dram["dbg_aacc"],
                                                   in_=acc[:]), 16)
            emit_epilogue(ctx, block, "AE", acc, dram["asrc"], dram["adst"],
                          io2, rdy)

            @block.sync
            def _(sync):
                ctx.wv(sync, "AEyo", ctx.val("AEyo"))

        return phase_close(ctx, block)


PHASE_INPUTS = {
    "A": ["l0_wqT", "l0_wkT", "l0_wvT", "l0_woT", "amask"],
    "B": ["l0_wbT", "l0_bb", "l0_wrT", "l0_af", "l0_bf"],
    "C": ["l1_wq", "l1_wk", "l1_wv", "l1_wo", "dmask"],
    "D": ["wfc_slice", "wp_slice"],
}
INPUT_SHAPES = {
    "x_loc": ([TOK, D], F32),
    "rk": ([1, 1], mybir.dt.int32),
    "ident": ([128, 128], BF16),
    "ones_row": ([1, 128], BF16),
    "ones_col": ([128, 1], BF16),
    "dmask": ([128, 128], F32),
    "amask": ([64 * 128, 128], BF16),
    "l0_wqT": ([D, D], F32), "l0_wkT": ([D, D], F32),
    "l0_wvT": ([D, D], F32), "l0_woT": ([D, D], F32),
    "l0_wbT": ([D, D], F32), "l0_bb": ([1, D], F32),
    "l0_wrT": ([D, 8], F32), "l0_af": ([D, 64], F32), "l0_bf": ([64, D], F32),
    "l1_wq": ([D, D], F32), "l1_wk": ([D, D], F32),
    "l1_wv": ([D, D], F32), "l1_wo": ([D, D], F32),
    "wfc_slice": ([256, 4 * D], F32), "wp_slice": ([1024, D], F32),
}

_PROGRAM_CACHE = {}


def build_program(group):
    group = tuple(group)
    if group in _PROGRAM_CACHE:
        return _PROGRAM_CACHE[group]
    nc = bass.Bass("TRN2", target_bir_lowering=False)
    names = ["x_loc", "rk", "ident", "ones_row", "ones_col"]
    for p in group:
        names += PHASE_INPUTS[p]
    ins = {}
    for n in names:
        sh, dt = INPUT_SHAPES[n]
        ins[n] = nc.dram_tensor(n, sh, dt, kind="ExternalInput").ap()
    y_out = nc.dram_tensor("y_loc", [TOK, D], F32, kind="ExternalOutput").ap()

    if DEBUG_TAPS and "A" in group:
        ins["dbg_aoT"] = nc.dram_tensor("dbg_aoT", [128, KC * TOK], BF16,
                                        kind="ExternalOutput").ap()
        ins["dbg_aqT"] = nc.dram_tensor("dbg_aqT", [128, KC * TOK], BF16,
                                        kind="ExternalOutput").ap()
        ins["dbg_agk"] = nc.dram_tensor("dbg_agk", [2048, TOK], BF16,
                                        kind="ExternalOutput").ap()
        ins["dbg_akb"] = nc.dram_tensor("dbg_akb", [128, 8192], BF16,
                                        kind="ExternalOutput").ap()
        ins["dbg_avb"] = nc.dram_tensor("dbg_avb", [128, 8192], BF16,
                                        kind="ExternalOutput").ap()
        ins["dbg_aacc"] = nc.dram_tensor("dbg_aacc", [128, LT * D], F32,
                                         kind="ExternalOutput").ap()
        ins["dbg_abias"] = nc.dram_tensor("dbg_abias", [128, 17], F32,
                                          kind="ExternalOutput").ap()
        ins["dbg_ak2"] = nc.dram_tensor("dbg_ak2", [2048, TOK], BF16,
                                        kind="ExternalOutput").ap()
    if DEBUG_TAPS and "D" in group:
        ins["dbg_dfeat"] = nc.dram_tensor("dbg_dfeat", [128, KC * TOK], BF16,
                                          kind="ExternalOutput").ap()
        ins["dbg_dh"] = nc.dram_tensor("dbg_dh", [TOK, 4 * D], BF16,
                                       kind="ExternalOutput").ap()
        ins["dbg_dwt"] = nc.dram_tensor("dbg_dwt", [128, 4 * D], BF16,
                                        kind="ExternalOutput").ap()
        ins["dbg_dwp"] = nc.dram_tensor("dbg_dwp", [128, D], BF16,
                                        kind="ExternalOutput").ap()
        ins["dbg_dqs"] = nc.dram_tensor("dbg_dqs", [128, 64], F32,
                                        kind="ExternalOutput").ap()
        ins["dbg_dlnb"] = nc.dram_tensor("dbg_dlnb", [128, LT * D], BF16,
                                         kind="ExternalOutput").ap()
        ins["dbg_dst"] = nc.dram_tensor("dbg_dst", [128, LT * 16], F32,
                                        kind="ExternalOutput").ap()
        ins["dbg_dacc"] = nc.dram_tensor("dbg_dacc", [128, LT * D], F32,
                                         kind="ExternalOutput").ap()
    if DEBUG_TAPS and "B" in group:
        dram_dbg = {
            "dbg_featT": nc.dram_tensor("dbg_featT", [128, KC * TOK], BF16,
                                        kind="ExternalOutput").ap(),
            "dbg_hst": nc.dram_tensor("dbg_hst", [64, LT * 128], BF16,
                                      kind="ExternalOutput").ap(),
            "dbg_acc": nc.dram_tensor("dbg_acc", [128, LT * D], F32,
                                      kind="ExternalOutput").ap(),
            "dbg_gs": nc.dram_tensor("dbg_gs", [128, 64], F32,
                                     kind="ExternalOutput").ap(),
            "dbg_lg": nc.dram_tensor("dbg_lg", [128, 64], F32,
                                     kind="ExternalOutput").ap(),
            "dbg_ct": nc.dram_tensor("dbg_ct", [128, 64], F32,
                                     kind="ExternalOutput").ap(),
            "dbg_hsb": nc.dram_tensor("dbg_hsb", [128, 512], F32,
                                      kind="ExternalOutput").ap(),
        }
        ins.update(dram_dbg)
    dram = dict(ins)
    chain = [ins["x_loc"]]
    for i in range(len(group) - 1):
        chain.append(nc.dram_tensor("s%d" % i, [TOK, D], F32).ap())
    chain.append(y_out)
    for i, p in enumerate(group):
        dram[p.lower() + "src"] = chain[i]
        dram[p.lower() + "dst"] = chain[i + 1]

    if "A" in group:
        dram["stage_k"] = nc.dram_tensor("stage_k", [NC * D, TOK], BF16).ap()
        dram["stage_k2"] = nc.dram_tensor("stage_k2", [D, TOK], BF16).ap()
        dram["gath_k"] = nc.dram_tensor("gath_k", [NC * D, TOK], BF16,
                                         addr_space="Shared").ap()
        dram["stage_v"] = nc.dram_tensor("stage_v", [NC * NH * TOK, 128],
                                         BF16).ap()
        dram["stage_v2"] = nc.dram_tensor("stage_v2", [NH * TOK, 128], BF16).ap()
        dram["gath_v"] = nc.dram_tensor("gath_v", [NC * NH * TOK, 128], BF16,
                                        addr_space="Shared").ap()
        dram["stage_mk"] = nc.dram_tensor("stage_mk", [1, 1], F32).ap()
        dram["gath_mk"] = nc.dram_tensor("gath_mk", [1, 1], F32,
                                         addr_space="Shared").ap()
        dram["fence_a"] = nc.dram_tensor("fence_a", [1, 1], F32,
                                         addr_space="Shared").ap()
    if "D" in group:
        dram["psc"] = nc.dram_tensor("psc", [128, 2], F32).ap()
        dram["ar_in"] = nc.dram_tensor("ar_in", [2, 1], F32).ap()
        dram["ar_out"] = nc.dram_tensor("ar_out", [2, 1], F32,
                                        addr_space="Shared").ap()
        dram["stage_wfc"] = nc.dram_tensor("stage_wfc", [D, 4 * D], BF16).ap()
        dram["gath_wfc"] = nc.dram_tensor("gath_wfc", [D, 4 * D], BF16,
                                          addr_space="Shared").ap()
        dram["stage_wp"] = nc.dram_tensor("stage_wp", [4 * D, D], BF16).ap()
        dram["gath_wp"] = nc.dram_tensor("gath_wp", [4 * D, D], BF16,
                                          addr_space="Shared").ap()
        dram["fence_d"] = nc.dram_tensor("fence_d", [2, 1], F32,
                                         addr_space="Shared").ap()
        dram["h_spill"] = nc.dram_tensor("h_spill", [TOK, 4 * D], BF16).ap()

    with ExitStack() as stack:
        ctx = Ctx(nc, stack)
        ctx.vg = VGuard(ctx)
        ctx.ident = stack.enter_context(nc.sbuf_tensor("g_ident", [128, 128], BF16))
        ctx.ones_row = stack.enter_context(nc.sbuf_tensor("g_onr", [1, 128], BF16))
        ctx.ones_col = stack.enter_context(nc.sbuf_tensor("g_onc", [128, 1], BF16))
        ctx.junk = [stack.enter_context(nc.sbuf_tensor("g_jk%d" % i, [1, 4], F32))
                    for i in range(3)]
        ctx.rks = stack.enter_context(nc.sbuf_tensor("g_rk", [1, 1],
                                                     mybir.dt.int32))
        if "C" in group:
            ctx.dmask = stack.enter_context(
                nc.sbuf_tensor("g_dmask", [128, 128], F32))
        block = stack.enter_context(nc.Block())

        @block.gpsimd
        def _(g):
            ctx.inc("ini", g.dma_start(out=ctx.rks[:], in_=ins["rk"]), 16)
            ctx.inc("ini", g.dma_start(out=ctx.ident[:], in_=ins["ident"]), 16)
            ctx.inc("ini", g.dma_start(out=ctx.ones_row[:], in_=ins["ones_row"]), 16)
            ctx.inc("ini", g.dma_start(out=ctx.ones_col[:], in_=ins["ones_col"]), 16)
            if "C" in group:
                ctx.inc("ini", g.dma_start(out=ctx.dmask[:], in_=ins["dmask"]), 16)

        need = ctx.val("ini")

        @block.vector
        def _(v):
            v = ctx.vg.bind(v)
            ctx.wv(v, "ini", need)
            ctx.inc("ph", v.memset(ctx.junk[0][:], 0.0))

        @block.scalar
        def _(s):
            ctx.wv(s, "ini", need)
            ctx.inc("ph", s.copy(ctx.junk[1][:], ctx.junk[0][:]))

        @block.gpsimd
        def _(g):
            ctx.wv(g, "ini", need)
            ctx.inc("ph", g.memset(ctx.junk[2][:], 0.0))

        bar = ctx.val("ph")
        emitters = {"A": emit_phase_A, "B": emit_phase_B,
                    "C": emit_phase_C, "D": emit_phase_D}
        for i, p in enumerate(group):
            if i >= 1:
                ctx.retire(group[i - 1])
            bar = emitters[p](ctx, block, dram, bar)

    _PROGRAM_CACHE[group] = (nc, names)
    return nc, names


def host_prep(inputs):
    import ml_dtypes
    bf = ml_dtypes.bfloat16
    x = np.asarray(inputs["L0_attn_x"])[0]
    pr = {}
    pr["x_loc"] = [np.ascontiguousarray(
        np.concatenate([x[jg * 128:(jg + 1) * 128] for jg in core_tiles(c)], 0))
        for c in range(NC)]
    pr["rk"] = [np.full((1, 1), c, np.int32) for c in range(NC)]
    pr["ident"] = np.eye(128, dtype=bf)
    pr["ones_row"] = np.ones((1, 128), dtype=bf)
    pr["ones_col"] = np.ones((128, 1), dtype=bf)
    r = np.arange(128)
    pr["dmask"] = ((r[:, None] % 8) == (r[None, :] % 8)).astype(np.float32)
    band = np.where(r[:, None] > r[None, :], np.float32(NEG), np.float32(0.0))
    zeros = np.zeros((128, 128), np.float32)
    dead = np.full((128, 128), np.float32(NEG))
    ams = []
    for c in range(NC):
        tiles = []
        for lt in range(LT):
            ib = c if lt % 2 == 0 else 7 - c
            for i in range(8):
                tiles.append(zeros if i < ib else (band if i == ib else dead))
        ams.append(np.concatenate(tiles, 0).astype(bf))
    pr["amask"] = ams
    f32 = np.float32
    pr["l0_wqT"] = np.ascontiguousarray(np.asarray(inputs["L0_attn_wq"]).T)
    pr["l0_wkT"] = np.ascontiguousarray(np.asarray(inputs["L0_attn_wk"]).T)
    pr["l0_wvT"] = np.ascontiguousarray(np.asarray(inputs["L0_attn_wv"]).T)
    pr["l0_woT"] = np.ascontiguousarray(np.asarray(inputs["L0_attn_wo"]).T)
    pr["l0_wbT"] = np.ascontiguousarray(np.asarray(inputs["L0_ffn_W_base"]).T)
    pr["l0_bb"] = np.ascontiguousarray(np.asarray(inputs["L0_ffn_b_base"])
                                       .reshape(1, D))
    pr["l0_wrT"] = np.ascontiguousarray(np.asarray(inputs["L0_ffn_W_router"]).T)
    pr["l0_af"] = np.ascontiguousarray(
        np.asarray(inputs["L0_ffn_A_w"]).transpose(2, 0, 1).reshape(D, 64))
    pr["l0_bf"] = np.ascontiguousarray(
        np.asarray(inputs["L0_ffn_B_w"]).transpose(0, 2, 1).reshape(64, D))
    pr["l1_wq"] = np.ascontiguousarray(np.asarray(inputs["L1_attn_Wq"]))
    pr["l1_wk"] = np.ascontiguousarray(np.asarray(inputs["L1_attn_Wk"]))
    pr["l1_wv"] = np.ascontiguousarray(np.asarray(inputs["L1_attn_Wv"]))
    pr["l1_wo"] = np.ascontiguousarray(np.asarray(inputs["L1_attn_Wo"]))
    wfcT = np.ascontiguousarray(np.asarray(inputs["L1_ffn_w_fc"]).T)
    wpT = np.ascontiguousarray(np.asarray(inputs["L1_ffn_w_proj"]).T)
    pr["wfc_slice"] = [np.ascontiguousarray(wfcT[c * 256:(c + 1) * 256])
                       for c in range(NC)]
    pr["wp_slice"] = [np.ascontiguousarray(wpT[c * 1024:(c + 1) * 1024])
                      for c in range(NC)]
    return pr


def kernel(**inputs):
    pr = host_prep(inputs)
    xcur = pr["x_loc"]
    for group in LAUNCH_GROUPS:
        nc, names = build_program(tuple(group))
        in_maps = []
        for c in range(NC):
            m = {}
            for n in names:
                if n == "x_loc":
                    m[n] = xcur[c]
                elif isinstance(pr[n], list):
                    m[n] = pr[n][c]
                else:
                    m[n] = pr[n]
            in_maps.append(m)
        res = run_bass_kernel_spmd(nc, in_maps, core_ids=list(range(NC)))
        xcur = [np.asarray(res.results[c]["y_loc"]) for c in range(NC)]
    out = np.empty((1, S, D), np.float32)
    for c in range(NC):
        for l, jg in enumerate(core_tiles(c)):
            out[0, jg * 128:(jg + 1) * 128] = xcur[c][l * 128:(l + 1) * 128]
    return out
